# version 05
# baseline (speedup 1.0000x reference)
import contextlib
import numpy as np
import ml_dtypes
import concourse.bass as bass
import concourse.mybir as mybir
from concourse.bass_utils import run_bass_kernel_spmd
from concourse.replica_groups import maybe_share_collective_output_space

F32 = mybir.dt.float32
BF16 = mybir.dt.bfloat16
U8 = mybir.dt.uint8
AF = mybir.ActivationFunctionType
AL = mybir.AluOpType
AX = mybir.AxisListType
BF = ml_dtypes.bfloat16

NCORE = 8
S = 8192
TS = S // NCORE
DM = 4096
NP = 32
C = 128
NCH = TS // C
NG = 4
GP = NP // NG
EPS = 1e-5
NVAR = 12


class Sem:
    def __init__(self, h):
        self.h = h
        self.n = 0


class Ctx:
    ENGS = ("sync", "gpsimd", "tensor", "vector", "scalar")

    def __init__(self, nc):
        self.nc = nc
        self.st = {e: [] for e in self.ENGS}
        self.gstack = contextlib.ExitStack()
        self.pstack = None

    def sem(self, name):
        return Sem(self.gstack.enter_context(self.nc.semaphore(name)))

    def sb(self, name, shape, dt, phase=True):
        stk = self.pstack if (phase and self.pstack is not None) else self.gstack
        return stk.enter_context(self.nc.sbuf_tensor(name, shape, dt))

    def ps(self, name, shape, dt=F32, phase=True):
        stk = self.pstack if (phase and self.pstack is not None) else self.gstack
        return stk.enter_context(self.nc.psum_tensor(name, shape, dt))

    def phase_open(self):
        assert self.pstack is None
        self.pstack = contextlib.ExitStack()

    def phase_close(self):
        self.pstack.close()
        self.pstack = None

    def emit(self, eng, fn):
        self.st[eng].append(fn)

    def op(self, eng, sem, fn):
        def run(e, fn=fn, h=sem.h):
            fn(e).then_inc(h, 1)
        self.st[eng].append(run)
        sem.n += 1
        return sem.n

    def dma(self, q, sem, o, i, transpose=False):
        def run(e, o=o, i=i, h=sem.h, tr=transpose):
            e.dma_start(o, i, transpose=True).then_inc(h, 16) if tr else \
                e.dma_start(o, i).then_inc(h, 16)
        self.st[q].append(run)
        sem.n += 16
        return sem.n

    def wait(self, eng, sem, val=None):
        v = sem.n if val is None else val
        if v > 0:
            self.emit(eng, lambda e, h=sem.h, v=v: e.wait_ge(h, v))

    def barrier(self, bsem, dma_sems):
        for q, s in dma_sems.items():
            self.wait(q, s, s.n)
        for eng in self.ENGS:
            self.op(eng, bsem, lambda e: e.nop())
        for eng in self.ENGS:
            self.wait(eng, bsem, bsem.n)

    def run(self, block):
        def mk(lst):
            def body(eng):
                for f in lst:
                    f(eng)
            return body
        block.sync(mk(self.st["sync"]))
        block.gpsimd(mk(self.st["gpsimd"]))
        block.tensor(mk(self.st["tensor"]))
        block.vector(mk(self.st["vector"]))
        block.scalar(mk(self.st["scalar"]))


def _host_consts():
    j = np.arange(C)
    pats = [3 - (j % 8), 7 - (j % 8), 15 - (j % 16), 31 - (j % 32), 63 - (j % 64),
            (j % 8) - 4, j % 8, j % 16, j % 32, j % 64, j, 127 - j]
    iota = np.concatenate([p.astype(np.float32) for p in pats])[None, :]
    iota = np.ascontiguousarray(np.broadcast_to(iota, (128, NVAR * C)))
    ii = j[None, :]
    jj = j[:, None]
    lower = ii > jj

    def adj(L):
        return lower & ((ii // L) == (jj // L) + 1) & ((ii // (2 * L)) == (jj // (2 * L)))
    masks = [lower & ((ii // 8) == (jj // 8)), adj(8), adj(16), adj(32), adj(64), ii == jj]
    masks = np.ascontiguousarray(np.concatenate([m.astype(np.uint8) for m in masks], axis=1))
    ident = np.ascontiguousarray(np.eye(128, dtype=np.float32)).astype(BF)
    return iota, masks, ident


def build_program(debug_taps=()):
    nc = bass.Bass("TRN2", target_bir_lowering=False)
    c = Ctx(nc)
    DT = nc.dram_tensor
    IN = lambda n, sh, dt=F32: DT(n, sh, dt, kind="ExternalInput").ap()
    xT = IN("xT", [DM, TS + 1])
    x_tm = IN("x_tm", [TS, DM])
    wk = IN("wkT_b", [32, DM, 128])
    wv = IN("wvT_b", [32, DM, 128])
    wr = IN("wrT_b", [32, DM, 128])
    wg = IN("wgT_b", [32, DM, 128])
    wo = IN("woT_b", [8, DM, 512])
    wf = IN("wfT_b", [8, DM, 512])
    mcols = IN("mcols", [128, 4 * 32])
    tdp = IN("tdp", [128, 32])
    tfp = IN("tfp", [128, 32])
    gnw = IN("gnw", [128, 32])
    gnb = IN("gnb", [128, 32])
    fbr = IN("fbias_rep", [128, DM])
    iota_c = IN("iota_c", [128, NVAR * C])
    mask_c = IN("mask_c", [128, 6 * C], U8)
    id_c = IN("ident_c", [128, 128], BF16)
    gapix = IN("gapix", [128, 8])
    out = DT("out_sh", [TS, DM], F32, kind="ExternalOutput").ap()

    sc_k = DT("sc_k", [DM, TS], BF16, kind="Internal").ap()
    sc_v = DT("sc_v", [DM, TS], BF16, kind="Internal").ap()
    sc_r = DT("sc_r", [DM, TS], BF16, kind="Internal").ap()
    sc_g = DT("sc_g", [DM, TS], BF16, kind="Internal").ap()
    sc_xn = DT("sc_xn", [TS, DM], BF16, kind="Internal").ap()
    sc_O = DT("sc_O", [TS, DM], F32, kind="Internal").ap()
    sc_yp = DT("sc_yp", [TS, DM], F32, kind="Internal").ap()
    sc_zp = DT("sc_zp", [TS, DM], F32, kind="Internal").ap()
    cc_in = DT("cc_in", [NP, 128, 128], F32, kind="Internal").ap()
    aspace = maybe_share_collective_output_space("AllGather", [list(range(NCORE))])
    cc_out = DT("cc_out", [NCORE, NP, 128, 128], F32, kind="Internal", addr_space=aspace).ap()

    tap_specs = {"k": (sc_k, BF16), "v": (sc_v, BF16), "r": (sc_r, BF16), "g": (sc_g, BF16),
                 "xn": (sc_xn, BF16), "yp": (sc_yp, F32), "zp": (sc_zp, F32),
                 "B": (cc_in, F32), "Bg": (cc_out, F32), "O": (sc_O, F32)}
    taps = {}
    for name in debug_taps:
        src, dt = tap_specs[name]
        taps[name] = (DT("tap_" + name, list(src.shape), dt, kind="ExternalOutput").ap(), src)

    sq = c.sem("sq")
    gq = c.sem("gq")
    sv = c.sem("sv")
    ss = c.sem("ss")
    stn = c.sem("stn")
    scc = c.sem("scc")
    bar = c.sem("bar")
    DMAS = {"sync": sq, "gpsimd": gq}

    iota_s = c.sb("iota_s", [128, NVAR * C], F32, phase=False)
    mask_s = c.sb("mask_s", [128, 6 * C], U8, phase=False)
    id_s = c.sb("id_s", [128, 128], BF16, phase=False)
    wcol = c.sb("wcol", [128, 32], F32, phase=False)
    tfcol = c.sb("tfcol", [128, 32], F32, phase=False)
    mc = c.sb("mc", [128, 4 * 32], F32, phase=False)
    gnwc = c.sb("gnwc", [128, 32], F32, phase=False)
    gnbc = c.sb("gnbc", [128, 32], F32, phase=False)
    gapix_s = c.sb("gapix_s", [128, 8], F32, phase=False)
    epsc = c.sb("epsc", [128, 1], F32, phase=False)
    negc = c.sb("negc", [128, 1], F32, phase=False)
    c.op("vector", sv, lambda e: e.memset(epsc[:], EPS))
    c.op("vector", sv, lambda e: e.memset(negc[:], -1.0))
    nc.const_aps.register(F32, EPS, epsc[:, 0:1])
    nc.const_aps.register(F32, -1.0, negc[:, 0:1])

    for dst, src in ((iota_s, iota_c), (mask_s, mask_c), (id_s, id_c), (wcol, tdp),
                     (tfcol, tfp), (mc, mcols), (gnwc, gnw), (gnbc, gnb), (gapix_s, gapix)):
        c.dma("sync", sq, dst[:], src)
    c.wait("scalar", sq, sq.n)
    c.op("scalar", ss, lambda e: e.activation(wcol[:], wcol[:], AF.Exp))
    c.wait("vector", ss)
    c.op("vector", sv, lambda e: e.tensor_scalar_mul(wcol[:], wcol[:], -1.0))
    n_wcol = sv.n
    c.wait("tensor", sv, n_wcol)
    c.wait("scalar", sv, n_wcol)

    c.phase_open()
    x_sb = [c.sb(f"x_sb{i}", [128, TS + 1], BF16) for i in range(32)]
    mix = [c.sb(f"mix{i}", [128, TS], BF16) for i in range(32)]
    tmpm = [c.sb(f"tmpm{i}", [128, TS], BF16) for i in range(2)]
    wslab = [c.sb(f"wslab{i}", [128, DM], BF16) for i in range(2)]
    pstage = [c.sb(f"pstage{i}", [128, TS], BF16) for i in range(2)]
    pkp = [c.ps(f"pk{i}", [128, 512]) for i in range(4)]

    for i in range(32):
        c.dma("gpsimd", gq, x_sb[i][:], xT[i * 128:(i + 1) * 128, :])
    n_xload = gq.n

    def slab_ap(wt, o):
        return wt[o].rearrange("(it q) oc -> q it oc", q=128)

    projs = [(wk, 0, sc_k, None), (wv, 1, sc_v, None), (wr, 2, sc_r, None), (wg, 3, sc_g, AF.Silu)]
    mm_mark = {}
    ev_mark = {}
    st_mark = {}
    for pj, (wt, mi, scr, act) in enumerate(projs):
        c.wait("vector", gq, n_xload)
        if pj > 0:
            c.wait("vector", stn, mm_mark[(pj - 1, 31)])
        for i in range(32):
            t = tmpm[i % 2]
            cur = x_sb[i][:, 1:TS + 1]
            prv = x_sb[i][:, 0:TS]
            c.op("vector", sv, lambda e, t=t, cur=cur, prv=prv: e.tensor_sub(t[:], cur, prv))
            c.op("vector", sv, lambda e, t=t, i=i, mi=mi, prv=prv:
                 e.scalar_tensor_tensor(mix[i][:], t[:], mc[:, mi * 32 + i:mi * 32 + i + 1],
                                        prv, AL.mult, AL.add))
        n_mixv = sv.n
        for o in range(32):
            gk = (pj, o)
            if o >= 2:
                c.wait("gpsimd", stn, mm_mark[(pj, o - 2)])
            elif pj > 0:
                c.wait("gpsimd", stn, mm_mark[(pj - 1, 30 + o)])
            c.dma("gpsimd", gq, wslab[o % 2][:].rearrange("q (it oc) -> q it oc", oc=128),
                  slab_ap(wt, o))
            n_sl = gq.n
            c.wait("tensor", gq, n_sl)
            if o == 0:
                c.wait("tensor", sv, n_mixv)
            pk2 = (pj, o - 2) if o >= 2 else ((pj - 1, 30 + o) if pj > 0 else None)
            if pk2 is not None:
                sE, nE = ev_mark[pk2]
                c.wait("tensor", sE, nE)
            pA, pB = pkp[(o % 2) * 2], pkp[(o % 2) * 2 + 1]

            def mms(e, o=o, pA=pA, pB=pB):
                w2 = wslab[o % 2]
                last = None
                for i in range(32):
                    lhs = w2[:, i * 128:(i + 1) * 128]
                    e.matmul(pA[:], lhs, mix[i][:, 0:512], start=(i == 0), stop=(i == 31))
                    last = e.matmul(pB[:], lhs, mix[i][:, 512:1024], start=(i == 0), stop=(i == 31))
                return last
            c.op("tensor", stn, mms)
            mm_mark[gk] = stn.n
            eng = "vector" if act is None else "scalar"
            esem = sv if act is None else ss
            c.wait(eng, stn, mm_mark[gk])
            if pk2 is not None and pk2 in st_mark:
                c.wait(eng, sq, st_mark[pk2])
            stg = pstage[o % 2]
            if act is None:
                c.op("vector", sv, lambda e, stg=stg, pA=pA: e.tensor_copy(stg[:, 0:512], pA[:]))
                c.op("vector", sv, lambda e, stg=stg, pB=pB: e.tensor_copy(stg[:, 512:1024], pB[:]))
            else:
                c.op("scalar", ss, lambda e, stg=stg, pA=pA, act=act: e.activation(stg[:, 0:512], pA[:], act))
                c.op("scalar", ss, lambda e, stg=stg, pB=pB, act=act: e.activation(stg[:, 512:1024], pB[:], act))
            ev_mark[gk] = (esem, esem.n)
            c.wait("sync", esem, esem.n)
            c.dma("sync", sq, scr[o * 128:(o + 1) * 128, :], stg[:])
            st_mark[gk] = sq.n
    c.phase_close()
    c.barrier(bar, DMAS)

    mstack = contextlib.ExitStack()
    MSB = lambda n, sh, dt: mstack.enter_context(nc.sbuf_tensor(n, sh, dt))
    ktm = [MSB(f"ktm{i}", [128, TS], BF16) for i in range(NCH)]
    vtm = [MSB(f"vtm{i}", [128, TS], BF16) for i in range(NCH)]
    eupdT_all = MSB("eupdT_all", [128, GP * 128], BF16)
    e128 = MSB("e128", [128, GP], F32)
    Sst = [MSB(f"Sst{i}", [128, 128], F32) for i in range(GP)]
    ktmpA = [MSB(f"ktmpA{i}", [128, GP * 128], BF16) for i in range(2)]
    expn = MSB("expn", [128, NVAR * C], F32)
    eupd_f = MSB("eupd_f", [128, 128], BF16)

    c.phase_open()
    pU = []
    pU.append(c.ps("p1U0", [128, 64]))
    c.ps("p1pad0", [128, 448])
    pU.append(c.ps("p1U1", [128, 64]))
    c.ps("p1pad1", [128, 448])
    pTr = c.ps("p1Tr", [128, 128], BF16)

    def build_pair_patterns(pi, p, full):
        c.wait("vector", ss)
        c.wait("scalar", stn)
        if full:
            c.op("vector", sv, lambda e, p=p: e.tensor_scalar(
                expn[:], iota_s[:], wcol[:, p:p + 1], None, AL.mult))
            c.wait("scalar", sv)
            c.op("scalar", ss, lambda e, pi=pi: e.activation(pat2[pi][:], expn[:], AF.Exp))
            c.wait("tensor", ss)
            c.wait("tensor", sv)
            c.op("tensor", stn, lambda e, pi=pi: e.transpose(
                trg_ap(), pat2[pi][:, 11 * C:12 * C], id_s[:]))
        else:
            c.op("vector", sv, lambda e, p=p: e.tensor_scalar(
                expn[:, 0:C], iota_s[:, 11 * C:12 * C], wcol[:, p:p + 1], None, AL.mult))
            c.wait("scalar", sv)
            c.op("scalar", ss, lambda e: e.activation(eupd_f[:], expn[:, 0:C], AF.Exp))
            c.wait("tensor", ss)
            c.wait("tensor", sv)
            c.op("tensor", stn, lambda e: e.transpose(trg_ap(), eupd_f[:], id_s[:]))
        c.wait("vector", stn)
        c.op("vector", sv, lambda e, pi=pi: e.tensor_copy(
            eupdT_all[:, pi * 128:(pi + 1) * 128], trg_ap()))
        c.op("vector", sv, lambda e, p=p, pi=pi: e.tensor_scalar_mul(
            e128[:, pi:pi + 1], wcol[:, p:p + 1], 128.0))
        c.wait("scalar", sv)
        c.op("scalar", ss, lambda e, pi=pi: e.activation(e128[:, pi:pi + 1], e128[:, pi:pi + 1], AF.Exp))

    trg = [pTr]

    def trg_ap():
        t = trg[0]
        ap = t[:]
        if ap.dtype != BF16:
            ap = ap.bitcast(BF16)[:, 0:128]
        return ap
    pat2 = None
    upd_mark = {}
    stt_mark = {}
    for g in range(NG):
        c.wait("sync", stn, stn.n)
        c.wait("sync", sv, sv.n)
        c.wait("vector", sq, sq.n)
        for tt in range(NCH):
            c.dma("sync", sq, ktm[tt][:], sc_k[g * 1024:(g + 1) * 1024, tt * 128:(tt + 1) * 128],
                  transpose=True)
            c.dma("sync", sq, vtm[tt][:], sc_v[g * 1024:(g + 1) * 1024, tt * 128:(tt + 1) * 128],
                  transpose=True)
        n_ld = sq.n
        for pi in range(GP):
            build_pair_patterns(pi, g * GP + pi, full=False)
        c.wait("vector", ss)
        for pi in range(GP):
            c.op("vector", sv, lambda e, pi=pi: e.memset(Sst[pi][:], 0.0))
        c.wait("vector", sq, n_ld)
        c.wait("tensor", sq, n_ld)
        c.wait("tensor", ss)
        ch_upd = {}
        for ch in range(NCH):
            cpar = ch % 2
            if ch >= 2:
                c.wait("vector", stn, ch_upd[ch - 2])
            c.op("vector", sv, lambda e, ch=ch, cpar=cpar: e.tensor_mul(
                ktmpA[cpar][:], ktm[ch][:], eupdT_all[:]))
            n_kt = sv.n
            c.wait("tensor", sv, n_kt)
            for pi in range(GP):
                u = (g * NCH + ch) * GP + pi
                par = u % 2
                if u >= 2:
                    c.wait("tensor", sv, stt_mark[u - 2])

                def updmm(e, ch=ch, pi=pi, par=par, cpar=cpar):
                    e.matmul(pU[par][0:64, :], ktmpA[cpar][:, pi * 128:pi * 128 + 64],
                             vtm[ch][:, pi * 128:pi * 128 + 64], start=True, stop=True)
                    return e.matmul(pU[par][64:128, :],
                                    ktmpA[cpar][:, pi * 128 + 64:(pi + 1) * 128],
                                    vtm[ch][:, pi * 128 + 64:(pi + 1) * 128],
                                    start=True, stop=True, tile_position=(0, 64))
                c.op("tensor", stn, updmm)
                upd_mark[u] = stn.n
                c.wait("vector", stn, upd_mark[u])
                c.op("vector", sv, lambda e, pi=pi, par=par: e.scalar_tensor_tensor(
                    Sst[pi][0:64, 0:64], Sst[pi][0:64, 0:64], e128[0:64, pi:pi + 1],
                    pU[par][0:64, :], AL.mult, AL.add))
                c.op("vector", sv, lambda e, pi=pi, par=par: e.scalar_tensor_tensor(
                    Sst[pi][64:128, 64:128], Sst[pi][64:128, 64:128], e128[64:128, pi:pi + 1],
                    pU[par][64:128, :], AL.mult, AL.add))
                stt_mark[u] = sv.n
            ch_upd[ch] = stn.n
        c.wait("sync", sv, sv.n)
        for pi in range(GP):
            c.dma("sync", sq, cc_in[g * GP + pi], Sst[pi][:])
    c.wait("gpsimd", sq, sq.n)
    c.emit("gpsimd", lambda e: e.collective_compute(
        "AllGather", AL.bypass, replica_groups=[list(range(NCORE))],
        ins=[cc_in[:]], outs=[cc_out[:]]).then_inc(scc.h, 1))
    scc.n += 1
    c.phase_close()
    c.barrier(bar, DMAS)
    c.wait("sync", scc, scc.n)
    c.wait("gpsimd", scc, scc.n)
    c.wait("vector", scc, scc.n)
    c.wait("tensor", scc, scc.n)
    c.wait("scalar", scc, scc.n)

    c.phase_open()
    kf = [c.sb(f"kf{i}", [128, TS], BF16) for i in range(GP)]
    rf = [c.sb(f"rf{i}", [128, TS], BF16) for i in range(GP)]
    pat2 = [c.sb(f"pat{i}", [128, NVAR * C], BF16) for i in range(GP)]
    khat = [c.sb(f"khat{i}", [128, 5 * C], BF16) for i in range(2)]
    ktfb = [c.sb(f"ktf{i}", [128, C], BF16) for i in range(2)]
    rhat = [c.sb(f"rhat{i}", [128, 6 * C], BF16) for i in range(2)]
    pcs = [c.sb(f"pcs{i}", [128, 8 * 128], BF16) for i in range(2)]
    rstg = [c.sb(f"rstg{i}", [128, 96], BF16) for i in range(2)]
    zzb = c.sb("zzb", [128, C], BF16)
    A2b = [c.sb(f"A2b{i}", [128, 2 * C], BF16) for i in range(2)]
    Sbf = [c.sb(f"Sbf{i}", [128, 128], BF16) for i in range(GP)]
    xnb = [c.sb(f"xnb{i}", [128, TS], BF16) for i in range(2)]
    Bg = c.sb("Bg", [128, NCORE * 128], F32)
    egap = c.sb("egap", [128, 8], F32)
    b6 = [c.sb(f"b6_{i}", [128, 12], F32) for i in range(2)]
    stat = [c.sb(f"stat{i}", [128, 12], F32) for i in range(2)]
    zz = c.sb("zz", [128, C], F32)
    ostg = [c.sb(f"ostg{i}", [128, TS], F32) for i in range(2)] if "O" in taps else None
    P = [c.ps(f"p2P{i}", [128, 128]) for i in range(6)]
    pO2 = [c.ps(f"p2O{h}", [128, 64]) for h in range(2)]

    c.op("vector", sv, lambda e: e.memset(zz[:], 0.0))
    c.op("vector", sv, lambda e: e.memset(zzb[:], 0.0))

    marks = {}
    xn_dma = {}
    build_mark = {}
    pend_fin = None
    for g in range(NG):
        c.wait("sync", stn, stn.n)
        c.wait("sync", sv, sv.n)
        c.wait("sync", ss, ss.n)
        c.wait("vector", stn, stn.n)
        c.wait("vector", sq, sq.n)
        for pi in range(GP):
            p = g * GP + pi
            c.dma("sync", sq, kf[pi][:], sc_k[p * 128:(p + 1) * 128, :])
            c.dma("sync", sq, rf[pi][:], sc_r[p * 128:(p + 1) * 128, :])
        for tt in range(NCH):
            c.dma("sync", sq, ktm[tt][:], sc_k[g * 1024:(g + 1) * 1024, tt * 128:(tt + 1) * 128],
                  transpose=True)
            c.dma("sync", sq, vtm[tt][:], sc_v[g * 1024:(g + 1) * 1024, tt * 128:(tt + 1) * 128],
                  transpose=True)
        n_ld = sq.n
        trg[0] = P[1]
        for pi in range(GP):
            p = g * GP + pi
            build_pair_patterns(pi, p, full=True)
            c.op("vector", sv, lambda e, p=p: e.tensor_scalar(
                egap[:], gapix_s[:], wcol[:, p:p + 1], None, AL.mult))
            c.wait("scalar", sv)
            c.op("scalar", ss, lambda e: e.activation(egap[:], egap[:], AF.Exp))
            c.wait("gpsimd", sv, sv.n)
            c.dma("gpsimd", gq, Bg[:].rearrange("q (j f) -> q j f", j=NCORE),
                  cc_out[:, p, :, :].rearrange("j q f -> q j f"))
            c.wait("vector", gq)
            c.wait("vector", ss)
            c.op("vector", sv, lambda e, pi=pi: e.memset(Sst[pi][:], 0.0))
            for jc in range(NCORE):
                c.op("vector", sv, lambda e, pi=pi, jc=jc: e.scalar_tensor_tensor(
                    Sst[pi][:], Bg[:, jc * 128:(jc + 1) * 128], egap[:, jc:jc + 1],
                    Sst[pi][:], AL.mult, AL.add))
            c.op("vector", sv, lambda e, pi=pi: e.tensor_copy(Sbf[pi][:], Sst[pi][:]))
        c.wait("vector", sq, n_ld)
        c.wait("tensor", sq, n_ld)
        c.wait("scalar", sq, n_ld)
        def emit_builds(u):
            lc = u - g * NCH * GP
            ch_, pi_ = lc // GP, lc % GP
            par_ = u % 2
            kfc = kf[pi_][:, ch_ * C:(ch_ + 1) * C]
            rfc = rf[pi_][:, ch_ * C:(ch_ + 1) * C]
            if u >= 2:
                c.wait("vector", stn, marks[("ph1", u - 2)])
                c.wait("scalar", stn, marks[("ph1", u - 2)])
            c.op("vector", sv, lambda e, par=par_, kfc=kfc, pi=pi_: e.tensor_tensor(
                khat[par][:].rearrange("q (v f) -> q v f", v=5),
                kfc.rearrange("q (v f) -> q v f", v=1).broadcast_to((128, 5, C)),
                pat2[pi][:, 0:5 * C].rearrange("q (v f) -> q v f", v=5), AL.mult))
            c.op("vector", sv, lambda e, par=par_, rfc=rfc, pi=pi_: e.tensor_tensor(
                rhat[par][:].rearrange("q (v f) -> q v f", v=6),
                rfc.rearrange("q (v f) -> q v f", v=1).broadcast_to((128, 6, C)),
                pat2[pi][:, 5 * C:11 * C].rearrange("q (v f) -> q v f", v=6), AL.mult))
            c.op("scalar", ss, lambda e, par=par_, kfc=kfc, p=g * GP + pi_: e.mul(
                ktfb[par][:], kfc, tfcol[:, p:p + 1]))
            build_mark[u] = (sv.n, ss.n)

        u_base = g * NCH * GP
        u_last = u_base + NCH * GP - 1
        emit_builds(u_base)
        for ch in range(NCH):
            cpar = ch % 2
            if ch >= 2:
                c.wait("vector", stn, marks[("AV", (g * NCH + ch - 2) * GP + GP - 1)])
            c.op("vector", sv, lambda e, ch=ch, cpar=cpar: e.tensor_mul(
                ktmpA[cpar][:], ktm[ch][:], eupdT_all[:]))
            for pi in range(GP):
                u = (g * NCH + ch) * GP + pi
                par = u % 2
                rf_c = rf[pi][:, ch * C:(ch + 1) * C]
                gc = g * NCH + ch
                c.wait("tensor", sv, build_mark[u][0])
                c.wait("tensor", ss, build_mark[u][1])
                if u >= 1:
                    c.wait("tensor", sv, marks[("sel1", u - 1)])
                    c.wait("tensor", ss, marks[("rect", u - 1)])
                    c.wait("tensor", sv, marks[("stt", u - 1)])

                def pieces(e, par=par, rf_c=rf_c, h=0):
                    hs = slice(64 * h, 64 * (h + 1))
                    tp = None if h == 0 else (64, 0)
                    for px in range(5):
                        e.matmul(P[px][:], khat[par][hs, px * C:(px + 1) * C],
                                 rhat[par][hs, px * C:(px + 1) * C], start=True, stop=True,
                                 tile_position=tp)
                    return e.matmul(P[5][:], ktfb[par][hs, :], rf_c[hs, :],
                                    start=True, stop=True, tile_position=tp)
                c.op("tensor", stn, lambda e, par=par, rf_c=rf_c: pieces(e, par, rf_c, 0))
                marks[("ph0", u)] = stn.n
                c.wait("scalar", stn, marks[("ph0", u)])

                def pco0(e, par=par):
                    e.copy(pcs[par][:, 0:128], P[0][:])
                    e.copy(pcs[par][:, 128:256], P[1][:])
                    e.copy(pcs[par][:, 256:384], P[2][:])
                    e.copy(pcs[par][:, 384:512], P[5][:])
                    e.copy(rstg[par][0:64, 0:64], P[4][0:64, 64:128])
                    e.copy(rstg[par][0:32, 64:96], P[3][0:32, 32:64])
                    return e.copy(rstg[par][64:96, 64:96], P[3][64:96, 96:128])
                c.op("scalar", ss, pco0)
                marks[("pco0", u)] = ss.n
                if u < u_last:
                    emit_builds(u + 1)
                if u >= 2:
                    c.wait("vector", stn, marks[("AV", u - 2)])
                c.wait("vector", ss, marks[("pco0", u)])

                def sel(e, par=par, h=0):
                    dst = A2b[par][:, h * C:(h + 1) * C]
                    base = h * 512
                    e.select(dst, mask_s[:, 0:C], pcs[par][:, base:base + 128], zzb[:])
                    e.select(dst, mask_s[:, C:2 * C], pcs[par][:, base + 128:base + 256], dst)
                    e.select(dst, mask_s[:, 2 * C:3 * C], pcs[par][:, base + 256:base + 384], dst)
                    return e.select(dst, mask_s[:, 5 * C:6 * C], pcs[par][:, base + 384:base + 512], dst)
                c.op("vector", sv, lambda e, par=par: sel(e, par, 0))
                marks[("sel0", u)] = sv.n
                c.wait("tensor", ss, marks[("pco0", u)])
                c.op("tensor", stn, lambda e, par=par, rf_c=rf_c: pieces(e, par, rf_c, 1))
                marks[("ph1", u)] = stn.n
                c.wait("scalar", stn, marks[("ph1", u)])

                def pco1(e, par=par):
                    e.copy(pcs[par][:, 512:640], P[0][:])
                    e.copy(pcs[par][:, 640:768], P[1][:])
                    e.copy(pcs[par][:, 768:896], P[2][:])
                    return e.copy(pcs[par][:, 896:1024], P[5][:])
                c.op("scalar", ss, pco1)
                marks[("pco1", u)] = ss.n
                c.wait("vector", ss, marks[("pco1", u)])
                c.op("vector", sv, lambda e, par=par: sel(e, par, 1))
                marks[("sel1", u)] = sv.n
                c.wait("scalar", sv, marks[("sel1", u)])

                def rects(e, par=par):
                    e.copy(A2b[par][0:64, 64:128], rstg[par][0:64, 0:64])
                    e.copy(A2b[par][0:32, 32:64], rstg[par][0:32, 64:96])
                    e.copy(A2b[par][64:96, 96:128], rstg[par][64:96, 64:96])
                    e.copy(A2b[par][0:64, C + 64:C + 128], P[4][0:64, 64:128])
                    e.copy(A2b[par][0:32, C + 32:C + 64], P[3][0:32, 32:64])
                    return e.copy(A2b[par][64:96, C + 96:C + 128], P[3][64:96, 96:128])
                c.op("scalar", ss, rects)
                marks[("rect", u)] = ss.n
                if pend_fin is not None:
                    pend_fin()
                    pend_fin = None
                c.wait("tensor", sv, marks[("sel1", u)])
                c.wait("tensor", ss, marks[("rect", u)])
                if u >= 1:
                    c.wait("tensor", sv, marks[("gn", u - 1)])
                    if ostg is not None:
                        c.wait("tensor", sv, marks[("ocp", u - 1)])

                def avmm(e, par=par, ch=ch, pi=pi):
                    vt = vtm[ch]
                    rint = rhat[par][:, 5 * C:6 * C]
                    e.matmul(pO2[0][:], A2b[par][:, 0:C], vt[:, pi * 128:pi * 128 + 64],
                             start=True, stop=False)
                    e.matmul(pO2[0][:], rint, Sbf[pi][:, 0:64], start=False, stop=True)
                    e.matmul(pO2[1][:], A2b[par][:, C:2 * C],
                             vt[:, pi * 128 + 64:(pi + 1) * 128], start=True, stop=False)
                    e.matmul(pO2[1][:], rint, Sbf[pi][:, 64:128], start=False, stop=True)
                    e.matmul(P[0][0:64, 0:64], ktmpA[ch % 2][:, pi * 128:pi * 128 + 64],
                             vt[:, pi * 128:pi * 128 + 64], start=True, stop=True)
                    return e.matmul(P[0][64:128, 0:64],
                                    ktmpA[ch % 2][:, pi * 128 + 64:(pi + 1) * 128],
                                    vt[:, pi * 128 + 64:(pi + 1) * 128],
                                    start=True, stop=True, tile_position=(0, 64))
                c.op("tensor", stn, avmm)
                marks[("AV", u)] = stn.n
                c.wait("vector", stn, marks[("AV", u)])
                if ("sbf", g, pi) in marks:
                    c.wait("vector", ss, marks[("sbf", g, pi)])
                c.op("vector", sv, lambda e, pi=pi: e.scalar_tensor_tensor(
                    Sst[pi][0:64, 0:64], Sst[pi][0:64, 0:64], e128[0:64, pi:pi + 1],
                    P[0][0:64, 0:64], AL.mult, AL.add))
                c.op("vector", sv, lambda e, pi=pi: e.scalar_tensor_tensor(
                    Sst[pi][64:128, 64:128], Sst[pi][64:128, 64:128], e128[64:128, pi:pi + 1],
                    P[0][64:128, 0:64], AL.mult, AL.add))
                marks[("stt", u)] = sv.n

                def gn_bn(e, par=par):
                    e.bn_stats(b6[par][:, 0:6], pO2[0][:])
                    e.bn_stats(b6[par][:, 6:12], pO2[1][:])
                    e.drain()
                    e.bn_aggr(stat[par][:, 0:2], b6[par][:, 0:6])
                    return e.bn_aggr(stat[par][:, 2:4], b6[par][:, 6:12])
                c.op("vector", sv, gn_bn)
                n_bn = sv.n
                c.wait("scalar", sv, marks[("stt", u)])
                c.op("scalar", ss, lambda e, pi=pi: e.copy(Sbf[pi][:], Sst[pi][:]))
                marks[("sbf", g, pi)] = ss.n
                c.wait("scalar", sv, n_bn)
                c.op("scalar", ss, lambda e, par=par: e.activation(
                    stat[par][:, 4:6],
                    stat[par][:, 0:4].rearrange("q (a b) -> q a b", b=2)[:, :, 1],
                    AF.Sqrt, bias=EPS))
                n_sq = ss.n

                def fin(u=u, par=par, pi=pi, ch=ch, gc=gc, n_sq=n_sq):
                    c.wait("vector", ss, n_sq)
                    c.op("vector", sv, lambda e, par=par: e.reciprocal(
                        stat[par][:, 6:8], stat[par][:, 4:6]))
                    c.emit("vector", lambda e: e.drain())
                    if gc - 2 in xn_dma:
                        c.wait("vector", sq, xn_dma[gc - 2])

                    def ap0(e, par=par, pi=pi, ch=ch):
                        xd = xnb[ch % 2]
                        return e.tensor_scalar(xd[:, pi * 128:pi * 128 + 64], pO2[0][:],
                                               stat[par][:, 0:1], stat[par][:, 6:7],
                                               AL.subtract, AL.mult)

                    def ap1(e, par=par, pi=pi, ch=ch):
                        xd = xnb[ch % 2]
                        return e.tensor_scalar(xd[:, pi * 128 + 64:(pi + 1) * 128], pO2[1][:],
                                               stat[par][:, 2:3], stat[par][:, 7:8],
                                               AL.subtract, AL.mult)
                    c.op("vector", sv, ap0)
                    c.op("vector", sv, ap1)
                    if ostg is not None:
                        c.op("vector", sv, lambda e, pi=pi, ch=ch: e.tensor_copy(
                            ostg[ch % 2][:, pi * 128:pi * 128 + 64], pO2[0][:]))
                        c.op("vector", sv, lambda e, pi=pi, ch=ch: e.tensor_copy(
                            ostg[ch % 2][:, pi * 128 + 64:(pi + 1) * 128], pO2[1][:]))
                        marks[("ocp", u)] = sv.n
                    marks[("gn", u)] = sv.n
                pend_fin = fin
            if pend_fin is not None:
                pend_fin()
                pend_fin = None
            c.wait("sync", sv, sv.n)
            c.dma("sync", sq, sc_xn[ch * 128:(ch + 1) * 128, g * 1024:(g + 1) * 1024],
                  xnb[ch % 2][:])
            if ostg is not None:
                c.dma("sync", sq, sc_O[ch * 128:(ch + 1) * 128, g * 1024:(g + 1) * 1024],
                      ostg[ch % 2][:])
            xn_dma[g * NCH + ch] = sq.n
    c.phase_close()
    c.barrier(bar, DMAS)


    mstack.close()

    y1f = [c.sb(f"y1f{i}", [128, TS], BF16, phase=False) for i in range(32)]
    lstat = c.sb("lstat", [128, 8 * 8], F32, phase=False)
    lst6 = c.sb("lst6", [128, 8 * 48], F32, phase=False)
    c.phase_open()
    gxn = [c.sb(f"gxn{i}", [128, TS], BF16) for i in range(32)]
    gt = [c.sb(f"gt{i}", [128, TS], BF16) for i in range(2)]
    xf = [c.sb(f"xf{i}", [128, TS], BF16) for i in range(2)]
    woslab = [c.sb("woslab0", [128, 32 * 512], BF16)]
    xres = [c.sb(f"xres{i}", [128, 512], F32) for i in range(2)]
    ypst = [c.sb(f"ypst{i}", [128, 512], F32) for i in range(2)]
    pm = [c.ps(f"pDm{i}", [128, 512]) for i in range(2)]

    gxn_mark = {}
    for i in range(32):
        if i >= 2:
            c.wait("sync", sv, gxn_mark[i - 2])
        c.dma("sync", sq, xf[i % 2][:], sc_xn[0:TS, i * 128:(i + 1) * 128], transpose=True)
        c.dma("sync", sq, gt[i % 2][:], sc_g[i * 128:(i + 1) * 128, :])
        n_l = sq.n
        c.wait("vector", sq, n_l)
        c.op("vector", sv, lambda e, i=i: e.tensor_scalar(
            xf[i % 2][:], xf[i % 2][:], gnwc[:, i:i + 1], gnbc[:, i:i + 1], AL.mult, AL.add))
        c.op("vector", sv, lambda e, i=i: e.tensor_mul(gxn[i][:], xf[i % 2][:], gt[i % 2][:]))
        gxn_mark[i] = sv.n
    n_gxn = sv.n

    def slab_ap2(wt, oh):
        return wt[oh].rearrange("(it q) oc -> q it oc", q=128)

    wo_mark = {}
    ev2 = {}
    st2 = {}
    for oh in range(8):
        if oh >= 1:
            c.wait("gpsimd", stn, wo_mark[oh - 1])
        c.dma("gpsimd", gq, woslab[0][:].rearrange("q (it oc) -> q it oc", oc=512),
              slab_ap2(wo, oh))
        n_sl = gq.n
        for tt in range(8):
            un = oh * 8 + tt
            c.wait("tensor", gq, n_sl)
            if un == 0:
                c.wait("tensor", sv, n_gxn)
            if un >= 2:
                c.wait("tensor", sv, ev2[un - 2])

            def mmwo(e, oh=oh, tt=tt):
                last = None
                sl = woslab[0]
                for i in range(32):
                    last = e.matmul(pm[tt % 2][:], gxn[i][:, tt * 128:(tt + 1) * 128],
                                    sl[:, i * 512:(i + 1) * 512],
                                    start=(i == 0), stop=(i == 31))
                return last
            c.op("tensor", stn, mmwo)
            if tt == 7:
                wo_mark[oh] = stn.n
            n_mm = stn.n
            c.wait("gpsimd", sv, ev2.get(un - 2, 0))
            c.dma("gpsimd", gq, xres[tt % 2][:],
                  x_tm[tt * 128:(tt + 1) * 128, oh * 512:(oh + 1) * 512])
            n_xr = gq.n
            c.wait("vector", stn, n_mm)
            c.wait("vector", gq, n_xr)
            if un >= 2:
                c.wait("vector", sq, st2[un - 2])
            c.op("vector", sv, lambda e, tt=tt: e.tensor_add(
                ypst[tt % 2][:], pm[tt % 2][:], xres[tt % 2][:]))
            ev2[un] = sv.n
            c.wait("sync", sv, ev2[un])
            c.dma("sync", sq, sc_yp[tt * 128:(tt + 1) * 128, oh * 512:(oh + 1) * 512],
                  ypst[tt % 2][:])
            st2[un] = sq.n
            c.op("vector", sv, lambda e, tt=tt, oh=oh: e.bn_stats(
                lst6[:, tt * 48 + oh * 6:tt * 48 + oh * 6 + 6], ypst[tt % 2][:]))
    n_yp_all = sq.n
    c.phase_close()
    c.barrier(bar, DMAS)
    c.phase_open()
    yp_t = [c.sb(f"yp_t{i}", [128, DM], F32) for i in range(2)]
    y1tm = [c.sb(f"y1tm{i}", [128, DM], BF16) for i in range(2)]
    pTd = []
    pTd.append(c.ps("pDtr0", [128, 128], BF16))
    pTd.append(c.ps("pDtr1", [128, 128], BF16))

    def ln_finalize(L, S6):
        c.emit("vector", lambda e: e.drain())
        for tt in range(8):
            c.op("vector", sv, lambda e, tt=tt, L=L, S6=S6: e.bn_aggr(
                L[:, tt * 8:tt * 8 + 2], S6[:, tt * 48:(tt + 1) * 48]))
        c.wait("scalar", sv)
        for tt in range(8):
            c.op("scalar", ss, lambda e, tt=tt, L=L: e.activation(
                L[:, tt * 8 + 3:tt * 8 + 4], L[:, tt * 8 + 1:tt * 8 + 2], AF.Sqrt, bias=EPS))
        c.wait("vector", ss)
        for tt in range(8):
            c.op("vector", sv, lambda e, tt=tt, L=L: e.reciprocal(
                L[:, tt * 8 + 4:tt * 8 + 5], L[:, tt * 8 + 3:tt * 8 + 4]))
        c.wait("scalar", sv)
        for tt in range(8):
            c.op("scalar", ss, lambda e, tt=tt, L=L: e.mul(
                L[:, tt * 8 + 6:tt * 8 + 7], L[:, tt * 8 + 4:tt * 8 + 5], -1.0))
        c.wait("vector", ss)
        for tt in range(8):
            c.op("vector", sv, lambda e, tt=tt, L=L: e.tensor_mul(
                L[:, tt * 8 + 5:tt * 8 + 6], L[:, tt * 8:tt * 8 + 1],
                L[:, tt * 8 + 6:tt * 8 + 7]))

    ln_finalize(lstat, lst6)
    n_ln1 = sv.n

    c.wait("sync", sq, n_yp_all)
    ap_mark = {}
    tr_cp = {}
    ap_s = {}
    for tt in range(8):
        if tt >= 2:
            c.wait("sync", ss, ap_s[tt - 2])
        c.dma("sync", sq, yp_t[tt % 2][:], sc_yp[tt * 128:(tt + 1) * 128, :])
        n_l = sq.n
        c.wait("scalar", sq, n_l)
        if tt == 0:
            c.wait("scalar", sv, n_ln1)
        if tt >= 2:
            c.wait("scalar", stn, ap_mark[tt - 2])
        c.op("scalar", ss, lambda e, tt=tt: e.activation(
            y1tm[tt % 2][:], yp_t[tt % 2][:], AF.Identity,
            bias=lstat[:, tt * 8 + 5:tt * 8 + 6], scale=lstat[:, tt * 8 + 4:tt * 8 + 5]))
        n_ap = ss.n
        ap_s[tt] = n_ap
        c.wait("tensor", ss, n_ap)
        for ft in range(32):
            fg = tt * 32 + ft
            if fg >= 2:
                c.wait("tensor", sv, tr_cp[fg - 2])
            c.op("tensor", stn, lambda e, tt=tt, ft=ft, fg=fg: e.transpose(
                pTd[fg % 2][:], y1tm[tt % 2][:, ft * 128:(ft + 1) * 128], id_s[:]))
            c.wait("vector", stn, stn.n)
            c.op("vector", sv, lambda e, tt=tt, ft=ft, fg=fg: e.tensor_copy(
                y1f[ft][:, tt * 128:(tt + 1) * 128], pTd[fg % 2][:]))
            tr_cp[fg] = sv.n
        ap_mark[tt] = stn.n
    c.phase_close()
    c.barrier(bar, DMAS)

    l2stat = c.sb("l2stat", [128, 8 * 8], F32, phase=False)
    l2st6 = c.sb("l2st6", [128, 8 * 48], F32, phase=False)
    c.phase_open()
    wfslab = [c.sb(f"wfslab{i}", [128, 32 * 512], BF16) for i in range(2)]
    zpst = [c.sb(f"zpst{i}", [128, 512], F32) for i in range(2)]
    y1rs = [c.sb(f"y1rs{i}", [128, 512], F32) for i in range(2)]
    ypr = [c.sb(f"ypr{i}", [128, 512], F32) for i in range(2)]
    fbs = c.sb("fbs", [128, DM], F32)
    pm2 = [c.ps(f"pEm{i}", [128, 512]) for i in range(2)]

    c.dma("sync", sq, fbs[:], fbr)
    n_fb = sq.n
    wf_mark = {}
    ev3 = {}
    st3 = {}
    y1_mark = {}
    for oh in range(8):
        if oh >= 2:
            c.wait("gpsimd", stn, wf_mark[oh - 2])
        c.dma("gpsimd", gq, wfslab[oh % 2][:].rearrange("q (it oc) -> q it oc", oc=512),
              slab_ap2(wf, oh))
        n_sl = gq.n
        c.wait("scalar", gq, n_sl)
        if oh >= 2:
            c.wait("scalar", stn, wf_mark[oh - 2])
        for it in range(32):
            c.op("scalar", ss, lambda e, oh=oh, it=it: e.sign(
                wfslab[oh % 2][:, it * 512:(it + 1) * 512],
                wfslab[oh % 2][:, it * 512:(it + 1) * 512]))
        n_sign = ss.n
        for tt in range(8):
            un = oh * 8 + tt
            c.wait("tensor", ss, n_sign)
            if un >= 2:
                c.wait("tensor", sv, ev3[un - 2])

            def mmwf(e, oh=oh, tt=tt):
                last = None
                sl = wfslab[oh % 2]
                for i in range(32):
                    last = e.matmul(pm2[tt % 2][:], y1f[i][:, tt * 128:(tt + 1) * 128],
                                    sl[:, i * 512:(i + 1) * 512],
                                    start=(i == 0), stop=(i == 31))
                return last
            c.op("tensor", stn, mmwf)
            if tt == 7:
                wf_mark[oh] = stn.n
            n_mm = stn.n
            if un >= 2:
                c.wait("gpsimd", ss, y1_mark[un - 2])
            c.dma("gpsimd", gq, ypr[tt % 2][:],
                  sc_yp[tt * 128:(tt + 1) * 128, oh * 512:(oh + 1) * 512])
            n_ypr = gq.n
            c.wait("scalar", gq, n_ypr)
            if un >= 2:
                c.wait("scalar", sv, ev3[un - 2])
            c.op("scalar", ss, lambda e, tt=tt: e.activation(
                y1rs[tt % 2][:], ypr[tt % 2][:], AF.Identity,
                bias=lstat[:, tt * 8 + 5:tt * 8 + 6], scale=lstat[:, tt * 8 + 4:tt * 8 + 5]))
            y1_mark[un] = ss.n
            c.wait("vector", stn, n_mm)
            if un == 0:
                c.wait("vector", sq, n_fb)
            c.wait("vector", ss, y1_mark[un])
            if un >= 2:
                c.wait("vector", sq, st3[un - 2])
            c.op("vector", sv, lambda e, tt=tt, oh=oh: e.tensor_add(
                zpst[tt % 2][:], pm2[tt % 2][:], fbs[:, oh * 512:(oh + 1) * 512]))
            c.op("vector", sv, lambda e, tt=tt: e.tensor_add(
                zpst[tt % 2][:], zpst[tt % 2][:], y1rs[tt % 2][:]))
            ev3[un] = sv.n
            c.wait("sync", sv, ev3[un])
            c.dma("sync", sq, sc_zp[tt * 128:(tt + 1) * 128, oh * 512:(oh + 1) * 512],
                  zpst[tt % 2][:])
            st3[un] = sq.n
            c.op("vector", sv, lambda e, tt=tt, oh=oh: e.bn_stats(
                l2st6[:, tt * 48 + oh * 6:tt * 48 + oh * 6 + 6], zpst[tt % 2][:]))
    n_zp_all = sq.n
    c.phase_close()
    c.barrier(bar, DMAS)
    c.phase_open()
    zp_t = [c.sb(f"zp_t2{i}", [128, DM], F32) for i in range(2)]
    zt = [c.sb(f"zt{i}", [128, DM], F32) for i in range(2)]

    ln_finalize(l2stat, l2st6)
    n_ln2 = sv.n

    c.wait("sync", sq, n_zp_all)
    zst = {}
    zap = {}
    for tt in range(8):
        if tt >= 2:
            c.wait("sync", ss, zap[tt - 2])
        c.dma("sync", sq, zp_t[tt % 2][:], sc_zp[tt * 128:(tt + 1) * 128, :])
        n_l = sq.n
        c.wait("scalar", sq, n_l)
        if tt == 0:
            c.wait("scalar", sv, n_ln2)
        if tt >= 2:
            c.wait("scalar", sq, zst[tt - 2])
        c.op("scalar", ss, lambda e, tt=tt: e.activation(
            zt[tt % 2][:], zp_t[tt % 2][:], AF.Identity,
            bias=l2stat[:, tt * 8 + 5:tt * 8 + 6], scale=l2stat[:, tt * 8 + 4:tt * 8 + 5]))
        zap[tt] = ss.n
        c.wait("sync", ss, ss.n)
        c.dma("sync", sq, out[tt * 128:(tt + 1) * 128, :], zt[tt % 2][:])
        zst[tt] = sq.n
    c.phase_close()

    if taps:
        c.wait("gpsimd", sq, sq.n)
        c.wait("gpsimd", ss, ss.n)
        c.wait("gpsimd", sv, sv.n)
        for name, (tap, src) in taps.items():
            c.dma("gpsimd", gq, tap, src)
    c.wait("sync", sq, sq.n)
    c.wait("gpsimd", gq, gq.n)

    with nc.Block() as block:
        c.run(block)
    c.gstack.close()
    return nc


_NC_CACHE = {}


def _get_nc(key=()):
    if key not in _NC_CACHE:
        _NC_CACHE[key] = build_program(debug_taps=key)
    return _NC_CACHE[key]


def _cols32(a):
    return np.ascontiguousarray(np.asarray(a, np.float32).reshape(32, 128).T)


def make_inputs(inputs):
    x = np.asarray(inputs["L0_attn_hidden"], np.float32).reshape(S, DM)
    xT_full = np.ascontiguousarray(x.T)
    xprev = np.concatenate([np.zeros((DM, 1), np.float32), xT_full[:, :-1]], axis=1)

    def wblocks(w, bs):
        wt = np.ascontiguousarray(np.asarray(w, np.float32).T)
        return np.ascontiguousarray(wt.reshape(DM, DM // bs, bs).transpose(1, 0, 2))

    wkb = wblocks(inputs["L0_attn_w_key"], 128)
    wvb = wblocks(inputs["L0_attn_w_value"], 128)
    wrb = wblocks(inputs["L0_attn_w_receptance"], 128)
    wgb = wblocks(inputs["L0_attn_w_gate"], 128)
    wob = wblocks(inputs["L0_attn_w_output"], 512)
    wfb = wblocks(inputs["L0_ffn_weight"], 512)
    mcols = np.ascontiguousarray(np.concatenate(
        [_cols32(inputs["L0_attn_time_mix_key"]),
         _cols32(inputs["L0_attn_time_mix_value"]),
         _cols32(inputs["L0_attn_time_mix_receptance"]),
         _cols32(inputs["L0_attn_time_mix_gate"])], axis=1))
    tdp = _cols32(inputs["L0_attn_time_decay"])
    tfp = _cols32(inputs["L0_attn_time_faaaa"])
    gnw = _cols32(inputs["L0_attn_gn_weight"])
    gnb = _cols32(inputs["L0_attn_gn_bias"])
    fbr = np.ascontiguousarray(np.broadcast_to(
        np.asarray(inputs["L0_ffn_bias"], np.float32)[None, :], (128, DM)))
    iota, masks, ident = _host_consts()
    in_maps = []
    for core in range(NCORE):
        t0 = core * TS
        xTs = np.ascontiguousarray(
            np.concatenate([xprev[:, t0:t0 + 1], xT_full[:, t0:t0 + TS]], axis=1))
        x_tms = np.ascontiguousarray(x[t0:t0 + TS])
        gap = np.array([1024.0 * (core - 1 - j) if j < core else 1e9 for j in range(8)],
                      np.float32)
        gapx = np.ascontiguousarray(np.broadcast_to(gap[None, :], (128, 8)))
        in_maps.append({
            "xT": xTs, "x_tm": x_tms, "wkT_b": wkb, "wvT_b": wvb, "wrT_b": wrb,
            "wgT_b": wgb, "woT_b": wob, "wfT_b": wfb, "mcols": mcols, "tdp": tdp,
            "tfp": tfp, "gnw": gnw, "gnb": gnb, "fbias_rep": fbr, "iota_c": iota,
            "mask_c": masks, "ident_c": ident, "gapix": gapx,
        })
    return in_maps


def kernel(**inputs):
    nc = _get_nc()
    in_maps = make_inputs(inputs)
    res = run_bass_kernel_spmd(nc, in_maps, core_ids=list(range(NCORE)))
    shards = [np.asarray(res.results[i]["out_sh"]) for i in range(NCORE)]
    return np.concatenate(shards, axis=0).reshape(1, S, DM)


# version 06
# speedup vs baseline: 1.0428x; 1.0428x over previous
import contextlib
import numpy as np
import ml_dtypes
import concourse.bass as bass
import concourse.mybir as mybir
from concourse.bass_utils import run_bass_kernel_spmd
from concourse.replica_groups import maybe_share_collective_output_space

F32 = mybir.dt.float32
BF16 = mybir.dt.bfloat16
U8 = mybir.dt.uint8
AF = mybir.ActivationFunctionType
AL = mybir.AluOpType
AX = mybir.AxisListType
BF = ml_dtypes.bfloat16

NCORE = 8
S = 8192
TS = S // NCORE
DM = 4096
NP = 32
C = 128
NCH = TS // C
NG = 4
GP = NP // NG
EPS = 1e-5
NVAR = 12


class Sem:
    def __init__(self, h):
        self.h = h
        self.n = 0


class Ctx:
    ENGS = ("sync", "gpsimd", "tensor", "vector", "scalar")

    def __init__(self, nc):
        self.nc = nc
        self.st = {e: [] for e in self.ENGS}
        self.gstack = contextlib.ExitStack()
        self.pstack = None

    def sem(self, name):
        return Sem(self.gstack.enter_context(self.nc.semaphore(name)))

    def sb(self, name, shape, dt, phase=True):
        stk = self.pstack if (phase and self.pstack is not None) else self.gstack
        return stk.enter_context(self.nc.sbuf_tensor(name, shape, dt))

    def ps(self, name, shape, dt=F32, phase=True):
        stk = self.pstack if (phase and self.pstack is not None) else self.gstack
        return stk.enter_context(self.nc.psum_tensor(name, shape, dt))

    def phase_open(self):
        assert self.pstack is None
        self.pstack = contextlib.ExitStack()

    def phase_close(self):
        self.pstack.close()
        self.pstack = None

    def emit(self, eng, fn):
        self.st[eng].append(fn)

    def op(self, eng, sem, fn):
        def run(e, fn=fn, h=sem.h):
            fn(e).then_inc(h, 1)
        self.st[eng].append(run)
        sem.n += 1
        return sem.n

    def dma(self, q, sem, o, i, transpose=False):
        def run(e, o=o, i=i, h=sem.h, tr=transpose):
            e.dma_start(o, i, transpose=True).then_inc(h, 16) if tr else \
                e.dma_start(o, i).then_inc(h, 16)
        self.st[q].append(run)
        sem.n += 16
        return sem.n

    def wait(self, eng, sem, val=None):
        v = sem.n if val is None else val
        if v > 0:
            self.emit(eng, lambda e, h=sem.h, v=v: e.wait_ge(h, v))

    def barrier(self, bsem, dma_sems):
        for q, s in dma_sems.items():
            self.wait(q, s, s.n)
        for eng in self.ENGS:
            self.op(eng, bsem, lambda e: e.nop())
        for eng in self.ENGS:
            self.wait(eng, bsem, bsem.n)

    def run(self, block):
        def mk(lst):
            def body(eng):
                for f in lst:
                    f(eng)
            return body
        block.sync(mk(self.st["sync"]))
        block.gpsimd(mk(self.st["gpsimd"]))
        block.tensor(mk(self.st["tensor"]))
        block.vector(mk(self.st["vector"]))
        block.scalar(mk(self.st["scalar"]))


def _host_consts():
    j = np.arange(C)
    pats = [3 - (j % 8), 7 - (j % 8), 15 - (j % 16), 31 - (j % 32), 63 - (j % 64),
            (j % 8) - 4, j % 8, j % 16, j % 32, j % 64, j, 127 - j]
    iota = np.concatenate([p.astype(np.float32) for p in pats])[None, :]
    iota = np.ascontiguousarray(np.broadcast_to(iota, (128, NVAR * C)))
    ii = j[None, :]
    jj = j[:, None]
    lower = ii > jj

    def adj(L):
        return lower & ((ii // L) == (jj // L) + 1) & ((ii // (2 * L)) == (jj // (2 * L)))
    masks = [lower & ((ii // 8) == (jj // 8)), adj(8), adj(16), adj(32), adj(64), ii == jj]
    masks = np.ascontiguousarray(np.concatenate([m.astype(np.uint8) for m in masks], axis=1))
    ident = np.ascontiguousarray(np.eye(128, dtype=np.float32)).astype(BF)
    return iota, masks, ident


def build_program(debug_taps=()):
    nc = bass.Bass("TRN2", target_bir_lowering=False)
    c = Ctx(nc)
    DT = nc.dram_tensor
    IN = lambda n, sh, dt=F32: DT(n, sh, dt, kind="ExternalInput").ap()
    xT = IN("xT", [DM, TS + 1])
    x_tm = IN("x_tm", [TS, DM])
    wk = IN("wkT_b", [32, DM, 128])
    wv = IN("wvT_b", [32, DM, 128])
    wr = IN("wrT_b", [32, DM, 128])
    wg = IN("wgT_b", [32, DM, 128])
    wo = IN("woT_b", [8, DM, 512])
    wf = IN("wfT_b", [8, DM, 512])
    mcols = IN("mcols", [128, 4 * 32])
    tdp = IN("tdp", [128, 32])
    tfp = IN("tfp", [128, 32])
    gnw = IN("gnw", [128, 32])
    gnb = IN("gnb", [128, 32])
    fbr = IN("fbias_rep", [128, DM])
    iota_c = IN("iota_c", [128, NVAR * C])
    mask_c = IN("mask_c", [128, 6 * C], U8)
    id_c = IN("ident_c", [128, 128], BF16)
    gapix = IN("gapix", [128, 8])
    out = DT("out_sh", [TS, DM], F32, kind="ExternalOutput").ap()

    sc_k = DT("sc_k", [DM, TS], BF16, kind="Internal").ap()
    sc_v = DT("sc_v", [DM, TS], BF16, kind="Internal").ap()
    sc_r = DT("sc_r", [DM, TS], BF16, kind="Internal").ap()
    sc_g = DT("sc_g", [DM, TS], BF16, kind="Internal").ap()
    sc_xn = DT("sc_xn", [TS, DM], BF16, kind="Internal").ap()
    sc_O = DT("sc_O", [TS, DM], F32, kind="Internal").ap()
    sc_yp = DT("sc_yp", [TS, DM], F32, kind="Internal").ap()
    sc_zp = DT("sc_zp", [TS, DM], F32, kind="Internal").ap()
    cc_in = DT("cc_in", [NP, 128, 128], F32, kind="Internal").ap()
    aspace = maybe_share_collective_output_space("AllGather", [list(range(NCORE))])
    cc_out = DT("cc_out", [NCORE, NP, 128, 128], F32, kind="Internal", addr_space=aspace).ap()

    tap_specs = {"k": (sc_k, BF16), "v": (sc_v, BF16), "r": (sc_r, BF16), "g": (sc_g, BF16),
                 "xn": (sc_xn, BF16), "yp": (sc_yp, F32), "zp": (sc_zp, F32),
                 "B": (cc_in, F32), "Bg": (cc_out, F32), "O": (sc_O, F32)}
    taps = {}
    for name in debug_taps:
        src, dt = tap_specs[name]
        taps[name] = (DT("tap_" + name, list(src.shape), dt, kind="ExternalOutput").ap(), src)

    sq = c.sem("sq")
    gq = c.sem("gq")
    sv = c.sem("sv")
    ss = c.sem("ss")
    stn = c.sem("stn")
    scc = c.sem("scc")
    bar = c.sem("bar")
    DMAS = {"sync": sq, "gpsimd": gq}

    iota_s = c.sb("iota_s", [128, NVAR * C], F32, phase=False)
    mask_s = c.sb("mask_s", [128, 6 * C], U8, phase=False)
    id_s = c.sb("id_s", [128, 128], BF16, phase=False)
    wcol = c.sb("wcol", [128, 32], F32, phase=False)
    tfcol = c.sb("tfcol", [128, 32], F32, phase=False)
    mc = c.sb("mc", [128, 4 * 32], F32, phase=False)
    gnwc = c.sb("gnwc", [128, 32], F32, phase=False)
    gnbc = c.sb("gnbc", [128, 32], F32, phase=False)
    gapix_s = c.sb("gapix_s", [128, 8], F32, phase=False)
    epsc = c.sb("epsc", [128, 1], F32, phase=False)
    negc = c.sb("negc", [128, 1], F32, phase=False)
    c.op("vector", sv, lambda e: e.memset(epsc[:], EPS))
    c.op("vector", sv, lambda e: e.memset(negc[:], -1.0))
    nc.const_aps.register(F32, EPS, epsc[:, 0:1])
    nc.const_aps.register(F32, -1.0, negc[:, 0:1])

    for dst, src in ((iota_s, iota_c), (mask_s, mask_c), (id_s, id_c), (wcol, tdp),
                     (tfcol, tfp), (mc, mcols), (gnwc, gnw), (gnbc, gnb), (gapix_s, gapix)):
        c.dma("sync", sq, dst[:], src)
    c.wait("scalar", sq, sq.n)
    c.op("scalar", ss, lambda e: e.activation(wcol[:], wcol[:], AF.Exp))
    c.wait("vector", ss)
    c.op("vector", sv, lambda e: e.tensor_scalar_mul(wcol[:], wcol[:], -1.0))
    n_wcol = sv.n
    c.wait("tensor", sv, n_wcol)
    c.wait("scalar", sv, n_wcol)

    c.phase_open()
    x_sb = [c.sb(f"x_sb{i}", [128, TS + 1], BF16) for i in range(32)]
    mix = [c.sb(f"mix{i}", [128, TS], BF16) for i in range(32)]
    tmpm = [c.sb(f"tmpm{i}", [128, TS], BF16) for i in range(2)]
    wslab = [c.sb(f"wslab{i}", [128, DM], BF16) for i in range(2)]
    pstage = [c.sb(f"pstage{i}", [128, TS], BF16) for i in range(2)]
    pkp = [c.ps(f"pk{i}", [128, 512]) for i in range(4)]

    for i in range(32):
        c.dma("gpsimd", gq, x_sb[i][:], xT[i * 128:(i + 1) * 128, :])
    n_xload = gq.n

    def slab_ap(wt, o):
        return wt[o].rearrange("(it q) oc -> q it oc", q=128)

    projs = [(wk, 0, sc_k, None), (wv, 1, sc_v, None), (wr, 2, sc_r, None), (wg, 3, sc_g, AF.Silu)]
    mm_mark = {}
    ev_mark = {}
    st_mark = {}
    for pj, (wt, mi, scr, act) in enumerate(projs):
        c.wait("vector", gq, n_xload)
        if pj > 0:
            c.wait("vector", stn, mm_mark[(pj - 1, 31)])
        for i in range(32):
            t = tmpm[i % 2]
            cur = x_sb[i][:, 1:TS + 1]
            prv = x_sb[i][:, 0:TS]
            c.op("vector", sv, lambda e, t=t, cur=cur, prv=prv: e.tensor_sub(t[:], cur, prv))
            c.op("vector", sv, lambda e, t=t, i=i, mi=mi, prv=prv:
                 e.scalar_tensor_tensor(mix[i][:], t[:], mc[:, mi * 32 + i:mi * 32 + i + 1],
                                        prv, AL.mult, AL.add))
        n_mixv = sv.n
        for o in range(32):
            gk = (pj, o)
            if o >= 2:
                c.wait("gpsimd", stn, mm_mark[(pj, o - 2)])
            elif pj > 0:
                c.wait("gpsimd", stn, mm_mark[(pj - 1, 30 + o)])
            c.dma("gpsimd", gq, wslab[o % 2][:].rearrange("q (it oc) -> q it oc", oc=128),
                  slab_ap(wt, o))
            n_sl = gq.n
            c.wait("tensor", gq, n_sl)
            if o == 0:
                c.wait("tensor", sv, n_mixv)
            pk2 = (pj, o - 2) if o >= 2 else ((pj - 1, 30 + o) if pj > 0 else None)
            if pk2 is not None:
                sE, nE = ev_mark[pk2]
                c.wait("tensor", sE, nE)
            pA, pB = pkp[(o % 2) * 2], pkp[(o % 2) * 2 + 1]

            def mms(e, o=o, pA=pA, pB=pB):
                w2 = wslab[o % 2]
                last = None
                for i in range(32):
                    lhs = w2[:, i * 128:(i + 1) * 128]
                    e.matmul(pA[:], lhs, mix[i][:, 0:512], start=(i == 0), stop=(i == 31))
                    last = e.matmul(pB[:], lhs, mix[i][:, 512:1024], start=(i == 0), stop=(i == 31))
                return last
            c.op("tensor", stn, mms)
            mm_mark[gk] = stn.n
            eng = "vector" if act is None else "scalar"
            esem = sv if act is None else ss
            c.wait(eng, stn, mm_mark[gk])
            if pk2 is not None and pk2 in st_mark:
                c.wait(eng, sq, st_mark[pk2])
            stg = pstage[o % 2]
            if act is None:
                c.op("vector", sv, lambda e, stg=stg, pA=pA: e.tensor_copy(stg[:, 0:512], pA[:]))
                c.op("vector", sv, lambda e, stg=stg, pB=pB: e.tensor_copy(stg[:, 512:1024], pB[:]))
            else:
                c.op("scalar", ss, lambda e, stg=stg, pA=pA, act=act: e.activation(stg[:, 0:512], pA[:], act))
                c.op("scalar", ss, lambda e, stg=stg, pB=pB, act=act: e.activation(stg[:, 512:1024], pB[:], act))
            ev_mark[gk] = (esem, esem.n)
            c.wait("sync", esem, esem.n)
            c.dma("sync", sq, scr[o * 128:(o + 1) * 128, :], stg[:])
            st_mark[gk] = sq.n
    c.phase_close()
    c.barrier(bar, DMAS)

    mstack = contextlib.ExitStack()
    MSB = lambda n, sh, dt: mstack.enter_context(nc.sbuf_tensor(n, sh, dt))
    ktm = [MSB(f"ktm{i}", [128, TS], BF16) for i in range(NCH)]
    vtm = [MSB(f"vtm{i}", [128, TS], BF16) for i in range(NCH)]
    eupdT_all = MSB("eupdT_all", [128, GP * 128], BF16)
    e128 = MSB("e128", [128, GP], F32)
    Sst = [MSB(f"Sst{i}", [128, 128], F32) for i in range(GP)]
    ktmpA = [MSB(f"ktmpA{i}", [128, GP * 128], BF16) for i in range(2)]
    expn = MSB("expn", [128, NVAR * C], F32)
    eupd_f = MSB("eupd_f", [128, 128], BF16)

    c.phase_open()
    pU = []
    pU.append(c.ps("p1U0", [128, 64]))
    c.ps("p1pad0", [128, 448])
    pU.append(c.ps("p1U1", [128, 64]))
    c.ps("p1pad1", [128, 448])
    pTr = c.ps("p1Tr", [128, 128], BF16)

    def build_pair_patterns(pi, p, full):
        c.wait("vector", ss)
        c.wait("scalar", stn)
        if full:
            c.op("vector", sv, lambda e, p=p: e.tensor_scalar(
                expn[:], iota_s[:], wcol[:, p:p + 1], None, AL.mult))
            c.wait("scalar", sv)
            c.op("scalar", ss, lambda e, pi=pi: e.activation(pat2[pi][:], expn[:], AF.Exp))
            c.wait("tensor", ss)
            c.wait("tensor", sv)
            c.op("tensor", stn, lambda e, pi=pi: e.transpose(
                trg_ap(), pat2[pi][:, 11 * C:12 * C], id_s[:]))
        else:
            c.op("vector", sv, lambda e, p=p: e.tensor_scalar(
                expn[:, 0:C], iota_s[:, 11 * C:12 * C], wcol[:, p:p + 1], None, AL.mult))
            c.wait("scalar", sv)
            c.op("scalar", ss, lambda e: e.activation(eupd_f[:], expn[:, 0:C], AF.Exp))
            c.wait("tensor", ss)
            c.wait("tensor", sv)
            c.op("tensor", stn, lambda e: e.transpose(trg_ap(), eupd_f[:], id_s[:]))
        c.wait("vector", stn)
        c.op("vector", sv, lambda e, pi=pi: e.tensor_copy(
            eupdT_all[:, pi * 128:(pi + 1) * 128], trg_ap()))
        c.op("vector", sv, lambda e, p=p, pi=pi: e.tensor_scalar_mul(
            e128[:, pi:pi + 1], wcol[:, p:p + 1], 128.0))
        c.wait("scalar", sv)
        c.op("scalar", ss, lambda e, pi=pi: e.activation(e128[:, pi:pi + 1], e128[:, pi:pi + 1], AF.Exp))

    trg = [pTr]

    def trg_ap():
        t = trg[0]
        ap = t[:]
        if ap.dtype != BF16:
            ap = ap.bitcast(BF16)[:, 0:128]
        return ap
    pat2 = None
    upd_mark = {}
    stt_mark = {}
    for g in range(NG):
        c.wait("sync", stn, stn.n)
        c.wait("sync", sv, sv.n)
        c.wait("vector", sq, sq.n)
        for tt in range(NCH):
            c.dma("sync", sq, ktm[tt][:], sc_k[g * 1024:(g + 1) * 1024, tt * 128:(tt + 1) * 128],
                  transpose=True)
            c.dma("sync", sq, vtm[tt][:], sc_v[g * 1024:(g + 1) * 1024, tt * 128:(tt + 1) * 128],
                  transpose=True)
        n_ld = sq.n
        for pi in range(GP):
            build_pair_patterns(pi, g * GP + pi, full=False)
        c.wait("vector", ss)
        for pi in range(GP):
            c.op("vector", sv, lambda e, pi=pi: e.memset(Sst[pi][:], 0.0))
        c.wait("vector", sq, n_ld)
        c.wait("tensor", sq, n_ld)
        c.wait("tensor", ss)
        ch_upd = {}
        for ch in range(NCH):
            cpar = ch % 2
            if ch >= 2:
                c.wait("vector", stn, ch_upd[ch - 2])
            c.op("vector", sv, lambda e, ch=ch, cpar=cpar: e.tensor_mul(
                ktmpA[cpar][:], ktm[ch][:], eupdT_all[:]))
            n_kt = sv.n
            c.wait("tensor", sv, n_kt)
            for pi in range(GP):
                u = (g * NCH + ch) * GP + pi
                par = u % 2
                if u >= 2:
                    c.wait("tensor", sv, stt_mark[u - 2])

                def updmm(e, ch=ch, pi=pi, par=par, cpar=cpar):
                    e.matmul(pU[par][0:64, :], ktmpA[cpar][:, pi * 128:pi * 128 + 64],
                             vtm[ch][:, pi * 128:pi * 128 + 64], start=True, stop=True)
                    return e.matmul(pU[par][64:128, :],
                                    ktmpA[cpar][:, pi * 128 + 64:(pi + 1) * 128],
                                    vtm[ch][:, pi * 128 + 64:(pi + 1) * 128],
                                    start=True, stop=True, tile_position=(0, 64))
                c.op("tensor", stn, updmm)
                upd_mark[u] = stn.n
                c.wait("vector", stn, upd_mark[u])
                c.op("vector", sv, lambda e, pi=pi, par=par: e.scalar_tensor_tensor(
                    Sst[pi][0:64, 0:64], Sst[pi][0:64, 0:64], e128[0:64, pi:pi + 1],
                    pU[par][0:64, :], AL.mult, AL.add))
                c.op("vector", sv, lambda e, pi=pi, par=par: e.scalar_tensor_tensor(
                    Sst[pi][64:128, 64:128], Sst[pi][64:128, 64:128], e128[64:128, pi:pi + 1],
                    pU[par][64:128, :], AL.mult, AL.add))
                stt_mark[u] = sv.n
            ch_upd[ch] = stn.n
        c.wait("sync", sv, sv.n)
        for pi in range(GP):
            c.dma("sync", sq, cc_in[g * GP + pi], Sst[pi][:])
    c.wait("gpsimd", sq, sq.n)
    c.emit("gpsimd", lambda e: e.collective_compute(
        "AllGather", AL.bypass, replica_groups=[list(range(NCORE))],
        ins=[cc_in[:]], outs=[cc_out[:]]).then_inc(scc.h, 1))
    scc.n += 1
    c.phase_close()
    c.barrier(bar, DMAS)
    c.wait("sync", scc, scc.n)
    c.wait("gpsimd", scc, scc.n)
    c.wait("vector", scc, scc.n)
    c.wait("tensor", scc, scc.n)
    c.wait("scalar", scc, scc.n)

    c.phase_open()
    kf = [c.sb(f"kf{i}", [128, TS], BF16) for i in range(GP)]
    rf = [c.sb(f"rf{i}", [128, TS], BF16) for i in range(GP)]
    pat2 = [c.sb(f"pat{i}", [128, NVAR * C], BF16) for i in range(GP)]
    khat = [c.sb(f"khat{i}", [128, 5 * C], BF16) for i in range(2)]
    ktfb = [c.sb(f"ktf{i}", [128, C], BF16) for i in range(2)]
    rhat = [c.sb(f"rhat{i}", [128, 6 * C], BF16) for i in range(2)]
    A2f = [c.sb(f"A2f{i}", [128, 2 * C], F32) for i in range(2)]
    rstg = [c.sb(f"rstg{i}", [128, 96], BF16) for i in range(2)]
    A2b = [c.sb(f"A2b{i}", [128, 2 * C], BF16) for i in range(2)]
    Sbf = [c.sb(f"Sbf{i}", [128, 128], BF16) for i in range(GP)]
    xnb = [c.sb(f"xnb{i}", [128, TS], BF16) for i in range(2)]
    Bg = c.sb("Bg", [128, NCORE * 128], F32)
    egap = c.sb("egap", [128, 8], F32)
    b6 = [c.sb(f"b6_{i}", [128, 12], F32) for i in range(2)]
    stat = [c.sb(f"stat{i}", [128, 12], F32) for i in range(2)]
    zz = c.sb("zz", [128, C], F32)
    ostg = [c.sb(f"ostg{i}", [128, TS], F32) for i in range(2)] if "O" in taps else None
    P = [c.ps(f"p2P{i}", [128, 128]) for i in range(6)]
    pO2 = [c.ps(f"p2O{h}", [128, 64]) for h in range(2)]

    c.op("vector", sv, lambda e: e.memset(zz[:], 0.0))

    marks = {}
    xn_dma = {}
    build_mark = {}
    pend_fin = None
    for g in range(NG):
        c.wait("sync", stn, stn.n)
        c.wait("sync", sv, sv.n)
        c.wait("sync", ss, ss.n)
        c.wait("vector", stn, stn.n)
        c.wait("vector", sq, sq.n)
        for pi in range(GP):
            p = g * GP + pi
            c.dma("sync", sq, kf[pi][:], sc_k[p * 128:(p + 1) * 128, :])
            c.dma("sync", sq, rf[pi][:], sc_r[p * 128:(p + 1) * 128, :])
        for tt in range(NCH):
            c.dma("sync", sq, ktm[tt][:], sc_k[g * 1024:(g + 1) * 1024, tt * 128:(tt + 1) * 128],
                  transpose=True)
            c.dma("sync", sq, vtm[tt][:], sc_v[g * 1024:(g + 1) * 1024, tt * 128:(tt + 1) * 128],
                  transpose=True)
        n_ld = sq.n
        trg[0] = P[1]
        for pi in range(GP):
            p = g * GP + pi
            build_pair_patterns(pi, p, full=True)
            c.op("vector", sv, lambda e, p=p: e.tensor_scalar(
                egap[:], gapix_s[:], wcol[:, p:p + 1], None, AL.mult))
            c.wait("scalar", sv)
            c.op("scalar", ss, lambda e: e.activation(egap[:], egap[:], AF.Exp))
            c.wait("gpsimd", sv, sv.n)
            c.dma("gpsimd", gq, Bg[:].rearrange("q (j f) -> q j f", j=NCORE),
                  cc_out[:, p, :, :].rearrange("j q f -> q j f"))
            c.wait("vector", gq)
            c.wait("vector", ss)
            c.op("vector", sv, lambda e, pi=pi: e.memset(Sst[pi][:], 0.0))
            for jc in range(NCORE):
                c.op("vector", sv, lambda e, pi=pi, jc=jc: e.scalar_tensor_tensor(
                    Sst[pi][:], Bg[:, jc * 128:(jc + 1) * 128], egap[:, jc:jc + 1],
                    Sst[pi][:], AL.mult, AL.add))
            c.op("vector", sv, lambda e, pi=pi: e.tensor_copy(Sbf[pi][:], Sst[pi][:]))
        c.wait("vector", sq, n_ld)
        c.wait("tensor", sq, n_ld)
        c.wait("scalar", sq, n_ld)
        def emit_builds(u):
            lc = u - g * NCH * GP
            ch_, pi_ = lc // GP, lc % GP
            par_ = u % 2
            kfc = kf[pi_][:, ch_ * C:(ch_ + 1) * C]
            rfc = rf[pi_][:, ch_ * C:(ch_ + 1) * C]
            if u >= 2:
                c.wait("vector", stn, marks[("ph1", u - 2)])
                c.wait("scalar", stn, marks[("ph1", u - 2)])
            c.op("vector", sv, lambda e, par=par_, kfc=kfc, pi=pi_: e.tensor_tensor(
                khat[par][:].rearrange("q (v f) -> q v f", v=5),
                kfc.rearrange("q (v f) -> q v f", v=1).broadcast_to((128, 5, C)),
                pat2[pi][:, 0:5 * C].rearrange("q (v f) -> q v f", v=5), AL.mult))
            c.op("vector", sv, lambda e, par=par_, rfc=rfc, pi=pi_: e.tensor_tensor(
                rhat[par][:].rearrange("q (v f) -> q v f", v=6),
                rfc.rearrange("q (v f) -> q v f", v=1).broadcast_to((128, 6, C)),
                pat2[pi][:, 5 * C:11 * C].rearrange("q (v f) -> q v f", v=6), AL.mult))
            c.op("scalar", ss, lambda e, par=par_, kfc=kfc, p=g * GP + pi_: e.mul(
                ktfb[par][:], kfc, tfcol[:, p:p + 1]))
            build_mark[u] = (sv.n, ss.n)

        u_base = g * NCH * GP
        u_last = u_base + NCH * GP - 1
        emit_builds(u_base)
        for ch in range(NCH):
            cpar = ch % 2
            if ch >= 2:
                c.wait("vector", stn, marks[("AV", (g * NCH + ch - 2) * GP + GP - 1)])
            c.op("vector", sv, lambda e, ch=ch, cpar=cpar: e.tensor_mul(
                ktmpA[cpar][:], ktm[ch][:], eupdT_all[:]))
            for pi in range(GP):
                u = (g * NCH + ch) * GP + pi
                par = u % 2
                rf_c = rf[pi][:, ch * C:(ch + 1) * C]
                gc = g * NCH + ch
                c.wait("tensor", sv, build_mark[u][0])
                c.wait("tensor", ss, build_mark[u][1])
                if u >= 1:
                    c.wait("tensor", sv, marks[("sel1", u - 1)])
                    c.wait("tensor", ss, marks[("rect", u - 1)])
                    c.wait("tensor", sv, marks[("stt", u - 1)])

                def pieces(e, par=par, rf_c=rf_c, h=0):
                    hs = slice(64 * h, 64 * (h + 1))
                    tp = None if h == 0 else (64, 0)
                    for px in range(5):
                        e.matmul(P[px][:], khat[par][hs, px * C:(px + 1) * C],
                                 rhat[par][hs, px * C:(px + 1) * C], start=True, stop=True,
                                 tile_position=tp)
                    return e.matmul(P[5][:], ktfb[par][hs, :], rf_c[hs, :],
                                    start=True, stop=True, tile_position=tp)
                c.op("tensor", stn, lambda e, par=par, rf_c=rf_c: pieces(e, par, rf_c, 0))
                marks[("ph0", u)] = stn.n
                c.wait("scalar", stn, marks[("ph0", u)])

                def stage0(e, par=par):
                    e.copy(rstg[par][0:64, 0:64], P[4][0:64, 64:128])
                    e.copy(rstg[par][0:32, 64:96], P[3][0:32, 32:64])
                    return e.copy(rstg[par][64:96, 64:96], P[3][64:96, 96:128])
                c.op("scalar", ss, stage0)
                marks[("stg", u)] = ss.n
                c.wait("vector", stn, marks[("ph0", u)])

                def sel(e, par=par, h=0):
                    dst = A2f[par][:, h * C:(h + 1) * C]
                    e.select(dst, mask_s[:, 0:C], P[0][:], zz[:])
                    e.select(dst, mask_s[:, C:2 * C], P[1][:], dst)
                    e.select(dst, mask_s[:, 2 * C:3 * C], P[2][:], dst)
                    return e.select(dst, mask_s[:, 5 * C:6 * C], P[5][:], dst)
                c.op("vector", sv, lambda e, par=par: sel(e, par, 0))
                marks[("sel0", u)] = sv.n
                if u < u_last:
                    emit_builds(u + 1)
                c.wait("tensor", sv, marks[("sel0", u)])
                c.wait("tensor", ss, marks[("stg", u)])
                c.op("tensor", stn, lambda e, par=par, rf_c=rf_c: pieces(e, par, rf_c, 1))
                marks[("ph1", u)] = stn.n
                c.wait("vector", stn, marks[("ph1", u)])

                def sel1(e, par=par):
                    sel(e, par, 1)
                    return e.tensor_copy(A2b[par][:], A2f[par][:])
                c.op("vector", sv, sel1)
                marks[("sel1", u)] = sv.n
                c.wait("scalar", sv, marks[("sel1", u)])

                def rects(e, par=par):
                    e.copy(A2b[par][0:64, 64:128], rstg[par][0:64, 0:64])
                    e.copy(A2b[par][0:32, 32:64], rstg[par][0:32, 64:96])
                    e.copy(A2b[par][64:96, 96:128], rstg[par][64:96, 64:96])
                    e.copy(A2b[par][0:64, C + 64:C + 128], P[4][0:64, 64:128])
                    e.copy(A2b[par][0:32, C + 32:C + 64], P[3][0:32, 32:64])
                    return e.copy(A2b[par][64:96, C + 96:C + 128], P[3][64:96, 96:128])
                c.op("scalar", ss, rects)
                marks[("rect", u)] = ss.n
                if pend_fin is not None:
                    pend_fin()
                    pend_fin = None
                c.wait("tensor", sv, marks[("sel1", u)])
                c.wait("tensor", ss, marks[("rect", u)])
                if u >= 1:
                    c.wait("tensor", sv, marks[("gn", u - 1)])
                    if ostg is not None:
                        c.wait("tensor", sv, marks[("ocp", u - 1)])

                def avmm(e, par=par, ch=ch, pi=pi):
                    vt = vtm[ch]
                    rint = rhat[par][:, 5 * C:6 * C]
                    e.matmul(pO2[0][:], A2b[par][:, 0:C], vt[:, pi * 128:pi * 128 + 64],
                             start=True, stop=False)
                    e.matmul(pO2[0][:], rint, Sbf[pi][:, 0:64], start=False, stop=True)
                    e.matmul(pO2[1][:], A2b[par][:, C:2 * C],
                             vt[:, pi * 128 + 64:(pi + 1) * 128], start=True, stop=False)
                    e.matmul(pO2[1][:], rint, Sbf[pi][:, 64:128], start=False, stop=True)
                    e.matmul(P[0][0:64, 0:64], ktmpA[ch % 2][:, pi * 128:pi * 128 + 64],
                             vt[:, pi * 128:pi * 128 + 64], start=True, stop=True)
                    return e.matmul(P[0][64:128, 0:64],
                                    ktmpA[ch % 2][:, pi * 128 + 64:(pi + 1) * 128],
                                    vt[:, pi * 128 + 64:(pi + 1) * 128],
                                    start=True, stop=True, tile_position=(0, 64))
                c.op("tensor", stn, avmm)
                marks[("AV", u)] = stn.n
                c.wait("vector", stn, marks[("AV", u)])
                if ("sbf", g, pi) in marks:
                    c.wait("vector", ss, marks[("sbf", g, pi)])
                c.op("vector", sv, lambda e, pi=pi: e.scalar_tensor_tensor(
                    Sst[pi][0:64, 0:64], Sst[pi][0:64, 0:64], e128[0:64, pi:pi + 1],
                    P[0][0:64, 0:64], AL.mult, AL.add))
                c.op("vector", sv, lambda e, pi=pi: e.scalar_tensor_tensor(
                    Sst[pi][64:128, 64:128], Sst[pi][64:128, 64:128], e128[64:128, pi:pi + 1],
                    P[0][64:128, 0:64], AL.mult, AL.add))
                marks[("stt", u)] = sv.n

                def gn_bn(e, par=par):
                    e.bn_stats(b6[par][:, 0:6], pO2[0][:])
                    e.bn_stats(b6[par][:, 6:12], pO2[1][:])
                    e.drain()
                    e.bn_aggr(stat[par][:, 0:2], b6[par][:, 0:6])
                    return e.bn_aggr(stat[par][:, 2:4], b6[par][:, 6:12])
                c.op("vector", sv, gn_bn)
                n_bn = sv.n
                c.wait("scalar", sv, marks[("stt", u)])
                c.op("scalar", ss, lambda e, pi=pi: e.copy(Sbf[pi][:], Sst[pi][:]))
                marks[("sbf", g, pi)] = ss.n
                c.wait("scalar", sv, n_bn)
                c.op("scalar", ss, lambda e, par=par: e.activation(
                    stat[par][:, 4:6],
                    stat[par][:, 0:4].rearrange("q (a b) -> q a b", b=2)[:, :, 1],
                    AF.Sqrt, bias=EPS))
                n_sq = ss.n

                def fin(u=u, par=par, pi=pi, ch=ch, gc=gc, n_sq=n_sq):
                    c.wait("vector", ss, n_sq)
                    c.op("vector", sv, lambda e, par=par: e.reciprocal(
                        stat[par][:, 6:8], stat[par][:, 4:6]))
                    c.emit("vector", lambda e: e.drain())
                    if gc - 2 in xn_dma:
                        c.wait("vector", sq, xn_dma[gc - 2])

                    def ap0(e, par=par, pi=pi, ch=ch):
                        xd = xnb[ch % 2]
                        return e.tensor_scalar(xd[:, pi * 128:pi * 128 + 64], pO2[0][:],
                                               stat[par][:, 0:1], stat[par][:, 6:7],
                                               AL.subtract, AL.mult)

                    def ap1(e, par=par, pi=pi, ch=ch):
                        xd = xnb[ch % 2]
                        return e.tensor_scalar(xd[:, pi * 128 + 64:(pi + 1) * 128], pO2[1][:],
                                               stat[par][:, 2:3], stat[par][:, 7:8],
                                               AL.subtract, AL.mult)
                    c.op("vector", sv, ap0)
                    c.op("vector", sv, ap1)
                    if ostg is not None:
                        c.op("vector", sv, lambda e, pi=pi, ch=ch: e.tensor_copy(
                            ostg[ch % 2][:, pi * 128:pi * 128 + 64], pO2[0][:]))
                        c.op("vector", sv, lambda e, pi=pi, ch=ch: e.tensor_copy(
                            ostg[ch % 2][:, pi * 128 + 64:(pi + 1) * 128], pO2[1][:]))
                        marks[("ocp", u)] = sv.n
                    marks[("gn", u)] = sv.n
                pend_fin = fin
            if pend_fin is not None:
                pend_fin()
                pend_fin = None
            c.wait("sync", sv, sv.n)
            c.dma("sync", sq, sc_xn[ch * 128:(ch + 1) * 128, g * 1024:(g + 1) * 1024],
                  xnb[ch % 2][:])
            if ostg is not None:
                c.dma("sync", sq, sc_O[ch * 128:(ch + 1) * 128, g * 1024:(g + 1) * 1024],
                      ostg[ch % 2][:])
            xn_dma[g * NCH + ch] = sq.n
    c.phase_close()
    c.barrier(bar, DMAS)


    mstack.close()

    y1f = [c.sb(f"y1f{i}", [128, TS], BF16, phase=False) for i in range(32)]
    lstat = c.sb("lstat", [128, 8 * 8], F32, phase=False)
    lst6 = c.sb("lst6", [128, 8 * 48], F32, phase=False)
    c.phase_open()
    gxn = [c.sb(f"gxn{i}", [128, TS], BF16) for i in range(32)]
    gt = [c.sb(f"gt{i}", [128, TS], BF16) for i in range(2)]
    xf = [c.sb(f"xf{i}", [128, TS], BF16) for i in range(2)]
    woslab = [c.sb(f"woslab{i}", [128, 16 * 512], BF16) for i in range(2)]
    xres = [c.sb(f"xres{i}", [128, 512], F32) for i in range(2)]
    ypst = [c.sb(f"ypst{i}", [128, 512], F32) for i in range(2)]
    pm = [c.ps(f"pDm{i}", [128, 512]) for i in range(2)]

    gxn_mark = {}
    for i in range(32):
        if i >= 2:
            c.wait("sync", sv, gxn_mark[i - 2])
        c.dma("sync", sq, xf[i % 2][:], sc_xn[0:TS, i * 128:(i + 1) * 128], transpose=True)
        c.dma("sync", sq, gt[i % 2][:], sc_g[i * 128:(i + 1) * 128, :])
        n_l = sq.n
        c.wait("vector", sq, n_l)
        c.op("vector", sv, lambda e, i=i: e.tensor_scalar(
            xf[i % 2][:], xf[i % 2][:], gnwc[:, i:i + 1], gnbc[:, i:i + 1], AL.mult, AL.add))
        c.op("vector", sv, lambda e, i=i: e.tensor_mul(gxn[i][:], xf[i % 2][:], gt[i % 2][:]))
        gxn_mark[i] = sv.n
    n_gxn = sv.n

    def slab_ap2(wt, oh, half):
        return wt[oh].rearrange("(hf it q) oc -> hf q it oc", hf=2, q=128)[half]

    wo_mark = {}
    wo_amark = {}
    ev2 = {}
    st2 = {}
    for oh in range(8):
        if oh >= 1:
            c.wait("gpsimd", stn, wo_amark[oh - 1])
        c.dma("gpsimd", gq, woslab[0][:].rearrange("q (it oc) -> q it oc", oc=512),
              slab_ap2(wo, oh, 0))
        n_slA = gq.n
        if oh >= 1:
            c.wait("gpsimd", stn, wo_mark[oh - 1])
        c.dma("gpsimd", gq, woslab[1][:].rearrange("q (it oc) -> q it oc", oc=512),
              slab_ap2(wo, oh, 1))
        n_slB = gq.n
        for tt in range(8):
            un = oh * 8 + tt
            if tt == 0:
                c.wait("tensor", gq, n_slA)
            if un == 0:
                c.wait("tensor", sv, n_gxn)
            if un >= 2:
                c.wait("tensor", sv, ev2[un - 2])

            def mmwoA(e, oh=oh, tt=tt):
                last = None
                for i in range(16):
                    last = e.matmul(pm[tt % 2][:], gxn[i][:, tt * 128:(tt + 1) * 128],
                                    woslab[0][:, i * 512:(i + 1) * 512],
                                    start=(i == 0), stop=False)
                return last
            c.op("tensor", stn, mmwoA)
            if tt == 7:
                wo_amark[oh] = stn.n
            if tt == 0:
                c.wait("tensor", gq, n_slB)

            def mmwoB(e, oh=oh, tt=tt):
                last = None
                for i in range(16):
                    last = e.matmul(pm[tt % 2][:], gxn[16 + i][:, tt * 128:(tt + 1) * 128],
                                    woslab[1][:, i * 512:(i + 1) * 512],
                                    start=False, stop=(i == 15))
                return last
            c.op("tensor", stn, mmwoB)
            if tt == 7:
                wo_mark[oh] = stn.n
            n_mm = stn.n
            c.wait("gpsimd", sv, ev2.get(un - 2, 0))
            c.dma("gpsimd", gq, xres[tt % 2][:],
                  x_tm[tt * 128:(tt + 1) * 128, oh * 512:(oh + 1) * 512])
            n_xr = gq.n
            c.wait("vector", stn, n_mm)
            c.wait("vector", gq, n_xr)
            if un >= 2:
                c.wait("vector", sq, st2[un - 2])
            c.op("vector", sv, lambda e, tt=tt: e.tensor_add(
                ypst[tt % 2][:], pm[tt % 2][:], xres[tt % 2][:]))
            ev2[un] = sv.n
            c.wait("sync", sv, ev2[un])
            c.dma("sync", sq, sc_yp[tt * 128:(tt + 1) * 128, oh * 512:(oh + 1) * 512],
                  ypst[tt % 2][:])
            st2[un] = sq.n
            c.op("vector", sv, lambda e, tt=tt, oh=oh: e.bn_stats(
                lst6[:, tt * 48 + oh * 6:tt * 48 + oh * 6 + 6], ypst[tt % 2][:]))
    n_yp_all = sq.n
    c.phase_close()
    c.barrier(bar, DMAS)
    c.phase_open()
    yp_t = [c.sb(f"yp_t{i}", [128, DM], F32) for i in range(2)]
    y1tm = [c.sb(f"y1tm{i}", [128, DM], BF16) for i in range(2)]
    pTd = []
    pTd.append(c.ps("pDtr0", [128, 128], BF16))
    pTd.append(c.ps("pDtr1", [128, 128], BF16))

    def ln_finalize(L, S6):
        c.emit("vector", lambda e: e.drain())
        for tt in range(8):
            c.op("vector", sv, lambda e, tt=tt, L=L, S6=S6: e.bn_aggr(
                L[:, tt * 8:tt * 8 + 2], S6[:, tt * 48:(tt + 1) * 48]))
        c.wait("scalar", sv)
        for tt in range(8):
            c.op("scalar", ss, lambda e, tt=tt, L=L: e.activation(
                L[:, tt * 8 + 3:tt * 8 + 4], L[:, tt * 8 + 1:tt * 8 + 2], AF.Sqrt, bias=EPS))
        c.wait("vector", ss)
        for tt in range(8):
            c.op("vector", sv, lambda e, tt=tt, L=L: e.reciprocal(
                L[:, tt * 8 + 4:tt * 8 + 5], L[:, tt * 8 + 3:tt * 8 + 4]))
        c.wait("scalar", sv)
        for tt in range(8):
            c.op("scalar", ss, lambda e, tt=tt, L=L: e.mul(
                L[:, tt * 8 + 6:tt * 8 + 7], L[:, tt * 8 + 4:tt * 8 + 5], -1.0))
        c.wait("vector", ss)
        for tt in range(8):
            c.op("vector", sv, lambda e, tt=tt, L=L: e.tensor_mul(
                L[:, tt * 8 + 5:tt * 8 + 6], L[:, tt * 8:tt * 8 + 1],
                L[:, tt * 8 + 6:tt * 8 + 7]))

    ln_finalize(lstat, lst6)
    n_ln1 = sv.n

    c.wait("sync", sq, n_yp_all)
    ap_mark = {}
    tr_cp = {}
    ap_s = {}
    for tt in range(8):
        if tt >= 2:
            c.wait("sync", ss, ap_s[tt - 2])
        c.dma("sync", sq, yp_t[tt % 2][:], sc_yp[tt * 128:(tt + 1) * 128, :])
        n_l = sq.n
        c.wait("scalar", sq, n_l)
        if tt == 0:
            c.wait("scalar", sv, n_ln1)
        if tt >= 2:
            c.wait("scalar", stn, ap_mark[tt - 2])
        c.op("scalar", ss, lambda e, tt=tt: e.activation(
            y1tm[tt % 2][:], yp_t[tt % 2][:], AF.Identity,
            bias=lstat[:, tt * 8 + 5:tt * 8 + 6], scale=lstat[:, tt * 8 + 4:tt * 8 + 5]))
        n_ap = ss.n
        ap_s[tt] = n_ap
        c.wait("tensor", ss, n_ap)
        for ft in range(32):
            fg = tt * 32 + ft
            if fg >= 2:
                c.wait("tensor", sv, tr_cp[fg - 2])
            c.op("tensor", stn, lambda e, tt=tt, ft=ft, fg=fg: e.transpose(
                pTd[fg % 2][:], y1tm[tt % 2][:, ft * 128:(ft + 1) * 128], id_s[:]))
            c.wait("vector", stn, stn.n)
            c.op("vector", sv, lambda e, tt=tt, ft=ft, fg=fg: e.tensor_copy(
                y1f[ft][:, tt * 128:(tt + 1) * 128], pTd[fg % 2][:]))
            tr_cp[fg] = sv.n
        ap_mark[tt] = stn.n
    c.phase_close()
    c.barrier(bar, DMAS)

    l2stat = c.sb("l2stat", [128, 8 * 8], F32, phase=False)
    l2st6 = c.sb("l2st6", [128, 8 * 48], F32, phase=False)
    c.phase_open()
    wfslab = [c.sb(f"wfslab{i}", [128, 32 * 512], BF16) for i in range(2)]
    zpst = [c.sb(f"zpst{i}", [128, 512], F32) for i in range(2)]
    y1rs = [c.sb(f"y1rs{i}", [128, 512], F32) for i in range(2)]
    ypr = [c.sb(f"ypr{i}", [128, 512], F32) for i in range(2)]
    fbs = c.sb("fbs", [128, DM], F32)
    pm2 = [c.ps(f"pEm{i}", [128, 512]) for i in range(2)]

    c.dma("sync", sq, fbs[:], fbr)
    n_fb = sq.n
    wf_mark = {}
    ev3 = {}
    st3 = {}
    y1_mark = {}
    for oh in range(8):
        if oh >= 2:
            c.wait("gpsimd", stn, wf_mark[oh - 2])
        for hf in range(2):
            c.dma("gpsimd", gq, wfslab[oh % 2][:, hf * 8192:(hf + 1) * 8192].rearrange(
                "q (it oc) -> q it oc", oc=512), slab_ap2(wf, oh, hf))
        n_sl = gq.n
        c.wait("scalar", gq, n_sl)
        if oh >= 2:
            c.wait("scalar", stn, wf_mark[oh - 2])
        for it in range(32):
            c.op("scalar", ss, lambda e, oh=oh, it=it: e.sign(
                wfslab[oh % 2][:, it * 512:(it + 1) * 512],
                wfslab[oh % 2][:, it * 512:(it + 1) * 512]))
        n_sign = ss.n
        for tt in range(8):
            un = oh * 8 + tt
            c.wait("tensor", ss, n_sign)
            if un >= 2:
                c.wait("tensor", sv, ev3[un - 2])

            def mmwf(e, oh=oh, tt=tt):
                last = None
                sl = wfslab[oh % 2]
                for i in range(32):
                    last = e.matmul(pm2[tt % 2][:], y1f[i][:, tt * 128:(tt + 1) * 128],
                                    sl[:, i * 512:(i + 1) * 512],
                                    start=(i == 0), stop=(i == 31))
                return last
            c.op("tensor", stn, mmwf)
            if tt == 7:
                wf_mark[oh] = stn.n
            n_mm = stn.n
            if un >= 2:
                c.wait("gpsimd", ss, y1_mark[un - 2])
            c.dma("gpsimd", gq, ypr[tt % 2][:],
                  sc_yp[tt * 128:(tt + 1) * 128, oh * 512:(oh + 1) * 512])
            n_ypr = gq.n
            c.wait("scalar", gq, n_ypr)
            if un >= 2:
                c.wait("scalar", sv, ev3[un - 2])
            c.op("scalar", ss, lambda e, tt=tt: e.activation(
                y1rs[tt % 2][:], ypr[tt % 2][:], AF.Identity,
                bias=lstat[:, tt * 8 + 5:tt * 8 + 6], scale=lstat[:, tt * 8 + 4:tt * 8 + 5]))
            y1_mark[un] = ss.n
            c.wait("vector", stn, n_mm)
            if un == 0:
                c.wait("vector", sq, n_fb)
            c.wait("vector", ss, y1_mark[un])
            if un >= 2:
                c.wait("vector", sq, st3[un - 2])
            c.op("vector", sv, lambda e, tt=tt, oh=oh: e.tensor_add(
                zpst[tt % 2][:], pm2[tt % 2][:], fbs[:, oh * 512:(oh + 1) * 512]))
            c.op("vector", sv, lambda e, tt=tt: e.tensor_add(
                zpst[tt % 2][:], zpst[tt % 2][:], y1rs[tt % 2][:]))
            ev3[un] = sv.n
            c.wait("sync", sv, ev3[un])
            c.dma("sync", sq, sc_zp[tt * 128:(tt + 1) * 128, oh * 512:(oh + 1) * 512],
                  zpst[tt % 2][:])
            st3[un] = sq.n
            c.op("vector", sv, lambda e, tt=tt, oh=oh: e.bn_stats(
                l2st6[:, tt * 48 + oh * 6:tt * 48 + oh * 6 + 6], zpst[tt % 2][:]))
    n_zp_all = sq.n
    c.phase_close()
    c.barrier(bar, DMAS)
    c.phase_open()
    zp_t = [c.sb(f"zp_t2{i}", [128, DM], F32) for i in range(2)]
    zt = [c.sb(f"zt{i}", [128, DM], F32) for i in range(2)]

    ln_finalize(l2stat, l2st6)
    n_ln2 = sv.n

    c.wait("sync", sq, n_zp_all)
    zst = {}
    zap = {}
    for tt in range(8):
        if tt >= 2:
            c.wait("sync", ss, zap[tt - 2])
        c.dma("sync", sq, zp_t[tt % 2][:], sc_zp[tt * 128:(tt + 1) * 128, :])
        n_l = sq.n
        c.wait("scalar", sq, n_l)
        if tt == 0:
            c.wait("scalar", sv, n_ln2)
        if tt >= 2:
            c.wait("scalar", sq, zst[tt - 2])
        c.op("scalar", ss, lambda e, tt=tt: e.activation(
            zt[tt % 2][:], zp_t[tt % 2][:], AF.Identity,
            bias=l2stat[:, tt * 8 + 5:tt * 8 + 6], scale=l2stat[:, tt * 8 + 4:tt * 8 + 5]))
        zap[tt] = ss.n
        c.wait("sync", ss, ss.n)
        c.dma("sync", sq, out[tt * 128:(tt + 1) * 128, :], zt[tt % 2][:])
        zst[tt] = sq.n
    c.phase_close()

    if taps:
        c.wait("gpsimd", sq, sq.n)
        c.wait("gpsimd", ss, ss.n)
        c.wait("gpsimd", sv, sv.n)
        for name, (tap, src) in taps.items():
            c.dma("gpsimd", gq, tap, src)
    c.wait("sync", sq, sq.n)
    c.wait("gpsimd", gq, gq.n)

    with nc.Block() as block:
        c.run(block)
    c.gstack.close()
    return nc


_NC_CACHE = {}


def _get_nc(key=()):
    if key not in _NC_CACHE:
        _NC_CACHE[key] = build_program(debug_taps=key)
    return _NC_CACHE[key]


def _cols32(a):
    return np.ascontiguousarray(np.asarray(a, np.float32).reshape(32, 128).T)


def make_inputs(inputs):
    x = np.asarray(inputs["L0_attn_hidden"], np.float32).reshape(S, DM)
    xT_full = np.ascontiguousarray(x.T)
    xprev = np.concatenate([np.zeros((DM, 1), np.float32), xT_full[:, :-1]], axis=1)

    def wblocks(w, bs):
        wt = np.ascontiguousarray(np.asarray(w, np.float32).T)
        return np.ascontiguousarray(wt.reshape(DM, DM // bs, bs).transpose(1, 0, 2))

    wkb = wblocks(inputs["L0_attn_w_key"], 128)
    wvb = wblocks(inputs["L0_attn_w_value"], 128)
    wrb = wblocks(inputs["L0_attn_w_receptance"], 128)
    wgb = wblocks(inputs["L0_attn_w_gate"], 128)
    wob = wblocks(inputs["L0_attn_w_output"], 512)
    wfb = wblocks(inputs["L0_ffn_weight"], 512)
    mcols = np.ascontiguousarray(np.concatenate(
        [_cols32(inputs["L0_attn_time_mix_key"]),
         _cols32(inputs["L0_attn_time_mix_value"]),
         _cols32(inputs["L0_attn_time_mix_receptance"]),
         _cols32(inputs["L0_attn_time_mix_gate"])], axis=1))
    tdp = _cols32(inputs["L0_attn_time_decay"])
    tfp = _cols32(inputs["L0_attn_time_faaaa"])
    gnw = _cols32(inputs["L0_attn_gn_weight"])
    gnb = _cols32(inputs["L0_attn_gn_bias"])
    fbr = np.ascontiguousarray(np.broadcast_to(
        np.asarray(inputs["L0_ffn_bias"], np.float32)[None, :], (128, DM)))
    iota, masks, ident = _host_consts()
    in_maps = []
    for core in range(NCORE):
        t0 = core * TS
        xTs = np.ascontiguousarray(
            np.concatenate([xprev[:, t0:t0 + 1], xT_full[:, t0:t0 + TS]], axis=1))
        x_tms = np.ascontiguousarray(x[t0:t0 + TS])
        gap = np.array([1024.0 * (core - 1 - j) if j < core else 1e9 for j in range(8)],
                      np.float32)
        gapx = np.ascontiguousarray(np.broadcast_to(gap[None, :], (128, 8)))
        in_maps.append({
            "xT": xTs, "x_tm": x_tms, "wkT_b": wkb, "wvT_b": wvb, "wrT_b": wrb,
            "wgT_b": wgb, "woT_b": wob, "wfT_b": wfb, "mcols": mcols, "tdp": tdp,
            "tfp": tfp, "gnw": gnw, "gnb": gnb, "fbias_rep": fbr, "iota_c": iota,
            "mask_c": masks, "ident_c": ident, "gapix": gapx,
        })
    return in_maps


def kernel(**inputs):
    nc = _get_nc()
    in_maps = make_inputs(inputs)
    res = run_bass_kernel_spmd(nc, in_maps, core_ids=list(range(NCORE)))
    shards = [np.asarray(res.results[i]["out_sh"]) for i in range(NCORE)]
    return np.concatenate(shards, axis=0).reshape(1, S, DM)


# version 07
# speedup vs baseline: 1.0645x; 1.0208x over previous
import contextlib
import numpy as np
import ml_dtypes
import concourse.bass as bass
import concourse.mybir as mybir
from concourse.bass_utils import run_bass_kernel_spmd
from concourse.replica_groups import maybe_share_collective_output_space

F32 = mybir.dt.float32
BF16 = mybir.dt.bfloat16
U8 = mybir.dt.uint8
AF = mybir.ActivationFunctionType
AL = mybir.AluOpType
AX = mybir.AxisListType
BF = ml_dtypes.bfloat16

NCORE = 8
S = 8192
TS = S // NCORE
DM = 4096
NP = 32
C = 128
NCH = TS // C
NG = 4
GP = NP // NG
EPS = 1e-5
NVAR = 12


class Sem:
    def __init__(self, h):
        self.h = h
        self.n = 0


class Ctx:
    ENGS = ("sync", "gpsimd", "tensor", "vector", "scalar")

    def __init__(self, nc):
        self.nc = nc
        self.st = {e: [] for e in self.ENGS}
        self.gstack = contextlib.ExitStack()
        self.pstack = None

    def sem(self, name):
        return Sem(self.gstack.enter_context(self.nc.semaphore(name)))

    def sb(self, name, shape, dt, phase=True):
        stk = self.pstack if (phase and self.pstack is not None) else self.gstack
        return stk.enter_context(self.nc.sbuf_tensor(name, shape, dt))

    def ps(self, name, shape, dt=F32, phase=True):
        stk = self.pstack if (phase and self.pstack is not None) else self.gstack
        return stk.enter_context(self.nc.psum_tensor(name, shape, dt))

    def phase_open(self):
        assert self.pstack is None
        self.pstack = contextlib.ExitStack()

    def phase_close(self):
        self.pstack.close()
        self.pstack = None

    def emit(self, eng, fn):
        self.st[eng].append(fn)

    def op(self, eng, sem, fn):
        def run(e, fn=fn, h=sem.h):
            fn(e).then_inc(h, 1)
        self.st[eng].append(run)
        sem.n += 1
        return sem.n

    def dma(self, q, sem, o, i, transpose=False):
        def run(e, o=o, i=i, h=sem.h, tr=transpose):
            e.dma_start(o, i, transpose=True).then_inc(h, 16) if tr else \
                e.dma_start(o, i).then_inc(h, 16)
        self.st[q].append(run)
        sem.n += 16
        return sem.n

    def wait(self, eng, sem, val=None):
        v = sem.n if val is None else val
        if v > 0:
            self.emit(eng, lambda e, h=sem.h, v=v: e.wait_ge(h, v))

    def barrier(self, bsem, dma_sems):
        for q, s in dma_sems.items():
            self.wait(q, s, s.n)
        for eng in self.ENGS:
            self.op(eng, bsem, lambda e: e.nop())
        for eng in self.ENGS:
            self.wait(eng, bsem, bsem.n)

    def run(self, block):
        def mk(lst):
            def body(eng):
                for f in lst:
                    f(eng)
            return body
        block.sync(mk(self.st["sync"]))
        block.gpsimd(mk(self.st["gpsimd"]))
        block.tensor(mk(self.st["tensor"]))
        block.vector(mk(self.st["vector"]))
        block.scalar(mk(self.st["scalar"]))


def _host_consts():
    j = np.arange(C)
    pats = [3 - (j % 8), 7 - (j % 8), 15 - (j % 16), 31 - (j % 32), 63 - (j % 64),
            (j % 8) - 4, j % 8, j % 16, j % 32, j % 64, j, 127 - j]
    iota = np.concatenate([p.astype(np.float32) for p in pats])[None, :]
    iota = np.ascontiguousarray(np.broadcast_to(iota, (128, NVAR * C)))
    ii = j[None, :]
    jj = j[:, None]
    lower = ii > jj

    def adj(L):
        return lower & ((ii // L) == (jj // L) + 1) & ((ii // (2 * L)) == (jj // (2 * L)))
    masks = [lower & ((ii // 8) == (jj // 8)), adj(8), adj(16), adj(32), adj(64), ii == jj]
    masks = np.ascontiguousarray(np.concatenate([m.astype(np.uint8) for m in masks], axis=1))
    ident = np.ascontiguousarray(np.eye(128, dtype=np.float32)).astype(BF)
    return iota, masks, ident


def build_program(debug_taps=()):
    nc = bass.Bass("TRN2", target_bir_lowering=False)
    c = Ctx(nc)
    DT = nc.dram_tensor
    IN = lambda n, sh, dt=F32: DT(n, sh, dt, kind="ExternalInput").ap()
    xT = IN("xT", [DM, TS + 1])
    x_tm = IN("x_tm", [TS, DM])
    wk = IN("wkT_b", [32, DM, 128])
    wv = IN("wvT_b", [32, DM, 128])
    wr = IN("wrT_b", [32, DM, 128])
    wg = IN("wgT_b", [32, DM, 128])
    wo = IN("woT_b", [8, DM, 512])
    wf = IN("wfT_b", [8, DM, 512])
    mcols = IN("mcols", [128, 4 * 32])
    tdp = IN("tdp", [128, 32])
    tfp = IN("tfp", [128, 32])
    gnw = IN("gnw", [128, 32])
    gnb = IN("gnb", [128, 32])
    fbr = IN("fbias_rep", [128, DM])
    iota_c = IN("iota_c", [128, NVAR * C])
    mask_c = IN("mask_c", [128, 6 * C], U8)
    id_c = IN("ident_c", [128, 128], BF16)
    gapix = IN("gapix", [128, 8])
    out = DT("out_sh", [TS, DM], F32, kind="ExternalOutput").ap()

    sc_k = DT("sc_k", [DM, TS], BF16, kind="Internal").ap()
    sc_v = DT("sc_v", [DM, TS], BF16, kind="Internal").ap()
    sc_r = DT("sc_r", [DM, TS], BF16, kind="Internal").ap()
    sc_g = DT("sc_g", [DM, TS], BF16, kind="Internal").ap()
    sc_xn = DT("sc_xn", [TS, DM], BF16, kind="Internal").ap()
    sc_O = DT("sc_O", [TS, DM], F32, kind="Internal").ap()
    sc_yp = DT("sc_yp", [TS, DM], F32, kind="Internal").ap()
    sc_zp = DT("sc_zp", [TS, DM], F32, kind="Internal").ap()
    cc_in = DT("cc_in", [NP, 128, 128], F32, kind="Internal").ap()
    aspace = maybe_share_collective_output_space("AllGather", [list(range(NCORE))])
    cc_out = DT("cc_out", [NCORE, NP, 128, 128], F32, kind="Internal", addr_space=aspace).ap()

    tap_specs = {"k": (sc_k, BF16), "v": (sc_v, BF16), "r": (sc_r, BF16), "g": (sc_g, BF16),
                 "xn": (sc_xn, BF16), "yp": (sc_yp, F32), "zp": (sc_zp, F32),
                 "B": (cc_in, F32), "Bg": (cc_out, F32), "O": (sc_O, F32)}
    taps = {}
    for name in debug_taps:
        src, dt = tap_specs[name]
        taps[name] = (DT("tap_" + name, list(src.shape), dt, kind="ExternalOutput").ap(), src)

    sq = c.sem("sq")
    gq = c.sem("gq")
    sv = c.sem("sv")
    ss = c.sem("ss")
    stn = c.sem("stn")
    scc = c.sem("scc")
    bar = c.sem("bar")
    DMAS = {"sync": sq, "gpsimd": gq}

    iota_s = c.sb("iota_s", [128, NVAR * C], F32, phase=False)
    mask_s = c.sb("mask_s", [128, 6 * C], U8, phase=False)
    id_s = c.sb("id_s", [128, 128], BF16, phase=False)
    wcol = c.sb("wcol", [128, 32], F32, phase=False)
    tfcol = c.sb("tfcol", [128, 32], F32, phase=False)
    mc = c.sb("mc", [128, 4 * 32], F32, phase=False)
    gnwc = c.sb("gnwc", [128, 32], F32, phase=False)
    gnbc = c.sb("gnbc", [128, 32], F32, phase=False)
    gapix_s = c.sb("gapix_s", [128, 8], F32, phase=False)
    epsc = c.sb("epsc", [128, 1], F32, phase=False)
    negc = c.sb("negc", [128, 1], F32, phase=False)
    c.op("vector", sv, lambda e: e.memset(epsc[:], EPS))
    c.op("vector", sv, lambda e: e.memset(negc[:], -1.0))
    nc.const_aps.register(F32, EPS, epsc[:, 0:1])
    nc.const_aps.register(F32, -1.0, negc[:, 0:1])

    for dst, src in ((iota_s, iota_c), (mask_s, mask_c), (id_s, id_c), (wcol, tdp),
                     (tfcol, tfp), (mc, mcols), (gnwc, gnw), (gnbc, gnb), (gapix_s, gapix)):
        c.dma("sync", sq, dst[:], src)
    c.wait("scalar", sq, sq.n)
    c.op("scalar", ss, lambda e: e.activation(wcol[:], wcol[:], AF.Exp))
    c.wait("vector", ss)
    c.op("vector", sv, lambda e: e.tensor_scalar_mul(wcol[:], wcol[:], -1.0))
    n_wcol = sv.n
    c.wait("tensor", sv, n_wcol)
    c.wait("scalar", sv, n_wcol)

    mstack = contextlib.ExitStack()
    MSB = lambda n, sh, dt: mstack.enter_context(nc.sbuf_tensor(n, sh, dt))
    ktm = [MSB(f"ktm{i}", [128, TS], BF16) for i in range(NCH)]
    vtm = [MSB(f"vtm{i}", [128, TS], BF16) for i in range(NCH)]
    eupdT_all = MSB("eupdT_all", [128, GP * 128], BF16)
    e128 = MSB("e128", [128, GP], F32)
    Sst = [MSB(f"Sst{i}", [128, 128], F32) for i in range(GP)]
    ktmpA = [MSB(f"ktmpA{i}", [128, GP * 128], BF16) for i in range(2)]
    expn = MSB("expn", [128, NVAR * C], F32)
    eupd_f = MSB("eupd_f", [128, 128], BF16)

    c.phase_open()
    x_sb = [c.sb(f"x_sb{i}", [128, TS + 1], BF16) for i in range(32)]
    mix = [c.sb(f"mix{i}", [128, TS], BF16) for i in range(32)]
    tmpm = [c.sb("tmpm0", [128, TS], BF16)]
    wslab = [c.sb(f"wslab{i}", [128, DM], BF16) for i in range(2)]
    pstage = [c.sb(f"pstage{i}", [128, TS], BF16) for i in range(2)]
    pkp = [c.ps(f"pk{i}", [128, 512]) for i in range(4)]
    pU = []
    pU.append(c.ps("p1U0", [128, 64]))
    c.ps("p1pad0", [128, 448])
    pU.append(c.ps("p1U1", [128, 64]))
    c.ps("p1pad1", [128, 448])
    pTr = pkp[3]

    for i in range(32):
        c.dma("gpsimd", gq, x_sb[i][:], xT[i * 128:(i + 1) * 128, :])
    n_xload = gq.n

    def slab_ap(wt, o):
        return wt[o].rearrange("(it q) oc -> q it oc", q=128)

    projs = [(wk, 0, sc_k, None), (wv, 1, sc_v, None), (wr, 2, sc_r, None), (wg, 3, sc_g, AF.Silu)]
    mm_mark = {}
    ev_mark = {}
    st_mark = {}
    def emit_proj(pj):
        wt, mi, scr, act = projs[pj]
        c.wait("vector", gq, n_xload)
        if pj > 0:
            c.wait("vector", stn, mm_mark[(pj - 1, 31)])
        for i in range(32):
            t = tmpm[0]
            cur = x_sb[i][:, 1:TS + 1]
            prv = x_sb[i][:, 0:TS]
            c.op("vector", sv, lambda e, t=t, cur=cur, prv=prv: e.tensor_sub(t[:], cur, prv))
            c.op("vector", sv, lambda e, t=t, i=i, mi=mi, prv=prv:
                 e.scalar_tensor_tensor(mix[i][:], t[:], mc[:, mi * 32 + i:mi * 32 + i + 1],
                                        prv, AL.mult, AL.add))
        n_mixv = sv.n
        for o in range(32):
            gk = (pj, o)
            if o >= 2:
                c.wait("gpsimd", stn, mm_mark[(pj, o - 2)])
            elif pj > 0:
                c.wait("gpsimd", stn, mm_mark[(pj - 1, 30 + o)])
            c.dma("gpsimd", gq, wslab[o % 2][:].rearrange("q (it oc) -> q it oc", oc=128),
                  slab_ap(wt, o))
            n_sl = gq.n
            c.wait("tensor", gq, n_sl)
            if o == 0:
                c.wait("tensor", sv, n_mixv)
            pk2 = (pj, o - 2) if o >= 2 else ((pj - 1, 30 + o) if pj > 0 else None)
            if pk2 is not None:
                sE, nE = ev_mark[pk2]
                c.wait("tensor", sE, nE)
            pA, pB = pkp[(o % 2) * 2], pkp[(o % 2) * 2 + 1]

            def mms(e, o=o, pA=pA, pB=pB):
                w2 = wslab[o % 2]
                last = None
                for i in range(32):
                    lhs = w2[:, i * 128:(i + 1) * 128]
                    e.matmul(pA[:], lhs, mix[i][:, 0:512], start=(i == 0), stop=(i == 31))
                    last = e.matmul(pB[:], lhs, mix[i][:, 512:1024], start=(i == 0), stop=(i == 31))
                return last
            c.op("tensor", stn, mms)
            mm_mark[gk] = stn.n
            eng = "vector" if act is None else "scalar"
            esem = sv if act is None else ss
            c.wait(eng, stn, mm_mark[gk])
            if pk2 is not None and pk2 in st_mark:
                c.wait(eng, sq, st_mark[pk2])
            stg = pstage[o % 2]
            if act is None:
                c.op("vector", sv, lambda e, stg=stg, pA=pA: e.tensor_copy(stg[:, 0:512], pA[:]))
                c.op("vector", sv, lambda e, stg=stg, pB=pB: e.tensor_copy(stg[:, 512:1024], pB[:]))
            else:
                c.op("scalar", ss, lambda e, stg=stg, pA=pA, act=act: e.activation(stg[:, 0:512], pA[:], act))
                c.op("scalar", ss, lambda e, stg=stg, pB=pB, act=act: e.activation(stg[:, 512:1024], pB[:], act))
            ev_mark[gk] = (esem, esem.n)
            c.wait("sync", esem, esem.n)
            c.dma("sync", sq, scr[o * 128:(o + 1) * 128, :], stg[:])
            st_mark[gk] = sq.n

    emit_proj(0)
    emit_proj(1)
    emit_proj(2)

    def build_pair_patterns(pi, p, full):
        c.wait("vector", ss)
        c.wait("scalar", stn)
        if full:
            c.op("vector", sv, lambda e, p=p: e.tensor_scalar(
                expn[:], iota_s[:], wcol[:, p:p + 1], None, AL.mult))
            c.wait("scalar", sv)
            c.op("scalar", ss, lambda e, pi=pi: e.activation(pat2[pi][:], expn[:], AF.Exp))
            c.wait("tensor", ss)
            c.wait("tensor", sv)
            c.op("tensor", stn, lambda e, pi=pi: e.transpose(
                trg_ap(), pat2[pi][:, 11 * C:12 * C], id_s[:]))
        else:
            c.op("vector", sv, lambda e, p=p: e.tensor_scalar(
                expn[:, 0:C], iota_s[:, 11 * C:12 * C], wcol[:, p:p + 1], None, AL.mult))
            c.wait("scalar", sv)
            c.op("scalar", ss, lambda e: e.activation(eupd_f[:], expn[:, 0:C], AF.Exp))
            c.wait("tensor", ss)
            c.wait("tensor", sv)
            c.op("tensor", stn, lambda e: e.transpose(trg_ap(), eupd_f[:], id_s[:]))
        c.wait("vector", stn)
        c.op("vector", sv, lambda e, pi=pi: e.tensor_copy(
            eupdT_all[:, pi * 128:(pi + 1) * 128], trg_ap()))
        c.op("vector", sv, lambda e, p=p, pi=pi: e.tensor_scalar_mul(
            e128[:, pi:pi + 1], wcol[:, p:p + 1], 128.0))
        c.wait("scalar", sv)
        c.op("scalar", ss, lambda e, pi=pi: e.activation(e128[:, pi:pi + 1], e128[:, pi:pi + 1], AF.Exp))

    trg = [pTr]

    def trg_ap():
        t = trg[0]
        ap = t[:]
        if ap.dtype != BF16:
            ap = ap.bitcast(BF16)[:, 0:128]
        return ap
    pat2 = None
    upd_mark = {}
    stt_mark = {}
    for g in range(NG):
        c.wait("sync", stn, stn.n)
        c.wait("sync", sv, sv.n)
        c.wait("vector", sq, sq.n)
        for tt in range(NCH):
            c.dma("sync", sq, ktm[tt][:], sc_k[g * 1024:(g + 1) * 1024, tt * 128:(tt + 1) * 128],
                  transpose=True)
            c.dma("sync", sq, vtm[tt][:], sc_v[g * 1024:(g + 1) * 1024, tt * 128:(tt + 1) * 128],
                  transpose=True)
        n_ld = sq.n
        for pi in range(GP):
            build_pair_patterns(pi, g * GP + pi, full=False)
        c.wait("vector", ss)
        for pi in range(GP):
            c.op("vector", sv, lambda e, pi=pi: e.memset(Sst[pi][:], 0.0))
        c.wait("vector", sq, n_ld)
        c.wait("tensor", sq, n_ld)
        c.wait("tensor", ss)
        ch_upd = {}
        for ch in range(NCH):
            cpar = ch % 2
            if ch >= 2:
                c.wait("vector", stn, ch_upd[ch - 2])
            c.op("vector", sv, lambda e, ch=ch, cpar=cpar: e.tensor_mul(
                ktmpA[cpar][:], ktm[ch][:], eupdT_all[:]))
            n_kt = sv.n
            c.wait("tensor", sv, n_kt)
            for pi in range(GP):
                u = (g * NCH + ch) * GP + pi
                par = u % 2
                if u >= 2:
                    c.wait("tensor", sv, stt_mark[u - 2])

                def updmm(e, ch=ch, pi=pi, par=par, cpar=cpar):
                    e.matmul(pU[par][0:64, :], ktmpA[cpar][:, pi * 128:pi * 128 + 64],
                             vtm[ch][:, pi * 128:pi * 128 + 64], start=True, stop=True)
                    return e.matmul(pU[par][64:128, :],
                                    ktmpA[cpar][:, pi * 128 + 64:(pi + 1) * 128],
                                    vtm[ch][:, pi * 128 + 64:(pi + 1) * 128],
                                    start=True, stop=True, tile_position=(0, 64))
                c.op("tensor", stn, updmm)
                upd_mark[u] = stn.n
                c.wait("vector", stn, upd_mark[u])
                c.op("vector", sv, lambda e, pi=pi, par=par: e.scalar_tensor_tensor(
                    Sst[pi][0:64, 0:64], Sst[pi][0:64, 0:64], e128[0:64, pi:pi + 1],
                    pU[par][0:64, :], AL.mult, AL.add))
                c.op("vector", sv, lambda e, pi=pi, par=par: e.scalar_tensor_tensor(
                    Sst[pi][64:128, 64:128], Sst[pi][64:128, 64:128], e128[64:128, pi:pi + 1],
                    pU[par][64:128, :], AL.mult, AL.add))
                stt_mark[u] = sv.n
            ch_upd[ch] = stn.n
        c.wait("sync", sv, sv.n)
        for pi in range(GP):
            c.dma("sync", sq, cc_in[g * GP + pi], Sst[pi][:])
    c.wait("gpsimd", sq, sq.n)
    c.emit("gpsimd", lambda e: e.collective_compute(
        "AllGather", AL.bypass, replica_groups=[list(range(NCORE))],
        ins=[cc_in[:]], outs=[cc_out[:]]).then_inc(scc.h, 1))
    scc.n += 1

    emit_proj(3)
    c.phase_close()
    c.barrier(bar, DMAS)

    c.wait("sync", scc, scc.n)
    c.wait("gpsimd", scc, scc.n)
    c.wait("vector", scc, scc.n)
    c.wait("tensor", scc, scc.n)
    c.wait("scalar", scc, scc.n)

    c.phase_open()
    kf = [c.sb(f"kf{i}", [128, TS], BF16) for i in range(GP)]
    rf = [c.sb(f"rf{i}", [128, TS], BF16) for i in range(GP)]
    pat2 = [c.sb(f"pat{i}", [128, NVAR * C], BF16) for i in range(GP)]
    khat = [c.sb(f"khat{i}", [128, 5 * C], BF16) for i in range(2)]
    ktfb = [c.sb(f"ktf{i}", [128, C], BF16) for i in range(2)]
    rhat = [c.sb(f"rhat{i}", [128, 6 * C], BF16) for i in range(2)]
    A2f = [c.sb(f"A2f{i}", [128, 2 * C], F32) for i in range(2)]
    rstg = [c.sb(f"rstg{i}", [128, 96], BF16) for i in range(2)]
    A2b = [c.sb(f"A2b{i}", [128, 2 * C], BF16) for i in range(2)]
    Sbf = [c.sb(f"Sbf{i}", [128, 128], BF16) for i in range(GP)]
    xnb = [c.sb(f"xnb{i}", [128, TS], BF16) for i in range(2)]
    Bg = c.sb("Bg", [128, NCORE * 128], F32)
    egap = c.sb("egap", [128, 8], F32)
    b6 = [c.sb(f"b6_{i}", [128, 12], F32) for i in range(2)]
    stat = [c.sb(f"stat{i}", [128, 12], F32) for i in range(2)]
    zz = c.sb("zz", [128, C], F32)
    ostg = [c.sb(f"ostg{i}", [128, TS], F32) for i in range(2)] if "O" in taps else None
    P = [c.ps(f"p2P{i}", [128, 128]) for i in range(6)]
    pO2 = [c.ps(f"p2O{h}", [128, 64]) for h in range(2)]

    c.op("vector", sv, lambda e: e.memset(zz[:], 0.0))

    marks = {}
    xn_dma = {}
    build_mark = {}
    pend_fin = None
    for g in range(NG):
        c.wait("sync", stn, stn.n)
        c.wait("sync", sv, sv.n)
        c.wait("sync", ss, ss.n)
        c.wait("vector", stn, stn.n)
        c.wait("vector", sq, sq.n)
        for pi in range(GP):
            p = g * GP + pi
            c.dma("sync", sq, kf[pi][:], sc_k[p * 128:(p + 1) * 128, :])
            c.dma("sync", sq, rf[pi][:], sc_r[p * 128:(p + 1) * 128, :])
        for tt in range(NCH):
            c.dma("sync", sq, ktm[tt][:], sc_k[g * 1024:(g + 1) * 1024, tt * 128:(tt + 1) * 128],
                  transpose=True)
            c.dma("sync", sq, vtm[tt][:], sc_v[g * 1024:(g + 1) * 1024, tt * 128:(tt + 1) * 128],
                  transpose=True)
        n_ld = sq.n
        trg[0] = P[1]
        for pi in range(GP):
            p = g * GP + pi
            build_pair_patterns(pi, p, full=True)
            c.op("vector", sv, lambda e, p=p: e.tensor_scalar(
                egap[:], gapix_s[:], wcol[:, p:p + 1], None, AL.mult))
            c.wait("scalar", sv)
            c.op("scalar", ss, lambda e: e.activation(egap[:], egap[:], AF.Exp))
            c.wait("gpsimd", sv, sv.n)
            c.dma("gpsimd", gq, Bg[:].rearrange("q (j f) -> q j f", j=NCORE),
                  cc_out[:, p, :, :].rearrange("j q f -> q j f"))
            c.wait("vector", gq)
            c.wait("vector", ss)
            c.op("vector", sv, lambda e, pi=pi: e.memset(Sst[pi][:], 0.0))
            for jc in range(NCORE):
                c.op("vector", sv, lambda e, pi=pi, jc=jc: e.scalar_tensor_tensor(
                    Sst[pi][:], Bg[:, jc * 128:(jc + 1) * 128], egap[:, jc:jc + 1],
                    Sst[pi][:], AL.mult, AL.add))
            c.op("vector", sv, lambda e, pi=pi: e.tensor_copy(Sbf[pi][:], Sst[pi][:]))
        c.wait("vector", sq, n_ld)
        c.wait("tensor", sq, n_ld)
        c.wait("scalar", sq, n_ld)
        def emit_builds(u):
            lc = u - g * NCH * GP
            ch_, pi_ = lc // GP, lc % GP
            par_ = u % 2
            kfc = kf[pi_][:, ch_ * C:(ch_ + 1) * C]
            rfc = rf[pi_][:, ch_ * C:(ch_ + 1) * C]
            if u >= 2:
                c.wait("vector", stn, marks[("ph1", u - 2)])
                c.wait("scalar", stn, marks[("ph1", u - 2)])
            c.op("vector", sv, lambda e, par=par_, kfc=kfc, pi=pi_: e.tensor_tensor(
                khat[par][:].rearrange("q (v f) -> q v f", v=5),
                kfc.rearrange("q (v f) -> q v f", v=1).broadcast_to((128, 5, C)),
                pat2[pi][:, 0:5 * C].rearrange("q (v f) -> q v f", v=5), AL.mult))
            c.op("vector", sv, lambda e, par=par_, rfc=rfc, pi=pi_: e.tensor_tensor(
                rhat[par][:].rearrange("q (v f) -> q v f", v=6),
                rfc.rearrange("q (v f) -> q v f", v=1).broadcast_to((128, 6, C)),
                pat2[pi][:, 5 * C:11 * C].rearrange("q (v f) -> q v f", v=6), AL.mult))
            c.op("scalar", ss, lambda e, par=par_, kfc=kfc, p=g * GP + pi_: e.mul(
                ktfb[par][:], kfc, tfcol[:, p:p + 1]))
            build_mark[u] = (sv.n, ss.n)

        u_base = g * NCH * GP
        u_last = u_base + NCH * GP - 1
        emit_builds(u_base)
        for ch in range(NCH):
            cpar = ch % 2
            if ch >= 2:
                c.wait("vector", stn, marks[("AV", (g * NCH + ch - 2) * GP + GP - 1)])
            c.op("vector", sv, lambda e, ch=ch, cpar=cpar: e.tensor_mul(
                ktmpA[cpar][:], ktm[ch][:], eupdT_all[:]))
            for pi in range(GP):
                u = (g * NCH + ch) * GP + pi
                par = u % 2
                rf_c = rf[pi][:, ch * C:(ch + 1) * C]
                gc = g * NCH + ch
                c.wait("tensor", sv, build_mark[u][0])
                c.wait("tensor", ss, build_mark[u][1])
                if u >= 1:
                    c.wait("tensor", sv, marks[("sel1", u - 1)])
                    c.wait("tensor", ss, marks[("rect", u - 1)])
                    c.wait("tensor", sv, marks[("stt", u - 1)])

                def pieces(e, par=par, rf_c=rf_c, h=0):
                    hs = slice(64 * h, 64 * (h + 1))
                    tp = None if h == 0 else (64, 0)
                    for px in range(5):
                        e.matmul(P[px][:], khat[par][hs, px * C:(px + 1) * C],
                                 rhat[par][hs, px * C:(px + 1) * C], start=True, stop=True,
                                 tile_position=tp)
                    return e.matmul(P[5][:], ktfb[par][hs, :], rf_c[hs, :],
                                    start=True, stop=True, tile_position=tp)
                c.op("tensor", stn, lambda e, par=par, rf_c=rf_c: pieces(e, par, rf_c, 0))
                marks[("ph0", u)] = stn.n
                c.wait("scalar", stn, marks[("ph0", u)])

                def stage0(e, par=par):
                    e.copy(rstg[par][0:64, 0:64], P[4][0:64, 64:128])
                    e.copy(rstg[par][0:32, 64:96], P[3][0:32, 32:64])
                    return e.copy(rstg[par][64:96, 64:96], P[3][64:96, 96:128])
                c.op("scalar", ss, stage0)
                marks[("stg", u)] = ss.n
                c.wait("vector", stn, marks[("ph0", u)])

                def sel(e, par=par, h=0):
                    dst = A2f[par][:, h * C:(h + 1) * C]
                    e.select(dst, mask_s[:, 0:C], P[0][:], zz[:])
                    e.select(dst, mask_s[:, C:2 * C], P[1][:], dst)
                    e.select(dst, mask_s[:, 2 * C:3 * C], P[2][:], dst)
                    return e.select(dst, mask_s[:, 5 * C:6 * C], P[5][:], dst)
                c.op("vector", sv, lambda e, par=par: sel(e, par, 0))
                marks[("sel0", u)] = sv.n
                if u < u_last:
                    emit_builds(u + 1)
                c.wait("tensor", sv, marks[("sel0", u)])
                c.wait("tensor", ss, marks[("stg", u)])
                c.op("tensor", stn, lambda e, par=par, rf_c=rf_c: pieces(e, par, rf_c, 1))
                marks[("ph1", u)] = stn.n
                c.wait("vector", stn, marks[("ph1", u)])

                def sel1(e, par=par):
                    sel(e, par, 1)
                    return e.tensor_copy(A2b[par][:], A2f[par][:])
                c.op("vector", sv, sel1)
                marks[("sel1", u)] = sv.n
                c.wait("scalar", sv, marks[("sel1", u)])

                def rects(e, par=par):
                    e.copy(A2b[par][0:64, 64:128], rstg[par][0:64, 0:64])
                    e.copy(A2b[par][0:32, 32:64], rstg[par][0:32, 64:96])
                    e.copy(A2b[par][64:96, 96:128], rstg[par][64:96, 64:96])
                    e.copy(A2b[par][0:64, C + 64:C + 128], P[4][0:64, 64:128])
                    e.copy(A2b[par][0:32, C + 32:C + 64], P[3][0:32, 32:64])
                    return e.copy(A2b[par][64:96, C + 96:C + 128], P[3][64:96, 96:128])
                c.op("scalar", ss, rects)
                marks[("rect", u)] = ss.n
                if pend_fin is not None:
                    pend_fin()
                    pend_fin = None
                c.wait("tensor", sv, marks[("sel1", u)])
                c.wait("tensor", ss, marks[("rect", u)])
                if u >= 1:
                    c.wait("tensor", sv, marks[("gn", u - 1)])
                    if ostg is not None:
                        c.wait("tensor", sv, marks[("ocp", u - 1)])

                def avmm(e, par=par, ch=ch, pi=pi):
                    vt = vtm[ch]
                    rint = rhat[par][:, 5 * C:6 * C]
                    e.matmul(pO2[0][:], A2b[par][:, 0:C], vt[:, pi * 128:pi * 128 + 64],
                             start=True, stop=False)
                    e.matmul(pO2[0][:], rint, Sbf[pi][:, 0:64], start=False, stop=True)
                    e.matmul(pO2[1][:], A2b[par][:, C:2 * C],
                             vt[:, pi * 128 + 64:(pi + 1) * 128], start=True, stop=False)
                    e.matmul(pO2[1][:], rint, Sbf[pi][:, 64:128], start=False, stop=True)
                    e.matmul(P[0][0:64, 0:64], ktmpA[ch % 2][:, pi * 128:pi * 128 + 64],
                             vt[:, pi * 128:pi * 128 + 64], start=True, stop=True)
                    return e.matmul(P[0][64:128, 0:64],
                                    ktmpA[ch % 2][:, pi * 128 + 64:(pi + 1) * 128],
                                    vt[:, pi * 128 + 64:(pi + 1) * 128],
                                    start=True, stop=True, tile_position=(0, 64))
                c.op("tensor", stn, avmm)
                marks[("AV", u)] = stn.n
                c.wait("vector", stn, marks[("AV", u)])
                if ("sbf", g, pi) in marks:
                    c.wait("vector", ss, marks[("sbf", g, pi)])
                c.op("vector", sv, lambda e, pi=pi: e.scalar_tensor_tensor(
                    Sst[pi][0:64, 0:64], Sst[pi][0:64, 0:64], e128[0:64, pi:pi + 1],
                    P[0][0:64, 0:64], AL.mult, AL.add))
                c.op("vector", sv, lambda e, pi=pi: e.scalar_tensor_tensor(
                    Sst[pi][64:128, 64:128], Sst[pi][64:128, 64:128], e128[64:128, pi:pi + 1],
                    P[0][64:128, 0:64], AL.mult, AL.add))
                marks[("stt", u)] = sv.n

                def gn_bn(e, par=par):
                    e.bn_stats(b6[par][:, 0:6], pO2[0][:])
                    e.bn_stats(b6[par][:, 6:12], pO2[1][:])
                    e.drain()
                    e.bn_aggr(stat[par][:, 0:2], b6[par][:, 0:6])
                    return e.bn_aggr(stat[par][:, 2:4], b6[par][:, 6:12])
                c.op("vector", sv, gn_bn)
                n_bn = sv.n
                c.wait("scalar", sv, marks[("stt", u)])
                c.op("scalar", ss, lambda e, pi=pi: e.copy(Sbf[pi][:], Sst[pi][:]))
                marks[("sbf", g, pi)] = ss.n
                c.wait("scalar", sv, n_bn)
                c.op("scalar", ss, lambda e, par=par: e.activation(
                    stat[par][:, 4:6],
                    stat[par][:, 0:4].rearrange("q (a b) -> q a b", b=2)[:, :, 1],
                    AF.Sqrt, bias=EPS))
                n_sq = ss.n

                def fin(u=u, par=par, pi=pi, ch=ch, gc=gc, n_sq=n_sq):
                    c.wait("vector", ss, n_sq)
                    c.op("vector", sv, lambda e, par=par: e.reciprocal(
                        stat[par][:, 6:8], stat[par][:, 4:6]))
                    c.emit("vector", lambda e: e.drain())
                    if gc - 2 in xn_dma:
                        c.wait("vector", sq, xn_dma[gc - 2])

                    def ap0(e, par=par, pi=pi, ch=ch):
                        xd = xnb[ch % 2]
                        return e.tensor_scalar(xd[:, pi * 128:pi * 128 + 64], pO2[0][:],
                                               stat[par][:, 0:1], stat[par][:, 6:7],
                                               AL.subtract, AL.mult)

                    def ap1(e, par=par, pi=pi, ch=ch):
                        xd = xnb[ch % 2]
                        return e.tensor_scalar(xd[:, pi * 128 + 64:(pi + 1) * 128], pO2[1][:],
                                               stat[par][:, 2:3], stat[par][:, 7:8],
                                               AL.subtract, AL.mult)
                    c.op("vector", sv, ap0)
                    c.op("vector", sv, ap1)
                    if ostg is not None:
                        c.op("vector", sv, lambda e, pi=pi, ch=ch: e.tensor_copy(
                            ostg[ch % 2][:, pi * 128:pi * 128 + 64], pO2[0][:]))
                        c.op("vector", sv, lambda e, pi=pi, ch=ch: e.tensor_copy(
                            ostg[ch % 2][:, pi * 128 + 64:(pi + 1) * 128], pO2[1][:]))
                        marks[("ocp", u)] = sv.n
                    marks[("gn", u)] = sv.n
                pend_fin = fin
            if pend_fin is not None:
                pend_fin()
                pend_fin = None
            c.wait("sync", sv, sv.n)
            c.dma("sync", sq, sc_xn[ch * 128:(ch + 1) * 128, g * 1024:(g + 1) * 1024],
                  xnb[ch % 2][:])
            if ostg is not None:
                c.dma("sync", sq, sc_O[ch * 128:(ch + 1) * 128, g * 1024:(g + 1) * 1024],
                      ostg[ch % 2][:])
            xn_dma[g * NCH + ch] = sq.n
    c.phase_close()
    c.barrier(bar, DMAS)


    mstack.close()

    y1f = [c.sb(f"y1f{i}", [128, TS], BF16, phase=False) for i in range(32)]
    lstat = c.sb("lstat", [128, 8 * 8], F32, phase=False)
    lst6 = c.sb("lst6", [128, 8 * 48], F32, phase=False)
    c.phase_open()
    gxn = [c.sb(f"gxn{i}", [128, TS], BF16) for i in range(32)]
    gt = [c.sb(f"gt{i}", [128, TS], BF16) for i in range(2)]
    xf = [c.sb(f"xf{i}", [128, TS], BF16) for i in range(2)]
    woslab = [c.sb(f"woslab{i}", [128, 16 * 512], BF16) for i in range(2)]
    xres = [c.sb(f"xres{i}", [128, 512], F32) for i in range(2)]
    ypst = [c.sb(f"ypst{i}", [128, 512], F32) for i in range(2)]
    pm = [c.ps(f"pDm{i}", [128, 512]) for i in range(2)]

    gxn_mark = {}
    for i in range(32):
        if i >= 2:
            c.wait("sync", sv, gxn_mark[i - 2])
        c.dma("sync", sq, xf[i % 2][:], sc_xn[0:TS, i * 128:(i + 1) * 128], transpose=True)
        c.dma("sync", sq, gt[i % 2][:], sc_g[i * 128:(i + 1) * 128, :])
        n_l = sq.n
        c.wait("vector", sq, n_l)
        c.op("vector", sv, lambda e, i=i: e.tensor_scalar(
            xf[i % 2][:], xf[i % 2][:], gnwc[:, i:i + 1], gnbc[:, i:i + 1], AL.mult, AL.add))
        c.op("vector", sv, lambda e, i=i: e.tensor_mul(gxn[i][:], xf[i % 2][:], gt[i % 2][:]))
        gxn_mark[i] = sv.n
    n_gxn = sv.n

    def slab_ap2(wt, oh, half):
        return wt[oh].rearrange("(hf it q) oc -> hf q it oc", hf=2, q=128)[half]

    wo_mark = {}
    wo_amark = {}
    ev2 = {}
    st2 = {}
    for oh in range(8):
        if oh >= 1:
            c.wait("gpsimd", stn, wo_amark[oh - 1])
        c.dma("gpsimd", gq, woslab[0][:].rearrange("q (it oc) -> q it oc", oc=512),
              slab_ap2(wo, oh, 0))
        n_slA = gq.n
        if oh >= 1:
            c.wait("gpsimd", stn, wo_mark[oh - 1])
        c.dma("gpsimd", gq, woslab[1][:].rearrange("q (it oc) -> q it oc", oc=512),
              slab_ap2(wo, oh, 1))
        n_slB = gq.n
        for tt in range(8):
            un = oh * 8 + tt
            if tt == 0:
                c.wait("tensor", gq, n_slA)
            if un == 0:
                c.wait("tensor", sv, n_gxn)
            if un >= 2:
                c.wait("tensor", sv, ev2[un - 2])

            def mmwoA(e, oh=oh, tt=tt):
                last = None
                for i in range(16):
                    last = e.matmul(pm[tt % 2][:], gxn[i][:, tt * 128:(tt + 1) * 128],
                                    woslab[0][:, i * 512:(i + 1) * 512],
                                    start=(i == 0), stop=False)
                return last
            c.op("tensor", stn, mmwoA)
            if tt == 7:
                wo_amark[oh] = stn.n
            if tt == 0:
                c.wait("tensor", gq, n_slB)

            def mmwoB(e, oh=oh, tt=tt):
                last = None
                for i in range(16):
                    last = e.matmul(pm[tt % 2][:], gxn[16 + i][:, tt * 128:(tt + 1) * 128],
                                    woslab[1][:, i * 512:(i + 1) * 512],
                                    start=False, stop=(i == 15))
                return last
            c.op("tensor", stn, mmwoB)
            if tt == 7:
                wo_mark[oh] = stn.n
            n_mm = stn.n
            c.wait("gpsimd", sv, ev2.get(un - 2, 0))
            c.dma("gpsimd", gq, xres[tt % 2][:],
                  x_tm[tt * 128:(tt + 1) * 128, oh * 512:(oh + 1) * 512])
            n_xr = gq.n
            c.wait("vector", stn, n_mm)
            c.wait("vector", gq, n_xr)
            if un >= 2:
                c.wait("vector", sq, st2[un - 2])
            c.op("vector", sv, lambda e, tt=tt: e.tensor_add(
                ypst[tt % 2][:], pm[tt % 2][:], xres[tt % 2][:]))
            ev2[un] = sv.n
            c.wait("sync", sv, ev2[un])
            c.dma("sync", sq, sc_yp[tt * 128:(tt + 1) * 128, oh * 512:(oh + 1) * 512],
                  ypst[tt % 2][:])
            st2[un] = sq.n
            c.op("vector", sv, lambda e, tt=tt, oh=oh: e.bn_stats(
                lst6[:, tt * 48 + oh * 6:tt * 48 + oh * 6 + 6], ypst[tt % 2][:]))
    n_yp_all = sq.n
    c.phase_close()
    c.barrier(bar, DMAS)
    c.phase_open()
    yp_t = [c.sb(f"yp_t{i}", [128, DM], F32) for i in range(2)]
    y1tm = [c.sb(f"y1tm{i}", [128, DM], BF16) for i in range(2)]
    pTd = []
    pTd.append(c.ps("pDtr0", [128, 128], BF16))
    pTd.append(c.ps("pDtr1", [128, 128], BF16))

    def ln_finalize(L, S6):
        c.emit("vector", lambda e: e.drain())
        for tt in range(8):
            c.op("vector", sv, lambda e, tt=tt, L=L, S6=S6: e.bn_aggr(
                L[:, tt * 8:tt * 8 + 2], S6[:, tt * 48:(tt + 1) * 48]))
        c.wait("scalar", sv)
        for tt in range(8):
            c.op("scalar", ss, lambda e, tt=tt, L=L: e.activation(
                L[:, tt * 8 + 3:tt * 8 + 4], L[:, tt * 8 + 1:tt * 8 + 2], AF.Sqrt, bias=EPS))
        c.wait("vector", ss)
        for tt in range(8):
            c.op("vector", sv, lambda e, tt=tt, L=L: e.reciprocal(
                L[:, tt * 8 + 4:tt * 8 + 5], L[:, tt * 8 + 3:tt * 8 + 4]))
        c.wait("scalar", sv)
        for tt in range(8):
            c.op("scalar", ss, lambda e, tt=tt, L=L: e.mul(
                L[:, tt * 8 + 6:tt * 8 + 7], L[:, tt * 8 + 4:tt * 8 + 5], -1.0))
        c.wait("vector", ss)
        for tt in range(8):
            c.op("vector", sv, lambda e, tt=tt, L=L: e.tensor_mul(
                L[:, tt * 8 + 5:tt * 8 + 6], L[:, tt * 8:tt * 8 + 1],
                L[:, tt * 8 + 6:tt * 8 + 7]))

    ln_finalize(lstat, lst6)
    n_ln1 = sv.n

    c.wait("sync", sq, n_yp_all)
    ap_mark = {}
    tr_cp = {}
    ap_s = {}
    for tt in range(8):
        if tt >= 2:
            c.wait("sync", ss, ap_s[tt - 2])
        c.dma("sync", sq, yp_t[tt % 2][:], sc_yp[tt * 128:(tt + 1) * 128, :])
        n_l = sq.n
        c.wait("scalar", sq, n_l)
        if tt == 0:
            c.wait("scalar", sv, n_ln1)
        if tt >= 2:
            c.wait("scalar", stn, ap_mark[tt - 2])
        c.op("scalar", ss, lambda e, tt=tt: e.activation(
            y1tm[tt % 2][:], yp_t[tt % 2][:], AF.Identity,
            bias=lstat[:, tt * 8 + 5:tt * 8 + 6], scale=lstat[:, tt * 8 + 4:tt * 8 + 5]))
        n_ap = ss.n
        ap_s[tt] = n_ap
        c.wait("tensor", ss, n_ap)
        for ft in range(32):
            fg = tt * 32 + ft
            if fg >= 2:
                c.wait("tensor", sv, tr_cp[fg - 2])
            c.op("tensor", stn, lambda e, tt=tt, ft=ft, fg=fg: e.transpose(
                pTd[fg % 2][:], y1tm[tt % 2][:, ft * 128:(ft + 1) * 128], id_s[:]))
            c.wait("vector", stn, stn.n)
            c.op("vector", sv, lambda e, tt=tt, ft=ft, fg=fg: e.tensor_copy(
                y1f[ft][:, tt * 128:(tt + 1) * 128], pTd[fg % 2][:]))
            tr_cp[fg] = sv.n
        ap_mark[tt] = stn.n
    c.phase_close()
    c.barrier(bar, DMAS)

    l2stat = c.sb("l2stat", [128, 8 * 8], F32, phase=False)
    l2st6 = c.sb("l2st6", [128, 8 * 48], F32, phase=False)
    c.phase_open()
    wfslab = [c.sb(f"wfslab{i}", [128, 32 * 512], BF16) for i in range(2)]
    zpst = [c.sb(f"zpst{i}", [128, 512], F32) for i in range(2)]
    y1rs = [c.sb(f"y1rs{i}", [128, 512], F32) for i in range(2)]
    ypr = [c.sb(f"ypr{i}", [128, 512], F32) for i in range(2)]
    fbs = c.sb("fbs", [128, DM], F32)
    pm2 = [c.ps(f"pEm{i}", [128, 512]) for i in range(2)]

    c.dma("sync", sq, fbs[:], fbr)
    n_fb = sq.n
    wf_mark = {}
    ev3 = {}
    st3 = {}
    y1_mark = {}
    for oh in range(8):
        if oh >= 2:
            c.wait("gpsimd", stn, wf_mark[oh - 2])
        for hf in range(2):
            c.dma("gpsimd", gq, wfslab[oh % 2][:, hf * 8192:(hf + 1) * 8192].rearrange(
                "q (it oc) -> q it oc", oc=512), slab_ap2(wf, oh, hf))
        n_sl = gq.n
        c.wait("scalar", gq, n_sl)
        if oh >= 2:
            c.wait("scalar", stn, wf_mark[oh - 2])
        for it in range(32):
            c.op("scalar", ss, lambda e, oh=oh, it=it: e.sign(
                wfslab[oh % 2][:, it * 512:(it + 1) * 512],
                wfslab[oh % 2][:, it * 512:(it + 1) * 512]))
        n_sign = ss.n
        for tt in range(8):
            un = oh * 8 + tt
            c.wait("tensor", ss, n_sign)
            if un >= 2:
                c.wait("tensor", sv, ev3[un - 2])

            def mmwf(e, oh=oh, tt=tt):
                last = None
                sl = wfslab[oh % 2]
                for i in range(32):
                    last = e.matmul(pm2[tt % 2][:], y1f[i][:, tt * 128:(tt + 1) * 128],
                                    sl[:, i * 512:(i + 1) * 512],
                                    start=(i == 0), stop=(i == 31))
                return last
            c.op("tensor", stn, mmwf)
            if tt == 7:
                wf_mark[oh] = stn.n
            n_mm = stn.n
            if un >= 2:
                c.wait("gpsimd", ss, y1_mark[un - 2])
            c.dma("gpsimd", gq, ypr[tt % 2][:],
                  sc_yp[tt * 128:(tt + 1) * 128, oh * 512:(oh + 1) * 512])
            n_ypr = gq.n
            c.wait("scalar", gq, n_ypr)
            if un >= 2:
                c.wait("scalar", sv, ev3[un - 2])
            c.op("scalar", ss, lambda e, tt=tt: e.activation(
                y1rs[tt % 2][:], ypr[tt % 2][:], AF.Identity,
                bias=lstat[:, tt * 8 + 5:tt * 8 + 6], scale=lstat[:, tt * 8 + 4:tt * 8 + 5]))
            y1_mark[un] = ss.n
            c.wait("vector", stn, n_mm)
            if un == 0:
                c.wait("vector", sq, n_fb)
            c.wait("vector", ss, y1_mark[un])
            if un >= 2:
                c.wait("vector", sq, st3[un - 2])
            c.op("vector", sv, lambda e, tt=tt, oh=oh: e.tensor_add(
                zpst[tt % 2][:], pm2[tt % 2][:], fbs[:, oh * 512:(oh + 1) * 512]))
            c.op("vector", sv, lambda e, tt=tt: e.tensor_add(
                zpst[tt % 2][:], zpst[tt % 2][:], y1rs[tt % 2][:]))
            ev3[un] = sv.n
            c.wait("sync", sv, ev3[un])
            c.dma("sync", sq, sc_zp[tt * 128:(tt + 1) * 128, oh * 512:(oh + 1) * 512],
                  zpst[tt % 2][:])
            st3[un] = sq.n
            c.op("vector", sv, lambda e, tt=tt, oh=oh: e.bn_stats(
                l2st6[:, tt * 48 + oh * 6:tt * 48 + oh * 6 + 6], zpst[tt % 2][:]))
    n_zp_all = sq.n
    c.phase_close()
    c.barrier(bar, DMAS)
    c.phase_open()
    zp_t = [c.sb(f"zp_t2{i}", [128, DM], F32) for i in range(2)]
    zt = [c.sb(f"zt{i}", [128, DM], F32) for i in range(2)]

    ln_finalize(l2stat, l2st6)
    n_ln2 = sv.n

    c.wait("sync", sq, n_zp_all)
    zst = {}
    zap = {}
    for tt in range(8):
        if tt >= 2:
            c.wait("sync", ss, zap[tt - 2])
        c.dma("sync", sq, zp_t[tt % 2][:], sc_zp[tt * 128:(tt + 1) * 128, :])
        n_l = sq.n
        c.wait("scalar", sq, n_l)
        if tt == 0:
            c.wait("scalar", sv, n_ln2)
        if tt >= 2:
            c.wait("scalar", sq, zst[tt - 2])
        c.op("scalar", ss, lambda e, tt=tt: e.activation(
            zt[tt % 2][:], zp_t[tt % 2][:], AF.Identity,
            bias=l2stat[:, tt * 8 + 5:tt * 8 + 6], scale=l2stat[:, tt * 8 + 4:tt * 8 + 5]))
        zap[tt] = ss.n
        c.wait("sync", ss, ss.n)
        c.dma("sync", sq, out[tt * 128:(tt + 1) * 128, :], zt[tt % 2][:])
        zst[tt] = sq.n
    c.phase_close()

    if taps:
        c.wait("gpsimd", sq, sq.n)
        c.wait("gpsimd", ss, ss.n)
        c.wait("gpsimd", sv, sv.n)
        for name, (tap, src) in taps.items():
            c.dma("gpsimd", gq, tap, src)
    c.wait("sync", sq, sq.n)
    c.wait("gpsimd", gq, gq.n)

    with nc.Block() as block:
        c.run(block)
    c.gstack.close()
    return nc


_NC_CACHE = {}


def _get_nc(key=()):
    if key not in _NC_CACHE:
        _NC_CACHE[key] = build_program(debug_taps=key)
    return _NC_CACHE[key]


def _cols32(a):
    return np.ascontiguousarray(np.asarray(a, np.float32).reshape(32, 128).T)


def make_inputs(inputs):
    x = np.asarray(inputs["L0_attn_hidden"], np.float32).reshape(S, DM)
    xT_full = np.ascontiguousarray(x.T)
    xprev = np.concatenate([np.zeros((DM, 1), np.float32), xT_full[:, :-1]], axis=1)

    def wblocks(w, bs):
        wt = np.ascontiguousarray(np.asarray(w, np.float32).T)
        return np.ascontiguousarray(wt.reshape(DM, DM // bs, bs).transpose(1, 0, 2))

    wkb = wblocks(inputs["L0_attn_w_key"], 128)
    wvb = wblocks(inputs["L0_attn_w_value"], 128)
    wrb = wblocks(inputs["L0_attn_w_receptance"], 128)
    wgb = wblocks(inputs["L0_attn_w_gate"], 128)
    wob = wblocks(inputs["L0_attn_w_output"], 512)
    wfb = wblocks(inputs["L0_ffn_weight"], 512)
    mcols = np.ascontiguousarray(np.concatenate(
        [_cols32(inputs["L0_attn_time_mix_key"]),
         _cols32(inputs["L0_attn_time_mix_value"]),
         _cols32(inputs["L0_attn_time_mix_receptance"]),
         _cols32(inputs["L0_attn_time_mix_gate"])], axis=1))
    tdp = _cols32(inputs["L0_attn_time_decay"])
    tfp = _cols32(inputs["L0_attn_time_faaaa"])
    gnw = _cols32(inputs["L0_attn_gn_weight"])
    gnb = _cols32(inputs["L0_attn_gn_bias"])
    fbr = np.ascontiguousarray(np.broadcast_to(
        np.asarray(inputs["L0_ffn_bias"], np.float32)[None, :], (128, DM)))
    iota, masks, ident = _host_consts()
    in_maps = []
    for core in range(NCORE):
        t0 = core * TS
        xTs = np.ascontiguousarray(
            np.concatenate([xprev[:, t0:t0 + 1], xT_full[:, t0:t0 + TS]], axis=1))
        x_tms = np.ascontiguousarray(x[t0:t0 + TS])
        gap = np.array([1024.0 * (core - 1 - j) if j < core else 1e9 for j in range(8)],
                      np.float32)
        gapx = np.ascontiguousarray(np.broadcast_to(gap[None, :], (128, 8)))
        in_maps.append({
            "xT": xTs, "x_tm": x_tms, "wkT_b": wkb, "wvT_b": wvb, "wrT_b": wrb,
            "wgT_b": wgb, "woT_b": wob, "wfT_b": wfb, "mcols": mcols, "tdp": tdp,
            "tfp": tfp, "gnw": gnw, "gnb": gnb, "fbias_rep": fbr, "iota_c": iota,
            "mask_c": masks, "ident_c": ident, "gapix": gapx,
        })
    return in_maps


def kernel(**inputs):
    nc = _get_nc()
    in_maps = make_inputs(inputs)
    res = run_bass_kernel_spmd(nc, in_maps, core_ids=list(range(NCORE)))
    shards = [np.asarray(res.results[i]["out_sh"]) for i in range(NCORE)]
    return np.concatenate(shards, axis=0).reshape(1, S, DM)


# version 08
# speedup vs baseline: 1.0684x; 1.0036x over previous
import contextlib
import numpy as np
import ml_dtypes
import concourse.bass as bass
import concourse.mybir as mybir
from concourse.bass_utils import run_bass_kernel_spmd
from concourse.replica_groups import maybe_share_collective_output_space

F32 = mybir.dt.float32
BF16 = mybir.dt.bfloat16
U8 = mybir.dt.uint8
AF = mybir.ActivationFunctionType
AL = mybir.AluOpType
AX = mybir.AxisListType
BF = ml_dtypes.bfloat16

NCORE = 8
S = 8192
TS = S // NCORE
DM = 4096
NP = 32
C = 128
NCH = TS // C
NG = 4
GP = NP // NG
EPS = 1e-5
NVAR = 12


class Sem:
    def __init__(self, h):
        self.h = h
        self.n = 0


class Ctx:
    ENGS = ("sync", "gpsimd", "tensor", "vector", "scalar")

    def __init__(self, nc):
        self.nc = nc
        self.st = {e: [] for e in self.ENGS}
        self.gstack = contextlib.ExitStack()
        self.pstack = None

    def sem(self, name):
        return Sem(self.gstack.enter_context(self.nc.semaphore(name)))

    def sb(self, name, shape, dt, phase=True):
        stk = self.pstack if (phase and self.pstack is not None) else self.gstack
        return stk.enter_context(self.nc.sbuf_tensor(name, shape, dt))

    def ps(self, name, shape, dt=F32, phase=True):
        stk = self.pstack if (phase and self.pstack is not None) else self.gstack
        return stk.enter_context(self.nc.psum_tensor(name, shape, dt))

    def phase_open(self):
        assert self.pstack is None
        self.pstack = contextlib.ExitStack()

    def phase_close(self):
        self.pstack.close()
        self.pstack = None

    def emit(self, eng, fn):
        self.st[eng].append(fn)

    def op(self, eng, sem, fn):
        def run(e, fn=fn, h=sem.h):
            fn(e).then_inc(h, 1)
        self.st[eng].append(run)
        sem.n += 1
        return sem.n

    def dma(self, q, sem, o, i, transpose=False):
        def run(e, o=o, i=i, h=sem.h, tr=transpose):
            e.dma_start(o, i, transpose=True).then_inc(h, 16) if tr else \
                e.dma_start(o, i).then_inc(h, 16)
        self.st[q].append(run)
        sem.n += 16
        return sem.n

    def wait(self, eng, sem, val=None):
        v = sem.n if val is None else val
        if v > 0:
            self.emit(eng, lambda e, h=sem.h, v=v: e.wait_ge(h, v))

    def barrier(self, bsem, dma_sems):
        for q, s in dma_sems.items():
            self.wait(q, s, s.n)
        for eng in self.ENGS:
            self.op(eng, bsem, lambda e: e.nop())
        for eng in self.ENGS:
            self.wait(eng, bsem, bsem.n)

    def run(self, block):
        def mk(lst):
            def body(eng):
                for f in lst:
                    f(eng)
            return body
        block.sync(mk(self.st["sync"]))
        block.gpsimd(mk(self.st["gpsimd"]))
        block.tensor(mk(self.st["tensor"]))
        block.vector(mk(self.st["vector"]))
        block.scalar(mk(self.st["scalar"]))


def _host_consts():
    j = np.arange(C)
    pats = [3 - (j % 8), 7 - (j % 8), 15 - (j % 16), 31 - (j % 32), 63 - (j % 64),
            (j % 8) - 4, j % 8, j % 16, j % 32, j % 64, j, 127 - j]
    iota = np.concatenate([p.astype(np.float32) for p in pats])[None, :]
    iota = np.ascontiguousarray(np.broadcast_to(iota, (128, NVAR * C)))
    ii = j[None, :]
    jj = j[:, None]
    lower = ii > jj

    def adj(L):
        return lower & ((ii // L) == (jj // L) + 1) & ((ii // (2 * L)) == (jj // (2 * L)))
    masks = [lower & ((ii // 8) == (jj // 8)), adj(8), adj(16), adj(32), adj(64), ii == jj]
    masks = np.ascontiguousarray(np.concatenate([m.astype(np.uint8) for m in masks], axis=1))
    ident = np.ascontiguousarray(np.eye(128, dtype=np.float32)).astype(BF)
    return iota, masks, ident


def build_program(debug_taps=()):
    nc = bass.Bass("TRN2", target_bir_lowering=False)
    c = Ctx(nc)
    DT = nc.dram_tensor
    IN = lambda n, sh, dt=F32: DT(n, sh, dt, kind="ExternalInput").ap()
    xT = IN("xT", [DM, TS + 1])
    x_tm = IN("x_tm", [TS, DM])
    wk = IN("wkT_b", [32, DM, 128])
    wv = IN("wvT_b", [32, DM, 128])
    wr = IN("wrT_b", [32, DM, 128])
    wg = IN("wgT_b", [32, DM, 128])
    wo = IN("woT_b", [8, DM, 512])
    wf = IN("wfT_b", [8, DM, 512])
    mcols = IN("mcols", [128, 4 * 32])
    tdp = IN("tdp", [128, 32])
    tfp = IN("tfp", [128, 32])
    gnw = IN("gnw", [128, 32])
    gnb = IN("gnb", [128, 32])
    fbr = IN("fbias_rep", [128, DM])
    iota_c = IN("iota_c", [128, NVAR * C])
    mask_c = IN("mask_c", [128, 6 * C], U8)
    id_c = IN("ident_c", [128, 128], BF16)
    gapix = IN("gapix", [128, 8])
    out = DT("out_sh", [TS, DM], F32, kind="ExternalOutput").ap()

    sc_k = DT("sc_k", [DM, TS], BF16, kind="Internal").ap()
    sc_v = DT("sc_v", [DM, TS], BF16, kind="Internal").ap()
    sc_r = DT("sc_r", [DM, TS], BF16, kind="Internal").ap()
    sc_g = DT("sc_g", [DM, TS], BF16, kind="Internal").ap()
    sc_xn = DT("sc_xn", [TS, DM], BF16, kind="Internal").ap()
    sc_O = DT("sc_O", [TS, DM], F32, kind="Internal").ap()
    sc_yp = DT("sc_yp", [TS, DM], F32, kind="Internal").ap()
    sc_zp = DT("sc_zp", [TS, DM], F32, kind="Internal").ap()
    cc_in = DT("cc_in", [NP, 128, 128], F32, kind="Internal").ap()
    aspace = maybe_share_collective_output_space("AllGather", [list(range(NCORE))])
    cc_out = DT("cc_out", [NCORE, NP, 128, 128], F32, kind="Internal", addr_space=aspace).ap()

    tap_specs = {"k": (sc_k, BF16), "v": (sc_v, BF16), "r": (sc_r, BF16), "g": (sc_g, BF16),
                 "xn": (sc_xn, BF16), "yp": (sc_yp, F32), "zp": (sc_zp, F32),
                 "B": (cc_in, F32), "Bg": (cc_out, F32), "O": (sc_O, F32)}
    taps = {}
    for name in debug_taps:
        src, dt = tap_specs[name]
        taps[name] = (DT("tap_" + name, list(src.shape), dt, kind="ExternalOutput").ap(), src)

    sq = c.sem("sq")
    gq = c.sem("gq")
    sv = c.sem("sv")
    ss = c.sem("ss")
    stn = c.sem("stn")
    scc = c.sem("scc")
    bar = c.sem("bar")
    DMAS = {"sync": sq, "gpsimd": gq}

    iota_s = c.sb("iota_s", [128, NVAR * C], F32, phase=False)
    mask_s = c.sb("mask_s", [128, 6 * C], U8, phase=False)
    id_s = c.sb("id_s", [128, 128], BF16, phase=False)
    wcol = c.sb("wcol", [128, 32], F32, phase=False)
    tfcol = c.sb("tfcol", [128, 32], F32, phase=False)
    mc = c.sb("mc", [128, 4 * 32], F32, phase=False)
    gnwc = c.sb("gnwc", [128, 32], F32, phase=False)
    gnbc = c.sb("gnbc", [128, 32], F32, phase=False)
    gapix_s = c.sb("gapix_s", [128, 8], F32, phase=False)
    epsc = c.sb("epsc", [128, 1], F32, phase=False)
    negc = c.sb("negc", [128, 1], F32, phase=False)
    c.op("vector", sv, lambda e: e.memset(epsc[:], EPS))
    c.op("vector", sv, lambda e: e.memset(negc[:], -1.0))
    nc.const_aps.register(F32, EPS, epsc[:, 0:1])
    nc.const_aps.register(F32, -1.0, negc[:, 0:1])

    for dst, src in ((iota_s, iota_c), (mask_s, mask_c), (id_s, id_c), (wcol, tdp),
                     (tfcol, tfp), (mc, mcols), (gnwc, gnw), (gnbc, gnb), (gapix_s, gapix)):
        c.dma("sync", sq, dst[:], src)
    c.wait("scalar", sq, sq.n)
    c.op("scalar", ss, lambda e: e.activation(wcol[:], wcol[:], AF.Exp))
    c.wait("vector", ss)
    c.op("vector", sv, lambda e: e.tensor_scalar_mul(wcol[:], wcol[:], -1.0))
    n_wcol = sv.n
    c.wait("tensor", sv, n_wcol)
    c.wait("scalar", sv, n_wcol)

    mstack = contextlib.ExitStack()
    MSB = lambda n, sh, dt: mstack.enter_context(nc.sbuf_tensor(n, sh, dt))
    ktm = [MSB(f"ktm{i}", [128, TS], BF16) for i in range(NCH)]
    vtm = [MSB(f"vtm{i}", [128, TS], BF16) for i in range(NCH)]
    eupdT_all = MSB("eupdT_all", [128, GP * 128], BF16)
    e128 = MSB("e128", [128, GP], F32)
    Sst = [MSB(f"Sst{i}", [128, 128], F32) for i in range(GP)]
    ktmpA = [MSB(f"ktmpA{i}", [128, GP * 128], BF16) for i in range(2)]
    expn = MSB("expn", [128, NVAR * C], F32)
    eupd_f = MSB("eupd_f", [128, 128], BF16)

    c.phase_open()
    x_sb = [c.sb(f"x_sb{i}", [128, TS + 1], BF16) for i in range(32)]
    mix = [c.sb(f"mix{i}", [128, TS], BF16) for i in range(32)]
    tmpm = [c.sb("tmpm0", [128, TS], BF16)]
    wslab = [c.sb(f"wslab{i}", [128, DM], BF16) for i in range(2)]
    pstage = [c.sb(f"pstage{i}", [128, TS], BF16) for i in range(2)]
    pkp = [c.ps(f"pk{i}", [128, 512]) for i in range(4)]
    pU = []
    pU.append(c.ps("p1U0", [128, 64]))
    c.ps("p1pad0", [128, 448])
    pU.append(c.ps("p1U1", [128, 64]))
    c.ps("p1pad1", [128, 448])
    pTr = pkp[3]

    for i in range(32):
        c.dma("gpsimd", gq, x_sb[i][:], xT[i * 128:(i + 1) * 128, :])
    n_xload = gq.n

    def slab_ap(wt, o):
        return wt[o].rearrange("(it q) oc -> q it oc", q=128)

    projs = [(wk, 0, sc_k, None), (wv, 1, sc_v, None), (wr, 2, sc_r, None), (wg, 3, sc_g, AF.Silu)]
    mm_mark = {}
    ev_mark = {}
    st_mark = {}
    def emit_proj(pj):
        wt, mi, scr, act = projs[pj]
        if pj > 0:
            c.wait("vector", gq, n_xload)
            c.wait("vector", stn, mm_mark[(pj - 1, 31)])
        mix_mark = {}
        for i in range(32):
            t = tmpm[0]
            cur = x_sb[i][:, 1:TS + 1]
            prv = x_sb[i][:, 0:TS]
            if pj == 0:
                c.wait("vector", gq, 16 * (i + 1))
            c.op("vector", sv, lambda e, t=t, cur=cur, prv=prv: e.tensor_sub(t[:], cur, prv))
            c.op("vector", sv, lambda e, t=t, i=i, mi=mi, prv=prv:
                 e.scalar_tensor_tensor(mix[i][:], t[:], mc[:, mi * 32 + i:mi * 32 + i + 1],
                                        prv, AL.mult, AL.add))
            mix_mark[i] = sv.n
        n_mixv = sv.n
        for o in range(32):
            gk = (pj, o)
            if o >= 2:
                c.wait("gpsimd", stn, mm_mark[(pj, o - 2)])
            elif pj > 0:
                c.wait("gpsimd", stn, mm_mark[(pj - 1, 30 + o)])
            c.dma("gpsimd", gq, wslab[o % 2][:].rearrange("q (it oc) -> q it oc", oc=128),
                  slab_ap(wt, o))
            n_sl = gq.n
            c.wait("tensor", gq, n_sl)
            pk2 = (pj, o - 2) if o >= 2 else ((pj - 1, 30 + o) if pj > 0 else None)
            if pk2 is not None:
                sE, nE = ev_mark[pk2]
                c.wait("tensor", sE, nE)
            pA, pB = pkp[(o % 2) * 2], pkp[(o % 2) * 2 + 1]
            if o == 0:
                for i in range(32):
                    c.wait("tensor", sv, mix_mark[i])

                    def mm1(e, i=i, pA=pA, pB=pB):
                        lhs = wslab[0][:, i * 128:(i + 1) * 128]
                        e.matmul(pA[:], lhs, mix[i][:, 0:512], start=(i == 0), stop=(i == 31))
                        return e.matmul(pB[:], lhs, mix[i][:, 512:1024],
                                        start=(i == 0), stop=(i == 31))
                    c.op("tensor", stn, mm1)
                mm_mark[gk] = stn.n

            if o > 0:
                def mms(e, o=o, pA=pA, pB=pB):
                    w2 = wslab[o % 2]
                    last = None
                    for i in range(32):
                        lhs = w2[:, i * 128:(i + 1) * 128]
                        e.matmul(pA[:], lhs, mix[i][:, 0:512], start=(i == 0), stop=(i == 31))
                        last = e.matmul(pB[:], lhs, mix[i][:, 512:1024],
                                        start=(i == 0), stop=(i == 31))
                    return last
                c.op("tensor", stn, mms)
                mm_mark[gk] = stn.n
            eng = "vector" if act is None else "scalar"
            esem = sv if act is None else ss
            c.wait(eng, stn, mm_mark[gk])
            if pk2 is not None and pk2 in st_mark:
                c.wait(eng, sq, st_mark[pk2])
            stg = pstage[o % 2]
            if act is None:
                c.op("vector", sv, lambda e, stg=stg, pA=pA: e.tensor_copy(stg[:, 0:512], pA[:]))
                c.op("vector", sv, lambda e, stg=stg, pB=pB: e.tensor_copy(stg[:, 512:1024], pB[:]))
            else:
                c.op("scalar", ss, lambda e, stg=stg, pA=pA, act=act: e.activation(stg[:, 0:512], pA[:], act))
                c.op("scalar", ss, lambda e, stg=stg, pB=pB, act=act: e.activation(stg[:, 512:1024], pB[:], act))
            ev_mark[gk] = (esem, esem.n)
            c.wait("sync", esem, esem.n)
            c.dma("sync", sq, scr[o * 128:(o + 1) * 128, :], stg[:])
            st_mark[gk] = sq.n

    emit_proj(0)
    emit_proj(1)
    emit_proj(2)

    def build_pair_patterns(pi, p, full):
        c.wait("vector", ss)
        c.wait("scalar", stn)
        if full:
            c.op("vector", sv, lambda e, p=p: e.tensor_scalar(
                expn[:], iota_s[:], wcol[:, p:p + 1], None, AL.mult))
            c.wait("scalar", sv)
            c.op("scalar", ss, lambda e, pi=pi: e.activation(pat2[pi][:], expn[:], AF.Exp))
            c.wait("tensor", ss)
            c.wait("tensor", sv)
            c.op("tensor", stn, lambda e, pi=pi: e.transpose(
                trg_ap(), pat2[pi][:, 11 * C:12 * C], id_s[:]))
        else:
            c.op("vector", sv, lambda e, p=p: e.tensor_scalar(
                expn[:, 0:C], iota_s[:, 11 * C:12 * C], wcol[:, p:p + 1], None, AL.mult))
            c.wait("scalar", sv)
            c.op("scalar", ss, lambda e: e.activation(eupd_f[:], expn[:, 0:C], AF.Exp))
            c.wait("tensor", ss)
            c.wait("tensor", sv)
            c.op("tensor", stn, lambda e: e.transpose(trg_ap(), eupd_f[:], id_s[:]))
        c.wait("vector", stn)
        c.op("vector", sv, lambda e, pi=pi: e.tensor_copy(
            eupdT_all[:, pi * 128:(pi + 1) * 128], trg_ap()))
        c.op("vector", sv, lambda e, p=p, pi=pi: e.tensor_scalar_mul(
            e128[:, pi:pi + 1], wcol[:, p:p + 1], 128.0))
        c.wait("scalar", sv)
        c.op("scalar", ss, lambda e, pi=pi: e.activation(e128[:, pi:pi + 1], e128[:, pi:pi + 1], AF.Exp))

    trg = [pTr]

    def trg_ap():
        t = trg[0]
        ap = t[:]
        if ap.dtype != BF16:
            ap = ap.bitcast(BF16)[:, 0:128]
        return ap
    pat2 = None
    upd_mark = {}
    stt_mark = {}
    for g in range(NG):
        c.wait("sync", stn, stn.n)
        c.wait("sync", sv, sv.n)
        c.wait("vector", sq, sq.n)
        for tt in range(NCH):
            c.dma("sync", sq, ktm[tt][:], sc_k[g * 1024:(g + 1) * 1024, tt * 128:(tt + 1) * 128],
                  transpose=True)
            c.dma("sync", sq, vtm[tt][:], sc_v[g * 1024:(g + 1) * 1024, tt * 128:(tt + 1) * 128],
                  transpose=True)
        n_ld = sq.n
        for pi in range(GP):
            build_pair_patterns(pi, g * GP + pi, full=False)
        c.wait("vector", ss)
        for pi in range(GP):
            c.op("vector", sv, lambda e, pi=pi: e.memset(Sst[pi][:], 0.0))
        c.wait("vector", sq, n_ld)
        c.wait("tensor", sq, n_ld)
        c.wait("tensor", ss)
        ch_upd = {}
        for ch in range(NCH):
            cpar = ch % 2
            if ch >= 2:
                c.wait("vector", stn, ch_upd[ch - 2])
            c.op("vector", sv, lambda e, ch=ch, cpar=cpar: e.tensor_mul(
                ktmpA[cpar][:], ktm[ch][:], eupdT_all[:]))
            n_kt = sv.n
            c.wait("tensor", sv, n_kt)
            for pi in range(GP):
                u = (g * NCH + ch) * GP + pi
                par = u % 2
                if u >= 2:
                    c.wait("tensor", sv, stt_mark[u - 2])

                def updmm(e, ch=ch, pi=pi, par=par, cpar=cpar):
                    e.matmul(pU[par][0:64, :], ktmpA[cpar][:, pi * 128:pi * 128 + 64],
                             vtm[ch][:, pi * 128:pi * 128 + 64], start=True, stop=True)
                    return e.matmul(pU[par][64:128, :],
                                    ktmpA[cpar][:, pi * 128 + 64:(pi + 1) * 128],
                                    vtm[ch][:, pi * 128 + 64:(pi + 1) * 128],
                                    start=True, stop=True, tile_position=(0, 64))
                c.op("tensor", stn, updmm)
                upd_mark[u] = stn.n
                c.wait("vector", stn, upd_mark[u])
                c.op("vector", sv, lambda e, pi=pi, par=par: e.scalar_tensor_tensor(
                    Sst[pi][0:64, 0:64], Sst[pi][0:64, 0:64], e128[0:64, pi:pi + 1],
                    pU[par][0:64, :], AL.mult, AL.add))
                c.op("vector", sv, lambda e, pi=pi, par=par: e.scalar_tensor_tensor(
                    Sst[pi][64:128, 64:128], Sst[pi][64:128, 64:128], e128[64:128, pi:pi + 1],
                    pU[par][64:128, :], AL.mult, AL.add))
                stt_mark[u] = sv.n
            ch_upd[ch] = stn.n
        c.wait("sync", sv, sv.n)
        for pi in range(GP):
            c.dma("sync", sq, cc_in[g * GP + pi], Sst[pi][:])
    c.wait("gpsimd", sq, sq.n)
    c.emit("gpsimd", lambda e: e.collective_compute(
        "AllGather", AL.bypass, replica_groups=[list(range(NCORE))],
        ins=[cc_in[:]], outs=[cc_out[:]]).then_inc(scc.h, 1))
    scc.n += 1

    emit_proj(3)
    c.phase_close()
    c.barrier(bar, DMAS)

    c.wait("sync", scc, scc.n)
    c.wait("gpsimd", scc, scc.n)
    c.wait("vector", scc, scc.n)
    c.wait("tensor", scc, scc.n)
    c.wait("scalar", scc, scc.n)

    c.phase_open()
    kf = [c.sb(f"kf{i}", [128, TS], BF16) for i in range(GP)]
    rf = [c.sb(f"rf{i}", [128, TS], BF16) for i in range(GP)]
    pat2 = [c.sb(f"pat{i}", [128, NVAR * C], BF16) for i in range(GP)]
    khat = [c.sb(f"khat{i}", [128, 5 * C], BF16) for i in range(2)]
    ktfb = [c.sb(f"ktf{i}", [128, C], BF16) for i in range(2)]
    rhat = [c.sb(f"rhat{i}", [128, 6 * C], BF16) for i in range(2)]
    A2f = [c.sb(f"A2f{i}", [128, 2 * C], F32) for i in range(2)]
    rstg = [c.sb(f"rstg{i}", [128, 96], BF16) for i in range(2)]
    A2b = [c.sb(f"A2b{i}", [128, 2 * C], BF16) for i in range(2)]
    Sbf = [c.sb(f"Sbf{i}", [128, 128], BF16) for i in range(GP)]
    xnb = [c.sb(f"xnb{i}", [128, TS], BF16) for i in range(2)]
    Bg = c.sb("Bg", [128, NCORE * 128], F32)
    egap = c.sb("egap", [128, 8], F32)
    b6 = [c.sb(f"b6_{i}", [128, 12], F32) for i in range(2)]
    stat = [c.sb(f"stat{i}", [128, 12], F32) for i in range(2)]
    zz = c.sb("zz", [128, C], F32)
    ostg = [c.sb(f"ostg{i}", [128, TS], F32) for i in range(2)] if "O" in taps else None
    P = [c.ps(f"p2P{i}", [128, 128]) for i in range(6)]
    pO2 = [c.ps(f"p2O{h}", [128, 64]) for h in range(2)]

    c.op("vector", sv, lambda e: e.memset(zz[:], 0.0))

    marks = {}
    xn_dma = {}
    build_mark = {}
    pend_fin = None
    for g in range(NG):
        c.wait("sync", stn, stn.n)
        c.wait("sync", sv, sv.n)
        c.wait("sync", ss, ss.n)
        c.wait("vector", stn, stn.n)
        c.wait("vector", sq, sq.n)
        for pi in range(GP):
            p = g * GP + pi
            c.dma("sync", sq, kf[pi][:], sc_k[p * 128:(p + 1) * 128, :])
            c.dma("sync", sq, rf[pi][:], sc_r[p * 128:(p + 1) * 128, :])
        for tt in range(NCH):
            c.dma("sync", sq, ktm[tt][:], sc_k[g * 1024:(g + 1) * 1024, tt * 128:(tt + 1) * 128],
                  transpose=True)
            c.dma("sync", sq, vtm[tt][:], sc_v[g * 1024:(g + 1) * 1024, tt * 128:(tt + 1) * 128],
                  transpose=True)
        n_ld = sq.n
        trg[0] = P[1]
        for pi in range(GP):
            p = g * GP + pi
            build_pair_patterns(pi, p, full=True)
            c.op("vector", sv, lambda e, p=p: e.tensor_scalar(
                egap[:], gapix_s[:], wcol[:, p:p + 1], None, AL.mult))
            c.wait("scalar", sv)
            c.op("scalar", ss, lambda e: e.activation(egap[:], egap[:], AF.Exp))
            c.wait("gpsimd", sv, sv.n)
            c.dma("gpsimd", gq, Bg[:].rearrange("q (j f) -> q j f", j=NCORE),
                  cc_out[:, p, :, :].rearrange("j q f -> q j f"))
            c.wait("vector", gq)
            c.wait("vector", ss)
            c.op("vector", sv, lambda e, pi=pi: e.memset(Sst[pi][:], 0.0))
            for jc in range(NCORE):
                c.op("vector", sv, lambda e, pi=pi, jc=jc: e.scalar_tensor_tensor(
                    Sst[pi][:], Bg[:, jc * 128:(jc + 1) * 128], egap[:, jc:jc + 1],
                    Sst[pi][:], AL.mult, AL.add))
            c.op("vector", sv, lambda e, pi=pi: e.tensor_copy(Sbf[pi][:], Sst[pi][:]))
        c.wait("vector", sq, n_ld)
        c.wait("tensor", sq, n_ld)
        c.wait("scalar", sq, n_ld)
        def emit_builds(u):
            lc = u - g * NCH * GP
            ch_, pi_ = lc // GP, lc % GP
            par_ = u % 2
            kfc = kf[pi_][:, ch_ * C:(ch_ + 1) * C]
            rfc = rf[pi_][:, ch_ * C:(ch_ + 1) * C]
            if u >= 2:
                c.wait("vector", stn, marks[("ph1", u - 2)])
                c.wait("scalar", stn, marks[("ph1", u - 2)])
            c.op("vector", sv, lambda e, par=par_, kfc=kfc, pi=pi_: e.tensor_tensor(
                khat[par][:].rearrange("q (v f) -> q v f", v=5),
                kfc.rearrange("q (v f) -> q v f", v=1).broadcast_to((128, 5, C)),
                pat2[pi][:, 0:5 * C].rearrange("q (v f) -> q v f", v=5), AL.mult))
            c.op("vector", sv, lambda e, par=par_, rfc=rfc, pi=pi_: e.tensor_tensor(
                rhat[par][:].rearrange("q (v f) -> q v f", v=6),
                rfc.rearrange("q (v f) -> q v f", v=1).broadcast_to((128, 6, C)),
                pat2[pi][:, 5 * C:11 * C].rearrange("q (v f) -> q v f", v=6), AL.mult))
            c.op("scalar", ss, lambda e, par=par_, kfc=kfc, p=g * GP + pi_: e.mul(
                ktfb[par][:], kfc, tfcol[:, p:p + 1]))
            build_mark[u] = (sv.n, ss.n)

        u_base = g * NCH * GP
        u_last = u_base + NCH * GP - 1
        emit_builds(u_base)
        for ch in range(NCH):
            cpar = ch % 2
            if ch >= 2:
                c.wait("vector", stn, marks[("AV", (g * NCH + ch - 2) * GP + GP - 1)])
            c.op("vector", sv, lambda e, ch=ch, cpar=cpar: e.tensor_mul(
                ktmpA[cpar][:], ktm[ch][:], eupdT_all[:]))
            for pi in range(GP):
                u = (g * NCH + ch) * GP + pi
                par = u % 2
                rf_c = rf[pi][:, ch * C:(ch + 1) * C]
                gc = g * NCH + ch
                c.wait("tensor", sv, build_mark[u][0])
                c.wait("tensor", ss, build_mark[u][1])
                if u >= 1:
                    c.wait("tensor", sv, marks[("sel1", u - 1)])
                    c.wait("tensor", ss, marks[("rect", u - 1)])
                    c.wait("tensor", sv, marks[("stt", u - 1)])

                def pieces(e, par=par, rf_c=rf_c, h=0):
                    hs = slice(64 * h, 64 * (h + 1))
                    tp = None if h == 0 else (64, 0)
                    for px in range(5):
                        e.matmul(P[px][:], khat[par][hs, px * C:(px + 1) * C],
                                 rhat[par][hs, px * C:(px + 1) * C], start=True, stop=True,
                                 tile_position=tp)
                    return e.matmul(P[5][:], ktfb[par][hs, :], rf_c[hs, :],
                                    start=True, stop=True, tile_position=tp)
                c.op("tensor", stn, lambda e, par=par, rf_c=rf_c: pieces(e, par, rf_c, 0))
                marks[("ph0", u)] = stn.n
                c.wait("scalar", stn, marks[("ph0", u)])

                def stage0(e, par=par):
                    e.copy(rstg[par][0:64, 0:64], P[4][0:64, 64:128])
                    e.copy(rstg[par][0:32, 64:96], P[3][0:32, 32:64])
                    return e.copy(rstg[par][64:96, 64:96], P[3][64:96, 96:128])
                c.op("scalar", ss, stage0)
                marks[("stg", u)] = ss.n
                c.wait("vector", stn, marks[("ph0", u)])

                def sel(e, par=par, h=0):
                    dst = A2f[par][:, h * C:(h + 1) * C]
                    e.select(dst, mask_s[:, 0:C], P[0][:], zz[:])
                    e.select(dst, mask_s[:, C:2 * C], P[1][:], dst)
                    e.select(dst, mask_s[:, 2 * C:3 * C], P[2][:], dst)
                    return e.select(dst, mask_s[:, 5 * C:6 * C], P[5][:], dst)
                c.op("vector", sv, lambda e, par=par: sel(e, par, 0))
                marks[("sel0", u)] = sv.n
                if u < u_last:
                    emit_builds(u + 1)
                c.wait("tensor", sv, marks[("sel0", u)])
                c.wait("tensor", ss, marks[("stg", u)])
                c.op("tensor", stn, lambda e, par=par, rf_c=rf_c: pieces(e, par, rf_c, 1))
                marks[("ph1", u)] = stn.n
                c.wait("vector", stn, marks[("ph1", u)])

                c.op("vector", sv, lambda e, par=par: sel(e, par, 1))
                marks[("sel1", u)] = sv.n
                c.wait("scalar", sv, marks[("sel1", u)])
                c.op("scalar", ss, lambda e, par=par: e.copy(A2b[par][:], A2f[par][:]))

                def rects(e, par=par):
                    e.copy(A2b[par][0:64, 64:128], rstg[par][0:64, 0:64])
                    e.copy(A2b[par][0:32, 32:64], rstg[par][0:32, 64:96])
                    e.copy(A2b[par][64:96, 96:128], rstg[par][64:96, 64:96])
                    e.copy(A2b[par][0:64, C + 64:C + 128], P[4][0:64, 64:128])
                    e.copy(A2b[par][0:32, C + 32:C + 64], P[3][0:32, 32:64])
                    return e.copy(A2b[par][64:96, C + 96:C + 128], P[3][64:96, 96:128])
                c.op("scalar", ss, rects)
                marks[("rect", u)] = ss.n
                if pend_fin is not None:
                    pend_fin()
                    pend_fin = None
                c.wait("tensor", sv, marks[("sel1", u)])
                c.wait("tensor", ss, marks[("rect", u)])
                if u >= 1:
                    c.wait("tensor", sv, marks[("gn", u - 1)])
                    if ostg is not None:
                        c.wait("tensor", sv, marks[("ocp", u - 1)])

                def avmm(e, par=par, ch=ch, pi=pi):
                    vt = vtm[ch]
                    rint = rhat[par][:, 5 * C:6 * C]
                    e.matmul(pO2[0][:], A2b[par][:, 0:C], vt[:, pi * 128:pi * 128 + 64],
                             start=True, stop=False)
                    e.matmul(pO2[0][:], rint, Sbf[pi][:, 0:64], start=False, stop=True)
                    e.matmul(pO2[1][:], A2b[par][:, C:2 * C],
                             vt[:, pi * 128 + 64:(pi + 1) * 128], start=True, stop=False)
                    e.matmul(pO2[1][:], rint, Sbf[pi][:, 64:128], start=False, stop=True)
                    e.matmul(P[0][0:64, 0:64], ktmpA[ch % 2][:, pi * 128:pi * 128 + 64],
                             vt[:, pi * 128:pi * 128 + 64], start=True, stop=True)
                    return e.matmul(P[0][64:128, 0:64],
                                    ktmpA[ch % 2][:, pi * 128 + 64:(pi + 1) * 128],
                                    vt[:, pi * 128 + 64:(pi + 1) * 128],
                                    start=True, stop=True, tile_position=(0, 64))
                c.op("tensor", stn, avmm)
                marks[("AV", u)] = stn.n
                c.wait("vector", stn, marks[("AV", u)])
                if ("sbf", g, pi) in marks:
                    c.wait("vector", ss, marks[("sbf", g, pi)])
                c.op("vector", sv, lambda e, pi=pi: e.scalar_tensor_tensor(
                    Sst[pi][0:64, 0:64], Sst[pi][0:64, 0:64], e128[0:64, pi:pi + 1],
                    P[0][0:64, 0:64], AL.mult, AL.add))
                c.op("vector", sv, lambda e, pi=pi: e.scalar_tensor_tensor(
                    Sst[pi][64:128, 64:128], Sst[pi][64:128, 64:128], e128[64:128, pi:pi + 1],
                    P[0][64:128, 0:64], AL.mult, AL.add))
                marks[("stt", u)] = sv.n

                def gn_bn(e, par=par):
                    e.bn_stats(b6[par][:, 0:6], pO2[0][:])
                    e.bn_stats(b6[par][:, 6:12], pO2[1][:])
                    e.drain()
                    e.bn_aggr(stat[par][:, 0:2], b6[par][:, 0:6])
                    return e.bn_aggr(stat[par][:, 2:4], b6[par][:, 6:12])
                c.op("vector", sv, gn_bn)
                n_bn = sv.n
                c.wait("scalar", sv, marks[("stt", u)])
                c.op("scalar", ss, lambda e, pi=pi: e.copy(Sbf[pi][:], Sst[pi][:]))
                marks[("sbf", g, pi)] = ss.n
                c.wait("scalar", sv, n_bn)
                c.op("scalar", ss, lambda e, par=par: e.activation(
                    stat[par][:, 4:6],
                    stat[par][:, 0:4].rearrange("q (a b) -> q a b", b=2)[:, :, 1],
                    AF.Sqrt, bias=EPS))
                n_sq = ss.n

                def fin(u=u, par=par, pi=pi, ch=ch, gc=gc, n_sq=n_sq):
                    c.wait("vector", ss, n_sq)
                    c.op("vector", sv, lambda e, par=par: e.reciprocal(
                        stat[par][:, 6:8], stat[par][:, 4:6]))
                    c.emit("vector", lambda e: e.drain())
                    if gc - 2 in xn_dma:
                        c.wait("vector", sq, xn_dma[gc - 2])

                    def ap0(e, par=par, pi=pi, ch=ch):
                        xd = xnb[ch % 2]
                        return e.tensor_scalar(xd[:, pi * 128:pi * 128 + 64], pO2[0][:],
                                               stat[par][:, 0:1], stat[par][:, 6:7],
                                               AL.subtract, AL.mult)

                    def ap1(e, par=par, pi=pi, ch=ch):
                        xd = xnb[ch % 2]
                        return e.tensor_scalar(xd[:, pi * 128 + 64:(pi + 1) * 128], pO2[1][:],
                                               stat[par][:, 2:3], stat[par][:, 7:8],
                                               AL.subtract, AL.mult)
                    c.op("vector", sv, ap0)
                    c.op("vector", sv, ap1)
                    if ostg is not None:
                        c.op("vector", sv, lambda e, pi=pi, ch=ch: e.tensor_copy(
                            ostg[ch % 2][:, pi * 128:pi * 128 + 64], pO2[0][:]))
                        c.op("vector", sv, lambda e, pi=pi, ch=ch: e.tensor_copy(
                            ostg[ch % 2][:, pi * 128 + 64:(pi + 1) * 128], pO2[1][:]))
                        marks[("ocp", u)] = sv.n
                    marks[("gn", u)] = sv.n
                pend_fin = fin
            if pend_fin is not None:
                pend_fin()
                pend_fin = None
            c.wait("sync", sv, sv.n)
            c.dma("sync", sq, sc_xn[ch * 128:(ch + 1) * 128, g * 1024:(g + 1) * 1024],
                  xnb[ch % 2][:])
            if ostg is not None:
                c.dma("sync", sq, sc_O[ch * 128:(ch + 1) * 128, g * 1024:(g + 1) * 1024],
                      ostg[ch % 2][:])
            xn_dma[g * NCH + ch] = sq.n
    c.phase_close()
    c.barrier(bar, DMAS)


    mstack.close()

    y1f = [c.sb(f"y1f{i}", [128, TS], BF16, phase=False) for i in range(32)]
    lstat = c.sb("lstat", [128, 8 * 8], F32, phase=False)
    lst6 = c.sb("lst6", [128, 8 * 48], F32, phase=False)
    c.phase_open()
    gxn = [c.sb(f"gxn{i}", [128, TS], BF16) for i in range(32)]
    gt = [c.sb(f"gt{i}", [128, TS], BF16) for i in range(2)]
    xf = [c.sb(f"xf{i}", [128, TS], BF16) for i in range(2)]
    woslab = [c.sb(f"woslab{i}", [128, 16 * 512], BF16) for i in range(2)]
    xres = [c.sb(f"xres{i}", [128, 512], F32) for i in range(2)]
    ypst = [c.sb(f"ypst{i}", [128, 512], F32) for i in range(2)]
    pm = [c.ps(f"pDm{i}", [128, 512]) for i in range(2)]

    gxn_mark = {}
    for i in range(32):
        if i >= 2:
            c.wait("sync", sv, gxn_mark[i - 2])
        c.dma("sync", sq, xf[i % 2][:], sc_xn[0:TS, i * 128:(i + 1) * 128], transpose=True)
        c.dma("sync", sq, gt[i % 2][:], sc_g[i * 128:(i + 1) * 128, :])
        n_l = sq.n
        c.wait("vector", sq, n_l)
        c.op("vector", sv, lambda e, i=i: e.tensor_scalar(
            xf[i % 2][:], xf[i % 2][:], gnwc[:, i:i + 1], gnbc[:, i:i + 1], AL.mult, AL.add))
        c.op("vector", sv, lambda e, i=i: e.tensor_mul(gxn[i][:], xf[i % 2][:], gt[i % 2][:]))
        gxn_mark[i] = sv.n
    n_gxn = sv.n

    def slab_ap2(wt, oh, half):
        return wt[oh].rearrange("(hf it q) oc -> hf q it oc", hf=2, q=128)[half]

    wo_mark = {}
    wo_amark = {}
    ev2 = {}
    st2 = {}
    for oh in range(8):
        if oh >= 1:
            c.wait("gpsimd", stn, wo_amark[oh - 1])
        c.dma("gpsimd", gq, woslab[0][:].rearrange("q (it oc) -> q it oc", oc=512),
              slab_ap2(wo, oh, 0))
        n_slA = gq.n
        if oh >= 1:
            c.wait("gpsimd", stn, wo_mark[oh - 1])
        c.dma("gpsimd", gq, woslab[1][:].rearrange("q (it oc) -> q it oc", oc=512),
              slab_ap2(wo, oh, 1))
        n_slB = gq.n
        for tt in range(8):
            un = oh * 8 + tt
            if tt == 0:
                c.wait("tensor", gq, n_slA)
            if un == 0:
                c.wait("tensor", sv, n_gxn)
            if un >= 2:
                c.wait("tensor", sv, ev2[un - 2])

            def mmwoA(e, oh=oh, tt=tt):
                last = None
                for i in range(16):
                    last = e.matmul(pm[tt % 2][:], gxn[i][:, tt * 128:(tt + 1) * 128],
                                    woslab[0][:, i * 512:(i + 1) * 512],
                                    start=(i == 0), stop=False)
                return last
            c.op("tensor", stn, mmwoA)
            if tt == 7:
                wo_amark[oh] = stn.n
            if tt == 0:
                c.wait("tensor", gq, n_slB)

            def mmwoB(e, oh=oh, tt=tt):
                last = None
                for i in range(16):
                    last = e.matmul(pm[tt % 2][:], gxn[16 + i][:, tt * 128:(tt + 1) * 128],
                                    woslab[1][:, i * 512:(i + 1) * 512],
                                    start=False, stop=(i == 15))
                return last
            c.op("tensor", stn, mmwoB)
            if tt == 7:
                wo_mark[oh] = stn.n
            n_mm = stn.n
            c.wait("gpsimd", sv, ev2.get(un - 2, 0))
            c.dma("gpsimd", gq, xres[tt % 2][:],
                  x_tm[tt * 128:(tt + 1) * 128, oh * 512:(oh + 1) * 512])
            n_xr = gq.n
            c.wait("vector", stn, n_mm)
            c.wait("vector", gq, n_xr)
            if un >= 2:
                c.wait("vector", sq, st2[un - 2])
            c.op("vector", sv, lambda e, tt=tt: e.tensor_add(
                ypst[tt % 2][:], pm[tt % 2][:], xres[tt % 2][:]))
            ev2[un] = sv.n
            c.wait("sync", sv, ev2[un])
            c.dma("sync", sq, sc_yp[tt * 128:(tt + 1) * 128, oh * 512:(oh + 1) * 512],
                  ypst[tt % 2][:])
            st2[un] = sq.n
            c.op("vector", sv, lambda e, tt=tt, oh=oh: e.bn_stats(
                lst6[:, tt * 48 + oh * 6:tt * 48 + oh * 6 + 6], ypst[tt % 2][:]))
    n_yp_all = sq.n
    c.phase_close()
    c.barrier(bar, DMAS)
    c.phase_open()
    yp_t = [c.sb(f"yp_t{i}", [128, DM], F32) for i in range(2)]
    y1tm = [c.sb(f"y1tm{i}", [128, DM], BF16) for i in range(2)]
    pTd = []
    pTd.append(c.ps("pDtr0", [128, 128], BF16))
    pTd.append(c.ps("pDtr1", [128, 128], BF16))

    def ln_finalize(L, S6):
        c.emit("vector", lambda e: e.drain())
        for tt in range(8):
            c.op("vector", sv, lambda e, tt=tt, L=L, S6=S6: e.bn_aggr(
                L[:, tt * 8:tt * 8 + 2], S6[:, tt * 48:(tt + 1) * 48]))
        c.wait("scalar", sv)
        for tt in range(8):
            c.op("scalar", ss, lambda e, tt=tt, L=L: e.activation(
                L[:, tt * 8 + 3:tt * 8 + 4], L[:, tt * 8 + 1:tt * 8 + 2], AF.Sqrt, bias=EPS))
        c.wait("vector", ss)
        for tt in range(8):
            c.op("vector", sv, lambda e, tt=tt, L=L: e.reciprocal(
                L[:, tt * 8 + 4:tt * 8 + 5], L[:, tt * 8 + 3:tt * 8 + 4]))
        c.wait("scalar", sv)
        for tt in range(8):
            c.op("scalar", ss, lambda e, tt=tt, L=L: e.mul(
                L[:, tt * 8 + 6:tt * 8 + 7], L[:, tt * 8 + 4:tt * 8 + 5], -1.0))
        c.wait("vector", ss)
        for tt in range(8):
            c.op("vector", sv, lambda e, tt=tt, L=L: e.tensor_mul(
                L[:, tt * 8 + 5:tt * 8 + 6], L[:, tt * 8:tt * 8 + 1],
                L[:, tt * 8 + 6:tt * 8 + 7]))

    ln_finalize(lstat, lst6)
    n_ln1 = sv.n

    c.wait("sync", sq, n_yp_all)
    ap_mark = {}
    tr_cp = {}
    ap_s = {}
    for tt in range(8):
        if tt >= 2:
            c.wait("sync", ss, ap_s[tt - 2])
        c.dma("sync", sq, yp_t[tt % 2][:], sc_yp[tt * 128:(tt + 1) * 128, :])
        n_l = sq.n
        c.wait("scalar", sq, n_l)
        if tt == 0:
            c.wait("scalar", sv, n_ln1)
        if tt >= 2:
            c.wait("scalar", stn, ap_mark[tt - 2])
        c.op("scalar", ss, lambda e, tt=tt: e.activation(
            y1tm[tt % 2][:], yp_t[tt % 2][:], AF.Identity,
            bias=lstat[:, tt * 8 + 5:tt * 8 + 6], scale=lstat[:, tt * 8 + 4:tt * 8 + 5]))
        n_ap = ss.n
        ap_s[tt] = n_ap
        c.wait("tensor", ss, n_ap)
        for ft in range(32):
            fg = tt * 32 + ft
            if fg >= 2:
                c.wait("tensor", sv, tr_cp[fg - 2])
            c.op("tensor", stn, lambda e, tt=tt, ft=ft, fg=fg: e.transpose(
                pTd[fg % 2][:], y1tm[tt % 2][:, ft * 128:(ft + 1) * 128], id_s[:]))
            c.wait("vector", stn, stn.n)
            c.op("vector", sv, lambda e, tt=tt, ft=ft, fg=fg: e.tensor_copy(
                y1f[ft][:, tt * 128:(tt + 1) * 128], pTd[fg % 2][:]))
            tr_cp[fg] = sv.n
        ap_mark[tt] = stn.n
    c.phase_close()
    c.barrier(bar, DMAS)

    l2stat = c.sb("l2stat", [128, 8 * 8], F32, phase=False)
    l2st6 = c.sb("l2st6", [128, 8 * 48], F32, phase=False)
    c.phase_open()
    wfslab = [c.sb(f"wfslab{i}", [128, 32 * 512], BF16) for i in range(2)]
    zpst = [c.sb(f"zpst{i}", [128, 512], F32) for i in range(2)]
    y1rs = [c.sb(f"y1rs{i}", [128, 512], F32) for i in range(2)]
    ypr = [c.sb(f"ypr{i}", [128, 512], F32) for i in range(2)]
    fbs = c.sb("fbs", [128, DM], F32)
    pm2 = [c.ps(f"pEm{i}", [128, 512]) for i in range(2)]

    c.dma("sync", sq, fbs[:], fbr)
    n_fb = sq.n
    wf_mark = {}
    ev3 = {}
    st3 = {}
    y1_mark = {}
    for oh in range(8):
        if oh >= 2:
            c.wait("gpsimd", stn, wf_mark[oh - 2])
        for hf in range(2):
            c.dma("gpsimd", gq, wfslab[oh % 2][:, hf * 8192:(hf + 1) * 8192].rearrange(
                "q (it oc) -> q it oc", oc=512), slab_ap2(wf, oh, hf))
        n_sl = gq.n
        c.wait("scalar", gq, n_sl)
        if oh >= 2:
            c.wait("scalar", stn, wf_mark[oh - 2])
        for it in range(32):
            c.op("scalar", ss, lambda e, oh=oh, it=it: e.sign(
                wfslab[oh % 2][:, it * 512:(it + 1) * 512],
                wfslab[oh % 2][:, it * 512:(it + 1) * 512]))
        n_sign = ss.n
        for tt in range(8):
            un = oh * 8 + tt
            c.wait("tensor", ss, n_sign)
            if un >= 2:
                c.wait("tensor", sv, ev3[un - 2])

            def mmwf(e, oh=oh, tt=tt):
                last = None
                sl = wfslab[oh % 2]
                for i in range(32):
                    last = e.matmul(pm2[tt % 2][:], y1f[i][:, tt * 128:(tt + 1) * 128],
                                    sl[:, i * 512:(i + 1) * 512],
                                    start=(i == 0), stop=(i == 31))
                return last
            c.op("tensor", stn, mmwf)
            if tt == 7:
                wf_mark[oh] = stn.n
            n_mm = stn.n
            if un >= 2:
                c.wait("gpsimd", ss, y1_mark[un - 2])
            c.dma("gpsimd", gq, ypr[tt % 2][:],
                  sc_yp[tt * 128:(tt + 1) * 128, oh * 512:(oh + 1) * 512])
            n_ypr = gq.n
            c.wait("scalar", gq, n_ypr)
            if un >= 2:
                c.wait("scalar", sv, ev3[un - 2])
            c.op("scalar", ss, lambda e, tt=tt: e.activation(
                y1rs[tt % 2][:], ypr[tt % 2][:], AF.Identity,
                bias=lstat[:, tt * 8 + 5:tt * 8 + 6], scale=lstat[:, tt * 8 + 4:tt * 8 + 5]))
            y1_mark[un] = ss.n
            c.wait("vector", stn, n_mm)
            if un == 0:
                c.wait("vector", sq, n_fb)
            c.wait("vector", ss, y1_mark[un])
            if un >= 2:
                c.wait("vector", sq, st3[un - 2])
            c.op("vector", sv, lambda e, tt=tt, oh=oh: e.tensor_add(
                zpst[tt % 2][:], pm2[tt % 2][:], fbs[:, oh * 512:(oh + 1) * 512]))
            c.op("vector", sv, lambda e, tt=tt: e.tensor_add(
                zpst[tt % 2][:], zpst[tt % 2][:], y1rs[tt % 2][:]))
            ev3[un] = sv.n
            c.wait("sync", sv, ev3[un])
            c.dma("sync", sq, sc_zp[tt * 128:(tt + 1) * 128, oh * 512:(oh + 1) * 512],
                  zpst[tt % 2][:])
            st3[un] = sq.n
            c.op("vector", sv, lambda e, tt=tt, oh=oh: e.bn_stats(
                l2st6[:, tt * 48 + oh * 6:tt * 48 + oh * 6 + 6], zpst[tt % 2][:]))
    n_zp_all = sq.n
    c.phase_close()
    c.barrier(bar, DMAS)
    c.phase_open()
    zp_t = [c.sb(f"zp_t2{i}", [128, DM], F32) for i in range(2)]
    zt = [c.sb(f"zt{i}", [128, DM], F32) for i in range(2)]

    ln_finalize(l2stat, l2st6)
    n_ln2 = sv.n

    c.wait("sync", sq, n_zp_all)
    zst = {}
    zap = {}
    for tt in range(8):
        if tt >= 2:
            c.wait("sync", ss, zap[tt - 2])
        c.dma("sync", sq, zp_t[tt % 2][:], sc_zp[tt * 128:(tt + 1) * 128, :])
        n_l = sq.n
        c.wait("scalar", sq, n_l)
        if tt == 0:
            c.wait("scalar", sv, n_ln2)
        if tt >= 2:
            c.wait("scalar", sq, zst[tt - 2])
        c.op("scalar", ss, lambda e, tt=tt: e.activation(
            zt[tt % 2][:], zp_t[tt % 2][:], AF.Identity,
            bias=l2stat[:, tt * 8 + 5:tt * 8 + 6], scale=l2stat[:, tt * 8 + 4:tt * 8 + 5]))
        zap[tt] = ss.n
        c.wait("sync", ss, ss.n)
        c.dma("sync", sq, out[tt * 128:(tt + 1) * 128, :], zt[tt % 2][:])
        zst[tt] = sq.n
    c.phase_close()

    if taps:
        c.wait("gpsimd", sq, sq.n)
        c.wait("gpsimd", ss, ss.n)
        c.wait("gpsimd", sv, sv.n)
        for name, (tap, src) in taps.items():
            c.dma("gpsimd", gq, tap, src)
    c.wait("sync", sq, sq.n)
    c.wait("gpsimd", gq, gq.n)

    with nc.Block() as block:
        c.run(block)
    c.gstack.close()
    return nc


_NC_CACHE = {}


def _get_nc(key=()):
    if key not in _NC_CACHE:
        _NC_CACHE[key] = build_program(debug_taps=key)
    return _NC_CACHE[key]


def _cols32(a):
    return np.ascontiguousarray(np.asarray(a, np.float32).reshape(32, 128).T)


def make_inputs(inputs):
    x = np.asarray(inputs["L0_attn_hidden"], np.float32).reshape(S, DM)
    xT_full = np.ascontiguousarray(x.T)
    xprev = np.concatenate([np.zeros((DM, 1), np.float32), xT_full[:, :-1]], axis=1)

    def wblocks(w, bs):
        wt = np.ascontiguousarray(np.asarray(w, np.float32).T)
        return np.ascontiguousarray(wt.reshape(DM, DM // bs, bs).transpose(1, 0, 2))

    wkb = wblocks(inputs["L0_attn_w_key"], 128)
    wvb = wblocks(inputs["L0_attn_w_value"], 128)
    wrb = wblocks(inputs["L0_attn_w_receptance"], 128)
    wgb = wblocks(inputs["L0_attn_w_gate"], 128)
    wob = wblocks(inputs["L0_attn_w_output"], 512)
    wfb = wblocks(inputs["L0_ffn_weight"], 512)
    mcols = np.ascontiguousarray(np.concatenate(
        [_cols32(inputs["L0_attn_time_mix_key"]),
         _cols32(inputs["L0_attn_time_mix_value"]),
         _cols32(inputs["L0_attn_time_mix_receptance"]),
         _cols32(inputs["L0_attn_time_mix_gate"])], axis=1))
    tdp = _cols32(inputs["L0_attn_time_decay"])
    tfp = _cols32(inputs["L0_attn_time_faaaa"])
    gnw = _cols32(inputs["L0_attn_gn_weight"])
    gnb = _cols32(inputs["L0_attn_gn_bias"])
    fbr = np.ascontiguousarray(np.broadcast_to(
        np.asarray(inputs["L0_ffn_bias"], np.float32)[None, :], (128, DM)))
    iota, masks, ident = _host_consts()
    in_maps = []
    for core in range(NCORE):
        t0 = core * TS
        xTs = np.ascontiguousarray(
            np.concatenate([xprev[:, t0:t0 + 1], xT_full[:, t0:t0 + TS]], axis=1))
        x_tms = np.ascontiguousarray(x[t0:t0 + TS])
        gap = np.array([1024.0 * (core - 1 - j) if j < core else 1e9 for j in range(8)],
                      np.float32)
        gapx = np.ascontiguousarray(np.broadcast_to(gap[None, :], (128, 8)))
        in_maps.append({
            "xT": xTs, "x_tm": x_tms, "wkT_b": wkb, "wvT_b": wvb, "wrT_b": wrb,
            "wgT_b": wgb, "woT_b": wob, "wfT_b": wfb, "mcols": mcols, "tdp": tdp,
            "tfp": tfp, "gnw": gnw, "gnb": gnb, "fbias_rep": fbr, "iota_c": iota,
            "mask_c": masks, "ident_c": ident, "gapix": gapx,
        })
    return in_maps


def kernel(**inputs):
    nc = _get_nc()
    in_maps = make_inputs(inputs)
    res = run_bass_kernel_spmd(nc, in_maps, core_ids=list(range(NCORE)))
    shards = [np.asarray(res.results[i]["out_sh"]) for i in range(NCORE)]
    return np.concatenate(shards, axis=0).reshape(1, S, DM)
